# version 04
# baseline (speedup 1.0000x reference)
import numpy as np
import concourse.bass as bass
import concourse.mybir as mybir
from concourse.bass_utils import run_bass_kernel_spmd

F32 = mybir.dt.float32
BF16 = mybir.dt.bfloat16
AF = mybir.ActivationFunctionType
ALU = mybir.AluOpType

import os
B, S, D = 8, int(os.environ.get("MK_S", "4096")), 1024
NT = S // 128
NC5 = S // 512

def host_constants():
    c = {}
    c["ident16"] = np.eye(128, dtype=np.float32)
    c["ident32"] = np.eye(128, dtype=np.float32)
    c["ones16"] = np.ones((128, 128), dtype=np.float32)
    c["ones32"] = np.ones((128, 128), dtype=np.float32)
    c["cum_ut32"] = np.triu(np.ones((128, 128), dtype=np.float32))
    for tag, base in (("g", 1e6), ("l", 1e4)):
        inv = 1.0 / base ** (np.arange(0, 64, 2, dtype=np.float32) / 64)
        fr = np.arange(S, dtype=np.float32)[:, None] * inv[None, :]
        cos, sin = np.cos(fr), np.sin(fr)
        Ct = np.zeros((S, 64), np.float32); St = np.zeros((S, 64), np.float32)
        Ct[:, 0::2] = cos; Ct[:, 1::2] = cos
        St[:, 0::2] = -sin; St[:, 1::2] = sin
        c["rope_c_" + tag] = Ct
        c["rope_s_" + tag] = St
    kr = np.arange(128)[:, None]; qf = np.arange(512)[None, :]
    mg = np.stack([(qf - kr >= o * 128) for o in range(4)], 0).astype(np.float32)
    c["maskg"] = mg.transpose(1, 0, 2).reshape(128, 4 * 512)
    ml = []
    for o in range(-3, 3):
        diff = qf - kr + o * 128
        ml.append(((diff >= 0) & (diff < 256)).astype(np.float32))
    c["maskl"] = np.stack(ml, 0).transpose(1, 0, 2).reshape(128, 6 * 512)
    fc = np.zeros((128, 16), np.float32)
    for i, v in enumerate(FCONST_VALS):
        fc[:, i] = v
    c["fconst32"] = fc
    idx = np.arange(128)
    c["m_lows32"] = (idx[:, None] > idx[None, :]).astype(np.float32)
    c["m_upi32"] = (idx[:, None] <= idx[None, :]).astype(np.float32)
    c["m_ups32"] = (idx[:, None] < idx[None, :]).astype(np.float32)
    return c

FCONST_VALS = [-8.0, 1e-6, 1.0, 1e-5, 0.5, -0.5, 2.0, -1.0, -2.4260151319598084]

CONST_SPECS = {
    "ident16": BF16, "ident32": F32, "ones16": BF16, "ones32": F32,
    "cum_ut32": F32, "fconst32": F32,
    "maskg": BF16, "maskl": BF16,
    "m_lows32": F32, "m_upi32": F32, "m_ups32": F32,
}
GLOBAL_CONSTS = ("ident16", "ident32", "ones16", "ones32", "cum_ut32", "fconst32", "m_lows32", "m_upi32", "m_ups32")
HBM_CONSTS = ["rope_c_g", "rope_s_g", "rope_c_l", "rope_s_l"]

HOSTC = host_constants()

class Ctr:
    __slots__ = ("name", "sem", "n", "step")
    def __init__(self, name, sem, step=1):
        self.name, self.sem, self.n, self.step = name, sem, 0, step

class Arena:
    def __init__(self, t, n):
        self.t, self.n, self.off, self.marks = t, n, 0, []
    def alloc(self, cols):
        cols = (cols + 15) // 16 * 16
        assert self.off + cols <= self.n, f"arena overflow {self.off}+{cols}>{self.n}"
        ap = self.t[:, self.off:self.off + cols]
        self.off += cols
        return ap
    def alloc32(self, cols):
        return self.alloc(2 * cols).bitcast(F32)
    def push(self): self.marks.append(self.off)
    def pop(self): self.off = self.marks.pop()

class P:
    def __init__(self, nc, cur, engines, sems):
        self.nc, self.cur, self.E, self.sems = nc, cur, engines, sems

    def wait(self, who, ctr, v=None):
        v = ctr.n if v is None else v
        if v > 0 and self.cur == who:
            self.E[who].wait_ge(ctr.sem, v)

    def _fin(self, who, instr, ctr):
        if ctr is not None:
            ctr.n += ctr.step
            if self.cur == who and instr is not None:
                instr.then_inc(ctr.sem, ctr.step)

    def dma(self, q, out, in_, ctr=None, transpose=False, accum=False, slow=False):
        i = None
        if self.cur == q:
            kw = {"accum_op": ALU.add} if accum else {}
            if slow:
                kw["allow_slow_non_contiguous"] = True
            i = self.E[q].dma_start(out=out, in_=in_, transpose=transpose, **kw)
        self._fin(q, i, ctr)

    def mm(self, out, lhsT, rhs, start, stop, ctr=None):
        i = None
        if self.cur == "tensor":
            i = self.E["tensor"].matmul(out, lhsT, rhs, start=start, stop=stop)
        self._fin("tensor", i, ctr)

    def trans(self, out, in_, ident, ctr=None):
        i = None
        if self.cur == "tensor":
            i = self.E["tensor"].transpose(out, in_, ident)
        self._fin("tensor", i, ctr)

    def act(self, out, in_, func, bias=0.0, scale=1.0, accum_out=None, ctr=None,
            edge=False):
        i = None
        if self.cur == "scalar":
            i = self.E["scalar"].activation(out, in_, func, bias=bias, scale=scale,
                                            accum_out=accum_out)
        c = ctr if ctr is not None else (self.sems["act"] if edge else None)
        self._fin("scalar", i, c)
        if edge:
            self.wait("scalar", self.sems["act"])

    def ve(self, eng, meth, *args, ctr=None, edge=False, **kw):
        i = None
        if self.cur == eng:
            i = getattr(self.E[eng], meth)(*args, **kw)
        nm = "dve" if eng == "vector" else "pool"
        c = ctr if ctr is not None else (self.sems[nm] if edge else None)
        self._fin(eng, i, c)
        if edge:
            self.wait(eng, self.sems[nm])

    def v(self, meth, *args, ctr=None, edge=False, **kw):
        self.ve("vector", meth, *args, ctr=ctr, edge=edge, **kw)

    def g(self, meth, *args, ctr=None, edge=False, **kw):
        self.ve("gpsimd", meth, *args, ctr=ctr, edge=edge, **kw)

    def barrier(self):
        for e in ("sync", "gpsimd", "tensor", "vector", "scalar"):
            if self.cur == e:
                for c in self.sems.values():
                    if c.n > 0:
                        self.E[e].wait_ge(c.sem, c.n)


def ev(ap):
    return ap.rearrange("p (a b) -> p a b", b=2)[:, :, 0:1]

def od(ap):
    return ap.rearrange("p (a b) -> p a b", b=2)[:, :, 1:2]


class MK:
    def __init__(self, nc, phases, dbg=()):
        self.nc, self.phases = nc, phases
        self.dbg = set(dbg)
        self.d = {}
        self.declare_dram()

    def declare_dram(self):
        nc, d = self.nc, self.d
        d["x0"] = nc.dram_tensor("x0", [S, D], F32, kind="ExternalInput")
        WSPECS = [
            ("L0_attn_norm1_w", [D]), ("L0_attn_norm2_w", [D]),
            ("L0_attn_g_wq", [D, D]), ("L0_attn_g_kv_down", [D, 256]),
            ("L0_attn_g_k_up", [256, 256]), ("L0_attn_g_v_up", [256, 256]),
            ("L0_attn_g_wo", [D, D]),
            ("L0_attn_l_wq", [D, D]), ("L0_attn_l_wk", [D, 256]),
            ("L0_attn_l_wv", [D, 256]), ("L0_attn_l_wo", [D, D]),
            ("L0_ffn_router_w", [D, 8]), ("L0_ffn_router_b", [8]),
            ("L0_ffn_W1", [8, D, 4096]), ("L0_ffn_b1", [8, 4096]),
            ("L0_ffn_W2", [8, 4096, D]), ("L0_ffn_b2", [8, D]),
            ("L1_attn_w_ln1", [D]), ("L1_attn_w_ln2", [D]),
            ("L1_attn_W_qkvz", [D, 3072]), ("L1_attn_W_ba", [D, 16]),
            ("L1_attn_conv_w", [2048, 4]), ("L1_attn_dt_bias", [8]),
            ("L1_attn_A_log", [8]), ("L1_attn_gate_norm_w", [128]),
            ("L1_attn_W_out", [D, D]), ("L1_attn_W_router", [D, 8]),
            ("L1_attn_W_gate_e", [8, D, 512]), ("L1_attn_W_up_e", [8, D, 512]),
            ("L1_attn_W_down_e", [8, 512, D]),
            ("L1_ffn_shared_w1", [4096, D]), ("L1_ffn_shared_w2", [D, 4096]),
            ("L1_ffn_experts_w1", [8, 4096, D]), ("L1_ffn_experts_w2", [8, D, 4096]),
            ("L1_ffn_gate_w", [8, D]),
        ]
        for name, shape in WSPECS:
            d[name] = nc.dram_tensor(name, shape, F32, kind="ExternalInput")
        for name, dt in CONST_SPECS.items():
            d[name] = nc.dram_tensor(name, list(HOSTC[name].shape), dt,
                                     kind="ExternalInput")
        for name in HBM_CONSTS:
            d[name] = nc.dram_tensor(name, list(HOSTC[name].shape), BF16,
                                     kind="ExternalInput")
        scr = [
            ("xfmdbg", [128, 4096], BF16),
            ("routedbg", [128, 64], F32),
            ("c32dbg", [128, NT * 8], F32), ("hedbg", [128, 32 * 512], BF16),
            ("xn32", [S, D], F32), ("xn16", [S, D], BF16), ("a016", [S, D], BF16),
            ("x132", [S, D], F32), ("a116", [S, D], BF16), ("s32", [S, D], F32),
            ("ffn0", [S, D], F32), ("X132", [S, D], F32),
            ("xn1_32", [S, D], F32), ("xn1_16", [S, D], BF16), ("h016", [S, D], BF16),
            ("x232", [S, D], F32), ("xh16", [S, D], BF16),
            ("moe1", [S, D], F32), ("ffn1", [S, D], F32), ("s132", [S, D], F32),
            ("sw1_16", [4096, D], BF16), ("sw2_16", [D, 4096], BF16),
            ("ew1_16", [8 * 4096, D], BF16), ("ew2_16", [8 * D, 4096], BF16),
            ("gw_16", [8, D], BF16),
        ]
        for name, shape, dt in scr:
            kind = "ExternalOutput" if name in self.dbg else "Internal"
            d[name] = nc.dram_tensor(name, shape, dt, kind=kind)
        d["out"] = nc.dram_tensor("out", [S, D], F32, kind="ExternalOutput")

    def build(self):
        nc = self.nc
        with (
            nc.sbuf_tensor("A16", [128, 99328], BF16) as A16,
            nc.sbuf_tensor("CB", [128, 2048], BF16) as CB,
            nc.psum_tensor("PS", [128, 3072], F32) as PS,
            nc.psum_tensor("PT", [128, 2048], BF16) as PT,
            nc.semaphore("c_pe") as s_pe,
            nc.semaphore("c_act") as s_act,
            nc.semaphore("c_dve") as s_dve,
            nc.semaphore("c_pool") as s_pool,
            nc.semaphore("q_sync") as s_qs,
            nc.semaphore("q_scal") as s_qc,
            nc.semaphore("q_gps") as s_qg,
            nc.Block() as block,
        ):
            self.A16, self.CB, self.PS, self.PT = A16, CB, PS, PT
            mk = self

            def run(cur, eng):
                sems = {
                    "pe": Ctr("pe", s_pe), "act": Ctr("act", s_act),
                    "dve": Ctr("dve", s_dve), "pool": Ctr("pool", s_pool),
                    "qs": Ctr("qs", s_qs, 16), "qc": Ctr("qc", s_qc, 16),
                    "qg": Ctr("qg", s_qg, 16),
                }
                emit_all(mk, P(self.nc, cur, {cur: eng}, sems))

            @block.sync
            def _(eng): run("sync", eng)
            @block.gpsimd
            def _(eng): run("gpsimd", eng)
            @block.tensor
            def _(eng): run("tensor", eng)
            @block.vector
            def _(eng): run("vector", eng)
            @block.scalar
            def _(eng): run("scalar", eng)
        return nc



def emit_rsqrt(p, dst, srcv, pre_scale=1.0, pre_bias=0.0, post_scale=1.0):
    import math
    act = p.sems["act"]
    p.act(dst, srcv, AF.Ln, scale=pre_scale, bias=pre_bias, edge=True)
    p.act(dst, dst, AF.Exp, scale=-0.5,
          bias=(0.0 if post_scale == 1.0 else math.log(post_scale)), ctr=act)

def entry_guard(p, queues=("sync", "gpsimd")):
    for q in queues:
        for cn in ("pe", "act", "dve", "pool", "qs", "qc", "qg"):
            p.wait(q, p.sems[cn])


def load_const(mk, p, ar, name):
    dt = CONST_SPECS[name]
    w = HOSTC[name].shape[1]
    buf = ar.alloc32(w) if dt == F32 else ar.alloc(w)
    p.dma("sync", buf, mk.d[name][:, :], p.sems["qs"])
    return buf

def load_row_bcast(mk, p, ar, src_ap, w):
    buf = ar.alloc32(w)
    p.dma("sync", buf, src_ap[None, :].to_broadcast([128, w]), p.sems["qs"])
    return buf

def emit_all(mk, p):
    a16 = Arena(mk.A16, mk.A16.shape[1])
    cst = Arena(mk.CB, mk.CB.shape[1])
    qs = p.sems["qs"]
    C = {name: load_const(mk, p, cst, name) for name in GLOBAL_CONSTS}
    C["consts_at"] = qs.n
    mk.C = C
    if not getattr(mk, "fconst_registered", False):
        for i, v in enumerate(FCONST_VALS):
            mk.nc.const_aps.register(F32, v, C["fconst32"][:, i:i + 1])
        mk.fconst_registered = True
    for e in ("tensor", "vector", "scalar", "gpsimd"):
        p.wait(e, qs, C["consts_at"])
    if "A" in mk.phases:
        phase_A(mk, p, a16)
    if "B" in mk.phases:
        phase_attn(mk, p, a16, "g")
    if "C" in mk.phases:
        phase_attn(mk, p, a16, "l")
    if "D" in mk.phases:
        phase_moe(mk, p, a16, "L0")
        phase_combine2(mk, p, a16, "X132", "x0", "s32", "ffn0")
    if "E" in mk.phases:
        phase_E(mk, p, a16)
    if "F" in mk.phases:
        phase_F(mk, p, a16)
    if "G" in mk.phases:
        phase_moe2(mk, p, a16, "G")
        phase_add(mk, p, a16, "s132", "x232", "moe1")
    if "H" in mk.phases:
        phase_prepass(mk, p, a16)
        phase_moe2(mk, p, a16, "H")
        phase_combine2(mk, p, a16, "out", "X132", "s132", "ffn1")
    p.barrier()


def phase_A(mk, p, a16):
    d, C = mk.d, mk.C
    qs, dve, act = p.sems["qs"], p.sems["dve"], p.sems["act"]
    a16.push()
    n1bc = load_row_bcast(mk, p, a16, d["L0_attn_norm1_w"], D)
    xin = [a16.alloc32(D) for _ in range(2)]
    st8 = [a16.alloc32(12) for _ in range(2)]
    mv = [a16.alloc32(4) for _ in range(2)]
    xn32t = [a16.alloc32(D) for _ in range(2)]
    xn16t = [a16.alloc(D) for _ in range(2)]
    a016t = [a16.alloc(D) for _ in range(2)]

    x0r = d["x0"].rearrange("(n p) d -> n p d", p=128)
    xn32r = d["xn32"].rearrange("(n p) d -> n p d", p=128)
    xn16r = d["xn16"].rearrange("(n p) d -> n p d", p=128)
    a016r = d["a016"].rearrange("(n p) d -> n p d", p=128)

    free_at = {}
    p.wait("sync", qs, qs.n)
    for i in range(NT):
        bi = i % 2
        if i >= 2:
            p.wait("sync", dve, free_at[i - 2][0])
            p.wait("sync", qs, free_at[i - 2][1])
        p.dma("sync", xin[bi], x0r[i], qs)
        ld = qs.n

        p.wait("vector", qs, ld)
        p.v("bn_stats", st8[bi][:, 0:6], xin[bi][:, 0:512], edge=True)
        p.v("bn_stats", st8[bi][:, 6:12], xin[bi][:, 512:1024], edge=True)
        p.v("bn_aggr", mv[bi][:, 0:2], st8[bi][:, 0:12], edge=True)
        p.wait("scalar", dve, dve.n)
        emit_rsqrt(p, mv[bi][:, 2:3], mv[bi][:, 1:2], pre_bias=1e-5)
        rstd_done = act.n
        p.v("tensor_scalar_add", mv[bi][:, 3:4], mv[bi][:, 1:2], 1e-5, edge=True)
        p.v("reciprocal", mv[bi][:, 3:4], mv[bi][:, 3:4], edge=True)
        p.v("tensor_mul", mv[bi][:, 3:4], mv[bi][:, 3:4], mv[bi][:, 1:2], edge=True)
        p.v("tensor_scalar_add", mv[bi][:, 3:4], mv[bi][:, 3:4], 1e-6, ctr=dve)
        p.wait("scalar", dve, dve.n)
        emit_rsqrt(p, mv[bi][:, 3:4], mv[bi][:, 3:4])
        rms2_done = act.n

        p.wait("vector", act, rstd_done)
        p.v("tensor_scalar", xn32t[bi], xin[bi], mv[bi][:, 0:1], mv[bi][:, 2:3],
            ALU.subtract, ALU.mult, edge=True)
        p.v("tensor_copy", xn16t[bi], xn32t[bi], ctr=dve)
        xn_done = dve.n
        p.wait("vector", act, rms2_done)
        p.v("scalar_tensor_tensor", a016t[bi], xn32t[bi], mv[bi][:, 3:4], n1bc,
            ALU.mult, ALU.mult, ctr=dve)
        a0_done = dve.n

        p.wait("sync", dve, xn_done)
        p.dma("sync", xn32r[i], xn32t[bi], qs)
        p.dma("sync", xn16r[i], xn16t[bi], qs)
        p.wait("sync", dve, a0_done)
        p.dma("sync", a016r[i], a016t[bi], qs)
        free_at[i] = (dve.n, qs.n)
    a16.pop()


def phase_attn(mk, p, a16, kind):
    d, C = mk.d, mk.C
    qs, qg = p.sems["qs"], p.sems["qg"]
    pe, act, dve = p.sems["pe"], p.sems["act"], p.sems["dve"]
    PS, PT = mk.PS, mk.PT
    a16.push()
    entry_guard(p)

    glob = kind == "g"
    src16 = d["a016"] if glob else d["a116"]
    wq = d["L0_attn_g_wq"] if glob else d["L0_attn_l_wq"]
    wo = d["L0_attn_g_wo"] if glob else d["L0_attn_l_wo"]

    wq_sb = a16.alloc(8 * D)
    for t in range(8):
        p.dma("gpsimd", wq_sb[:, t * D:(t + 1) * D], wq[t * 128:(t + 1) * 128, :], qg)
    if glob:
        kvd_sb = a16.alloc(8 * 256)
        ku_sb = a16.alloc(2 * 256)
        vu_sb = a16.alloc(2 * 256)
        for t in range(8):
            p.dma("gpsimd", kvd_sb[:, t * 256:(t + 1) * 256],
                  d["L0_attn_g_kv_down"][t * 128:(t + 1) * 128, :], qg)
        for t in range(2):
            p.dma("gpsimd", ku_sb[:, t * 256:(t + 1) * 256],
                  d["L0_attn_g_k_up"][t * 128:(t + 1) * 128, :], qg)
            p.dma("gpsimd", vu_sb[:, t * 256:(t + 1) * 256],
                  d["L0_attn_g_v_up"][t * 128:(t + 1) * 128, :], qg)
    else:
        wk_sb = a16.alloc(8 * 256)
        wv_sb = a16.alloc(8 * 256)
        for t in range(8):
            p.dma("gpsimd", wk_sb[:, t * 256:(t + 1) * 256],
                  d["L0_attn_l_wk"][t * 128:(t + 1) * 128, :], qg)
            p.dma("gpsimd", wv_sb[:, t * 256:(t + 1) * 256],
                  d["L0_attn_l_wv"][t * 128:(t + 1) * 128, :], qg)
    w_at = qg.n

    mtag = "maskg" if glob else "maskl"
    msk = a16.alloc(HOSTC[mtag].shape[1])
    p.dma("sync", msk, d[mtag][:, :], qs)
    rc = a16.alloc(NT * 64)
    rs = a16.alloc(NT * 64)
    p.dma("sync", rc.rearrange("p (n f) -> p n f", f=64),
          d["rope_c_" + kind].rearrange("(n p) f -> p n f", p=128), qs)
    p.dma("sync", rs.rearrange("p (n f) -> p n f", f=64),
          d["rope_s_" + kind].rearrange("(n p) f -> p n f", p=128), qs)
    const_at = qs.n

    afm = [a16.alloc(4096) for _ in range(2)]
    attn = a16.alloc(8 * S)
    a16.push()
    if glob:
        lat = a16.alloc(2 * S)
    qraw = a16.alloc(2 * S)
    qfmn = a16.alloc(2 * S)
    kvraw = a16.alloc(S)
    kfmn = a16.alloc(S)
    vaug = a16.alloc(NT * 66 + 16)
    tbuf = a16.alloc(128)
    rtmp = a16.alloc(512)
    rtmp4 = a16.alloc(512)
    rtmpq = a16.alloc(512)
    kvb = a16.alloc(512)
    q0b = a16.alloc(512)
    q1b = a16.alloc(512)
    nrm16 = a16.alloc32(20)
    nrm16b = a16.alloc32(20)
    nrm = a16.alloc32(8)
    prow = a16.alloc32(512)
    psb = [a16.alloc(512) for _ in range(2)]
    usb = a16.alloc(512)

    bP0, bP1 = PS[:, 0:512], PS[:, 512:1024]
    bS0, bS1 = PS[:, 1024:1536], PS[:, 1536:2048]
    bU = PS[:, 2048:2560]
    bB = PS[:, 2560:3072]
    tp = [PT[:, 0:1024], PT[:, 1024:2048]]
    vaug_t = vaug[:, 0:NT * 66].rearrange("p (n w) -> p n w", w=66)

    def load_afm(c, buf):
        for t in range(8):
            p.dma("sync", buf[:, t * 512:(t + 1) * 512],
                  src16[c * 512:(c + 1) * 512, t * 128:(t + 1) * 128],
                  qs, transpose=True)
        return qs.n

    p.wait("sync", qs, const_at)

    for g in range(4):
        evac = {"P0": 0, "P1": 0}
        afm_read = [0, 0]
        for c in range(NC5):
            buf = afm[c % 2]
            p.wait("sync", pe, afm_read[c % 2])
            ld = load_afm(c, buf)
            p.wait("tensor", qs, ld)
            if c == 0 and g == 0:
                p.wait("tensor", qg, w_at)
            for ct in range(2):
                col0 = g * 256 + ct * 128
                bnk, key = (bP0, "P0") if ct == 0 else (bP1, "P1")
                p.wait("tensor", dve, evac[key])
                for t in range(8):
                    p.mm(bnk, wq_sb[:, t * D + col0: t * D + col0 + 128],
                         buf[:, t * 512:(t + 1) * 512], start=(t == 0), stop=(t == 7),
                         ctr=pe if t == 7 else None)
                p.wait("vector", pe, pe.n)
                p.v("tensor_copy", qraw[:, ct * S + c * 512: ct * S + c * 512 + 512],
                    bnk, ctr=dve)
                evac[key] = dve.n
            if glob and g == 0:
                for ct in range(2):
                    bnk, key = (bP0, "P0") if ct == 0 else (bP1, "P1")
                    p.wait("tensor", dve, evac[key])
                    for t in range(8):
                        p.mm(bnk, kvd_sb[:, t * 256 + ct * 128: t * 256 + ct * 128 + 128],
                             buf[:, t * 512:(t + 1) * 512], start=(t == 0), stop=(t == 7),
                             ctr=pe if t == 7 else None)
                    p.wait("vector", pe, pe.n)
                    p.v("tensor_copy", lat[:, ct * S + c * 512: ct * S + c * 512 + 512],
                        bnk, ctr=dve)
                    evac[key] = dve.n
            afm_read[c % 2] = pe.n

        for c in range(NC5):
            p.wait("tensor", dve, max(evac["P0"], evac["P1"]))
            if glob:
                for t in range(2):
                    p.mm(bP0[0:64, :], ku_sb[:, t * 256 + g * 64: t * 256 + g * 64 + 64],
                         lat[:, t * S + c * 512: t * S + c * 512 + 512],
                         start=(t == 0), stop=(t == 1))
                for t in range(2):
                    p.mm(bP0[64:128, :], vu_sb[:, t * 256 + g * 64: t * 256 + g * 64 + 64],
                         lat[:, t * S + c * 512: t * S + c * 512 + 512],
                         start=(t == 0), stop=(t == 1), ctr=pe if t == 1 else None)
            else:
                buf = afm[c % 2]
                p.wait("sync", pe, pe.n)
                ld = load_afm(c, buf)
                p.wait("tensor", qs, ld)
                for t in range(8):
                    p.mm(bP0[0:64, :], wk_sb[:, t * 256 + g * 64: t * 256 + g * 64 + 64],
                         buf[:, t * 512:(t + 1) * 512], start=(t == 0), stop=(t == 7))
                for t in range(8):
                    p.mm(bP0[64:128, :], wv_sb[:, t * 256 + g * 64: t * 256 + g * 64 + 64],
                         buf[:, t * 512:(t + 1) * 512], start=(t == 0), stop=(t == 7),
                         ctr=pe if t == 7 else None)
            p.wait("vector", pe, pe.n)
            p.v("tensor_copy", kvraw[:, c * 512:(c + 1) * 512], bP0, ctr=dve)
            evac["P0"] = dve.n

        p.v("memset", vaug_t[:, :, 64:65], 1.0, ctr=dve)
        p.wait("tensor", dve, dve.n)

        for scn in range(NT // 4):
            i0 = scn * 4
            p.wait("tensor", dve, dve.n)
            for j in range(4):
                p.trans(tp[0][:, j * 128:(j + 1) * 128],
                        kvraw[:, (i0 + j) * 128:(i0 + j + 1) * 128], C["ident16"])
            for j in range(4):
                p.trans(tp[0][:, 512 + j * 128:512 + (j + 1) * 128],
                        qraw[:, (i0 + j) * 128:(i0 + j + 1) * 128], C["ident16"])
            for j in range(4):
                p.trans(tp[1][:, j * 128:(j + 1) * 128],
                        qraw[:, S + (i0 + j) * 128: S + (i0 + j + 1) * 128],
                        C["ident16"], ctr=pe if j == 3 else None)
            p.wait("vector", pe, pe.n)
            p.v("tensor_copy", kvb, tp[0][:, 0:512], edge=True)
            p.v("tensor_copy", q0b, tp[0][:, 512:1024], edge=True)
            p.v("tensor_copy", q1b, tp[1][:, 0:512], edge=True)
            p.v("tensor_copy",
                vaug[:, i0 * 66:(i0 + 4) * 66].rearrange("p (n w) -> p n w", w=66)[:, :, 0:64],
                kvb.rearrange("p (n w) -> p n w", w=128)[:, :, 64:128], ctr=dve)
            p.wait("vector", dve, dve.n)
            rcs = rc[:, i0 * 64:(i0 + 4) * 64].rearrange("p (n f) -> p n f", f=64)
            rss = rs[:, i0 * 64:(i0 + 4) * 64].rearrange("p (n f) -> p n f", f=64)
            kview = kvb.rearrange("p (n w) -> p n w", w=128)[:, :, 0:64]
            kev = kview.rearrange("p n (a b) -> p n a b", b=2)
            rt4 = rtmp4[:, 0:256].rearrange("p (n f) -> p n f", f=64)
            p.v("tensor_copy", rt4.rearrange("p n (a b) -> p n a b", b=2)[:, :, :, 0:1],
                kev[:, :, :, 1:2], edge=True)
            p.v("tensor_copy", rt4.rearrange("p n (a b) -> p n a b", b=2)[:, :, :, 1:2],
                kev[:, :, :, 0:1], edge=True)
            p.v("tensor_tensor", rt4, rt4, rss, ALU.mult, edge=True)
            p.v("tensor_tensor", kview, kview, rcs, ALU.mult, edge=True)
            p.v("tensor_tensor", kview, kview, rt4, ALU.add, edge=True)
            rc2 = rcs.rearrange("p n (o f) -> p n o f", o=1).to_broadcast([128, 4, 2, 64])
            rs2 = rss.rearrange("p n (o f) -> p n o f", o=1).to_broadcast([128, 4, 2, 64])
            for qb in (q0b, q1b):
                qm = qb.rearrange("p (m f) -> p m f", f=64)
                qv = qb.rearrange("p (n h f) -> p n h f", h=2, f=64)
                rqm = rtmpq.rearrange("p (m f) -> p m f", f=64)
                rqv = rtmpq.rearrange("p (n h f) -> p n h f", h=2, f=64)
                p.v("tensor_copy", rqm.rearrange("p m (a b) -> p m a b", b=2)[:, :, :, 0:1],
                    qm.rearrange("p m (a b) -> p m a b", b=2)[:, :, :, 1:2],
                    edge=True)
                p.v("tensor_copy", rqm.rearrange("p m (a b) -> p m a b", b=2)[:, :, :, 1:2],
                    qm.rearrange("p m (a b) -> p m a b", b=2)[:, :, :, 0:1],
                    edge=True)
                p.v("tensor_tensor", rqv, rqv, rs2, ALU.mult, edge=True)
                p.v("tensor_tensor", qv, qv, rc2, ALU.mult, edge=True)
                p.v("tensor_tensor", qb, qb, rtmpq, ALU.add, edge=True)
            p.wait("scalar", dve, dve.n)
            for j in range(4):
                p.act(rtmp[:, 0:64], kvb[:, j * 128:j * 128 + 64], AF.Square,
                      accum_out=nrm16[:, j:j + 1],
                      ctr=act if j == 3 else None)
            for j in range(4):
                for hh in range(2):
                    p.act(rtmp[:, 0:64], q0b[:, j * 128 + hh * 64:j * 128 + (hh + 1) * 64],
                          AF.Square, accum_out=nrm16[:, 4 + j * 2 + hh:5 + j * 2 + hh],
                          ctr=act if (j == 3 and hh == 1) else None)
            for j in range(4):
                for hh in range(2):
                    p.act(rtmp[:, 0:64], q1b[:, j * 128 + hh * 64:j * 128 + (hh + 1) * 64],
                          AF.Square, accum_out=nrm16[:, 12 + j * 2 + hh:13 + j * 2 + hh],
                          ctr=act)
            emit_rsqrt(p, nrm16b[:, 0:20], nrm16[:, 0:20], pre_scale=1.0 / 64, pre_bias=1e-6)
            p.wait("vector", act, act.n)
            for j in range(4):
                p.v("tensor_scalar_mul", kvb[:, j * 128:j * 128 + 64],
                    kvb[:, j * 128:j * 128 + 64], nrm16b[:, j:j + 1])
            for j in range(4):
                for hh in range(2):
                    p.v("tensor_scalar_mul", q0b[:, j * 128 + hh * 64:j * 128 + (hh + 1) * 64],
                        q0b[:, j * 128 + hh * 64:j * 128 + (hh + 1) * 64],
                        nrm16b[:, 4 + j * 2 + hh:5 + j * 2 + hh])
                    p.v("tensor_scalar_mul", q1b[:, j * 128 + hh * 64:j * 128 + (hh + 1) * 64],
                        q1b[:, j * 128 + hh * 64:j * 128 + (hh + 1) * 64],
                        nrm16b[:, 12 + j * 2 + hh:13 + j * 2 + hh], ctr=dve)
            p.wait("tensor", dve, dve.n)
            for j in range(4):
                p.trans(tp[0][0:64, j * 128:(j + 1) * 128],
                        kvb[:, j * 128:j * 128 + 64], C["ident16"])
                p.trans(tp[0][64:128, 512 + j * 128:512 + (j + 1) * 128],
                        kvb[:, j * 128:j * 128 + 64], C["ident16"])
                p.trans(tp[1][:, j * 128:(j + 1) * 128],
                        q0b[:, j * 128:(j + 1) * 128], C["ident16"])
                p.trans(tp[1][:, 512 + j * 128:512 + (j + 1) * 128],
                        q1b[:, j * 128:(j + 1) * 128], C["ident16"],
                        ctr=pe if j == 3 else None)
            p.wait("vector", pe, pe.n)
            p.v("tensor_copy", kfmn[0:64, i0 * 128:(i0 + 4) * 128], tp[0][0:64, 0:512],
                edge=True)
            p.v("tensor_copy", kfmn[64:128, i0 * 128:(i0 + 4) * 128],
                tp[0][64:128, 512:1024], edge=True)
            p.v("tensor_copy", qfmn[:, i0 * 128:(i0 + 4) * 128], tp[1][:, 0:512],
                edge=True)
            p.v("tensor_copy", qfmn[:, S + i0 * 128: S + (i0 + 4) * 128],
                tp[1][:, 512:1024], ctr=dve)
            p.wait("tensor", dve, dve.n)
        prep_done = dve.n

        p.wait("tensor", dve, prep_done)
        for hl in range(4):
            h = 4 * g + hl
            pt, base = hl // 2, (hl % 2) * 64
            for qc in range(NC5):
                if glob:
                    jlist = list(range(0, 4 * qc + 4))
                else:
                    jlist = [j for j in range(4 * qc - 2, 4 * qc + 4) if j >= 0]
                exp_at = {}
                umm_at = {}
                for idx, j in enumerate(jlist):
                    sb = (bS0, 0) if idx % 2 == 0 else (bS1, 1)
                    if idx >= 2:
                        p.wait("tensor", act, exp_at[idx - 2])
                    p.mm(sb[0], kfmn[base:base + 64, j * 128:(j + 1) * 128],
                         qfmn[base:base + 64, pt * S + qc * 512: pt * S + qc * 512 + 512],
                         start=True, stop=True, ctr=pe)
                    p.wait("scalar", pe, pe.n)
                    if idx >= 2:
                        p.wait("scalar", pe, umm_at[idx - 2])
                    p.act(psb[idx % 2], sb[0], AF.Exp, scale=0.125, bias=-8.0,
                          ctr=act)
                    exp_at[idx] = act.n
                    o = j - 4 * qc + (0 if glob else 3)
                    masked = (j - 4 * qc >= 0) if glob else True
                    if masked:
                        mo = (j - 4 * qc) if glob else (4 * qc - j + 3)
                        p.wait("vector", act, act.n)
                        p.v("tensor_mul", psb[idx % 2], psb[idx % 2],
                            msk[:, mo * 512:(mo + 1) * 512], ctr=dve)
                        p.wait("tensor", dve, dve.n)
                    else:
                        p.wait("tensor", act, act.n)
                    p.mm(bU[0:65, :], vaug[:, j * 66: j * 66 + 65], psb[idx % 2],
                         start=(idx == 0), stop=(idx == len(jlist) - 1), ctr=pe)
                    umm_at[idx] = pe.n
                p.wait("vector", pe, pe.n)
                p.v("reciprocal", prow[64:65, :], bU[64:65, :], ctr=dve)
                p.wait("tensor", dve, dve.n)
                p.mm(bB[0:64, :], C["ones32"][64:65, 0:64], prow[64:65, :],
                     start=True, stop=True, ctr=pe)
                p.wait("scalar", pe, umm_at[len(jlist) - 1])
                p.act(usb[0:64, :], bU[0:64, :], AF.Identity, ctr=act)
                p.wait("vector", act, act.n)
                p.wait("vector", pe, pe.n)
                th, bh = h // 2, (h % 2) * 64
                if bh == 0:
                    p.v("tensor_mul",
                        attn[0:64, th * S + qc * 512: th * S + qc * 512 + 512],
                        usb[0:64, :], bB[0:64, :], ctr=dve)
                    p.wait("tensor", dve, dve.n)
                else:
                    p.v("tensor_mul", usb[0:64, :], usb[0:64, :], bB[0:64, :],
                        ctr=dve)
                    p.wait("tensor", dve, dve.n)
                    p.mm(bB[64:128, :], C["ident16"][0:64, 0:64], usb[0:64, :],
                         start=True, stop=True, ctr=pe)
                    p.wait("vector", pe, pe.n)
                    p.v("tensor_copy",
                        attn[64:128, th * S + qc * 512: th * S + qc * 512 + 512],
                        bB[64:128, :], ctr=dve)
                    p.wait("tensor", dve, dve.n)

    a16.pop()
    entry_guard(p)
    wo_sb = a16.alloc(8 * D)
    for t in range(8):
        p.dma("gpsimd", wo_sb[:, t * D:(t + 1) * D], wo[t * 128:(t + 1) * 128, :], qg)
    wo_at = qg.n
    if glob:
        n2bc = load_row_bcast(mk, p, a16, d["L0_attn_norm2_w"], D)
    res32 = d["xn32"] if glob else d["x132"]
    out32 = d["x132"] if glob else d["s32"]
    res_r = res32.rearrange("(n p) d -> n p d", p=128)
    out_r = out32.rearrange("(n p) d -> n p d", p=128)
    a116r = d["a116"].rearrange("(n p) d -> n p d", p=128)

    xres = [a16.alloc32(D) for _ in range(2)]
    xout = [a16.alloc32(D) for _ in range(2)]
    a1t = [a16.alloc(D) for _ in range(2)]
    sq = a16.alloc32(D)
    mv2 = a16.alloc32(4)

    p.wait("tensor", qg, wo_at)
    free2 = {}
    for i in range(NT):
        bi = i % 2
        if i >= 2:
            p.wait("sync", dve, free2[i - 2][0])
            p.wait("sync", qs, free2[i - 2][1])
        p.dma("sync", xres[bi], res_r[i], qs)
        ld = qs.n
        for half in range(2):
            bnk = bP0 if half == 0 else bP1
            if i >= 1:
                p.wait("tensor", dve, free2.get((i - 1, half), 0))
            for t in range(8):
                p.mm(bnk, attn[:, t * S + i * 128:(t * S + (i + 1) * 128)],
                     wo_sb[:, t * D + half * 512: t * D + half * 512 + 512],
                     start=(t == 0), stop=(t == 7), ctr=pe if t == 7 else None)
            p.wait("vector", pe, pe.n)
            p.wait("vector", qs, ld)
            p.v("tensor_add", xout[bi][:, half * 512:(half + 1) * 512],
                xres[bi][:, half * 512:(half + 1) * 512], bnk, ctr=dve)
            free2[(i, half)] = dve.n
        xdone = dve.n
        p.wait("sync", dve, xdone)
        p.dma("sync", out_r[i], xout[bi], qs)
        st_at = qs.n
        if glob:
            p.wait("scalar", dve, xdone)
            p.act(sq, xout[bi], AF.Square, accum_out=mv2[:, 0:1], edge=True)
            emit_rsqrt(p, mv2[:, 1:2], mv2[:, 0:1], pre_scale=1.0 / D,
                       pre_bias=1e-6)
            p.wait("vector", act, act.n)
            p.v("scalar_tensor_tensor", a1t[bi], xout[bi], mv2[:, 1:2], n2bc,
                ALU.mult, ALU.mult, ctr=dve)
            p.wait("sync", dve, dve.n)
            p.dma("sync", a116r[i], a1t[bi], qs)
        free2[i] = (dve.n, qs.n)
    a16.pop()


def make_inputs_per_core(inputs, core_b):
    import ml_dtypes
    m = {"x0": np.ascontiguousarray(np.asarray(inputs["L0_attn_hidden"])[core_b],
                                    dtype=np.float32)}
    for k, v in inputs.items():
        if k != "L0_attn_hidden":
            m[k] = np.ascontiguousarray(np.asarray(v, dtype=np.float32))
    for name, dt in CONST_SPECS.items():
        m[name] = (HOSTC[name].astype(ml_dtypes.bfloat16) if dt == BF16
                   else HOSTC[name])
    for name in HBM_CONSTS:
        m[name] = HOSTC[name].astype(ml_dtypes.bfloat16)
    return m


def run_phases(inputs, phases, dbg=(), trace=False):
    nc = bass.Bass("TRN2", target_bir_lowering=False)
    mk = MK(nc, phases, dbg=dbg)
    nc2 = mk.build()
    in_maps = [make_inputs_per_core(inputs, b) for b in range(B)]
    res = run_bass_kernel_spmd(nc2, in_maps, core_ids=list(range(B)), trace=trace)
    return res


def phase_moe(mk, p, a16, which):
    d, C = mk.d, mk.C
    qs, qg = p.sems["qs"], p.sems["qg"]
    pe, act, dve = p.sems["pe"], p.sems["act"], p.sems["dve"]
    PS = mk.PS
    a16.push()
    entry_guard(p)

    src16 = d["xn16"]
    Wr, br = d["L0_ffn_router_w"], d["L0_ffn_router_b"]
    W1, W2 = d["L0_ffn_W1"], d["L0_ffn_W2"]
    b1, b2 = d["L0_ffn_b1"], d["L0_ffn_b2"]
    acc_h = d["ffn0"]

    c32 = a16.alloc32(NT * 8)
    xfm = [a16.alloc(4096) for _ in range(2)]
    a16.push()
    rw = a16.alloc(8 * 8)
    for t in range(8):
        p.dma("gpsimd", rw[:, t * 8:(t + 1) * 8], Wr[t * 128:(t + 1) * 128, :], qg)
    rb = a16.alloc(16)
    p.dma("gpsimd", rb[0:1, 0:8], br[None, :], qg)
    rw_at = qg.n
    m8 = a16.alloc32(8)
    lgt = a16.alloc32(8)
    wv = a16.alloc32(4)
    ctmp = a16.alloc32(8)
    lgts = a16.alloc32(32)
    m8s = a16.alloc32(32)
    bR = PS[:, 0:512]

    def load_xfm(c, buf):
        for t in range(8):
            p.dma("sync", buf[:, t * 512:(t + 1) * 512],
                  src16[c * 512:(c + 1) * 512, t * 128:(t + 1) * 128],
                  qs, transpose=True)
        return qs.n

    xfm_free = [0, 0]
    for c in range(NC5):
        buf = xfm[c % 2]
        p.wait("sync", pe, xfm_free[c % 2])
        ld = load_xfm(c, buf)
        if "xfmdbg" in mk.dbg and c == 0:
            p.dma("sync", d["xfmdbg"][:, :], buf, qs)
        p.wait("tensor", qs, ld)
        if c == 0:
            p.wait("tensor", qg, rw_at)
        for ts in range(4):
            it = c * 4 + ts
            p.wait("tensor", dve, dve.n)
            for t in range(8):
                p.mm(bR[:, 0:8], buf[:, t * 512 + ts * 128: t * 512 + (ts + 1) * 128],
                     rw[:, t * 8:(t + 1) * 8], start=(t == 0), stop=False)
            p.mm(bR[:, 0:8], C["ones16"][0:1, 0:128], rb[0:1, 0:8],
                 start=False, stop=True, ctr=pe)
            p.wait("vector", pe, pe.n)
            p.v("tensor_copy", lgt, bR[:, 0:8], edge=True)
            p.v("max", m8, lgt, edge=True)
            p.v("tensor_sub", wv[:, 0:1], m8[:, 0:1], m8[:, 1:2], ctr=dve)
            p.wait("scalar", dve, dve.n)
            p.act(wv[:, 1:2], wv[:, 0:1], AF.Sigmoid, ctr=act)
            p.wait("vector", act, act.n)
            p.v("tensor_scalar", wv[:, 2:3], wv[:, 1:2], -1.0, 1.0, ALU.mult,
                ALU.add, edge=True)
            p.v("tensor_scalar", ctmp, lgt, m8[:, 0:1], wv[:, 1:2],
                ALU.is_equal, ALU.mult, edge=True)
            p.v("tensor_scalar", c32[:, it * 8:(it + 1) * 8], lgt, m8[:, 1:2],
                wv[:, 2:3], ALU.is_equal, ALU.mult, edge=True)
            p.v("tensor_add", c32[:, it * 8:(it + 1) * 8],
                c32[:, it * 8:(it + 1) * 8], ctmp, ctr=dve)
            if "routedbg" in mk.dbg and c == 0:
                p.v("tensor_copy", lgts[:, ts * 8:(ts + 1) * 8], lgt, edge=True)
                p.v("tensor_copy", m8s[:, ts * 8:(ts + 1) * 8], m8, ctr=dve)
                if ts == 3:
                    p.wait("sync", dve, dve.n)
                    p.dma("sync", d["routedbg"][:, 0:32], lgts, qs)
                    p.dma("sync", d["routedbg"][:, 32:64], m8s, qs)
        xfm_free[c % 2] = pe.n
    routing_done = p.sems["dve"].n
    a16.pop()
    p.wait("gpsimd", pe, pe.n)
    p.wait("gpsimd", dve, dve.n)
    p.wait("gpsimd", act, act.n)
    if "c32dbg" in mk.dbg:
        p.wait("sync", p.sems["dve"], routing_done)
        p.dma("sync", d["c32dbg"][:, :], c32, qs)

    w1e = a16.alloc(8 * 4096)
    w2e = a16.alloc(32 * 1024)
    b1sb = a16.alloc32(32)
    b2sb = a16.alloc(1024)
    he = a16.alloc(32 * 512)
    ysb = a16.alloc32(4 * 512)
    bank_he = [PS[:, 2048:2560], PS[:, 2560:3072]]
    bank_y = [PS[:, 0:512], PS[:, 512:1024], PS[:, 1024:1536], PS[:, 1536:2048]]

    acc_r = acc_h.rearrange("s d -> s d")
    w_free = 0
    acc_order = {}
    ysb_free = {}
    xfm_ld = [0, 0]
    for e in range(8):
        p.wait("gpsimd", pe, w_free)
        for t in range(8):
            p.dma("gpsimd", w1e[:, t * 4096:(t + 1) * 4096],
                  W1[e, t * 128:(t + 1) * 128, :], qg)
        for t in range(32):
            p.dma("gpsimd", w2e[:, t * 1024:(t + 1) * 1024],
                  W2[e, t * 128:(t + 1) * 128, :], qg)
        p.dma("gpsimd", b2sb[0:1, :], b2[e][None, :], qg)
        w_at = qg.n
        p.dma("sync", b1sb, b1[e].rearrange("(t p) -> p t", p=128), qs, slow=True)
        b1_at = qs.n

        for c in range(NC5):
            buf = xfm[c % 2]
            if c == 0:
                p.wait("sync", pe, xfm_free[0])
                ld = load_xfm(0, buf)
                xfm_ld[0] = ld
            ld = xfm_ld[c % 2]
            if c + 1 < NC5:
                p.wait("sync", pe, xfm_free[(c + 1) % 2])
                xfm_ld[(c + 1) % 2] = load_xfm(c + 1, xfm[(c + 1) % 2])
            p.wait("tensor", qs, ld)
            if c == 0:
                p.wait("tensor", qg, w_at)
                p.wait("scalar", qs, b1_at)
                if e == 0:
                    p.wait("scalar", dve, routing_done)
            he_evac = {}
            for ff in range(32):
                bnk = bank_he[ff % 2]
                if ff >= 2:
                    p.wait("tensor", act, he_evac[ff - 2])
                for t in range(8):
                    p.mm(bnk, w1e[:, t * 4096 + ff * 128: t * 4096 + (ff + 1) * 128],
                         buf[:, t * 512:(t + 1) * 512], start=(t == 0), stop=(t == 7),
                         ctr=pe if t == 7 else None)
                p.wait("scalar", pe, pe.n)
                p.act(he[:, ff * 512:(ff + 1) * 512], bnk, AF.Relu,
                      bias=b1sb[:, ff:ff + 1], ctr=act)
                he_evac[ff] = act.n
            if "hedbg" in mk.dbg and e == 0 and c == 0:
                p.wait("sync", p.sems["act"], p.sems["act"].n)
                p.dma("sync", d["hedbg"][:, :], he[:, 0:32 * 512], qs)
            for dh in range(2):
                for ts in range(4):
                    it = c * 4 + ts
                    bnk = bank_y[ts]
                    if ts in ysb_free:
                        p.wait("tensor", qg, ysb_free[ts])
                        p.wait("scalar", qg, ysb_free[ts])
                        p.wait("vector", qg, ysb_free[ts])
                    for ff in range(32):
                        if ff == 0:
                            p.wait("tensor", act, he_evac[31])
                        p.mm(bnk, he[:, ff * 512 + ts * 128: ff * 512 + (ts + 1) * 128],
                             w2e[:, ff * 1024 + dh * 512: ff * 1024 + dh * 512 + 512],
                             start=(ff == 0), stop=False)
                    p.mm(bnk, C["ones16"][0:1, 0:128], b2sb[0:1, dh * 512: dh * 512 + 512],
                         start=False, stop=True, ctr=pe)
                    mmv = pe.n
                    ysl = ysb[:, ts * 512:(ts + 1) * 512]
                    csc = c32[:, it * 8 + e: it * 8 + e + 1]
                    if dh == 0:
                        p.wait("scalar", pe, mmv)
                        p.act(ysl, bnk, AF.Copy, scale=csc, ctr=act)
                        ev = ("scalar", act.n)
                    else:
                        p.wait("vector", pe, mmv)
                        p.v("tensor_scalar_mul", ysl, bnk, csc, ctr=dve)
                        ev = ("vector", dve.n)
                    p.wait("gpsimd", p.sems[{"scalar": "act", "vector": "dve"}[ev[0]]],
                           ev[1])
                    if (it, dh) in acc_order:
                        p.wait("gpsimd", qg, acc_order[(it, dh)])
                    dst = acc_h[it * 128:(it + 1) * 128, dh * 512:(dh + 1) * 512]
                    p.dma("gpsimd", dst, ysl, qg, accum=(e > 0))
                    acc_order[(it, dh)] = qg.n
                    ysb_free[ts] = qg.n
            xfm_free[c % 2] = pe.n
        w_free = pe.n
    a16.pop()


def phase_combine(mk, p, a16, which):
    d, C = mk.d, mk.C
    qs, dve, act = p.sems["qs"], p.sems["dve"], p.sems["act"]
    a16.push()
    entry_guard(p)
    x0r = d["x0"].rearrange("(n p) d -> n p d", p=128)
    s_r = d["s32"].rearrange("(n p) d -> n p d", p=128)
    f_r = d["ffn0"].rearrange("(n p) d -> n p d", p=128)
    o_r = d["X132"].rearrange("(n p) d -> n p d", p=128)
    xb = [a16.alloc32(D) for _ in range(2)]
    sb_ = [a16.alloc32(D) for _ in range(2)]
    fb = [a16.alloc32(D) for _ in range(2)]
    gb = [a16.alloc32(D) for _ in range(2)]
    free3 = {}
    for i in range(NT):
        bi = i % 2
        if i >= 2:
            p.wait("sync", dve, free3[i - 2][0])
            p.wait("sync", qs, free3[i - 2][1])
        p.dma("sync", xb[bi], x0r[i], qs)
        p.dma("sync", sb_[bi], s_r[i], qs)
        p.dma("sync", fb[bi], f_r[i], qs)
        ld = qs.n
        p.wait("scalar", qs, ld)
        p.act(gb[bi], fb[bi], AF.Silu, ctr=act)
        p.wait("vector", act, act.n)
        p.v("tensor_mul", gb[bi], gb[bi], sb_[bi], edge=True)
        p.v("tensor_add", gb[bi], gb[bi], xb[bi], ctr=dve)
        dn = dve.n
        p.wait("sync", dve, dn)
        p.dma("sync", o_r[i], gb[bi], qs)
        free3[i] = (dn, qs.n)
    a16.pop()


def phase_E(mk, p, a16):
    d, C = mk.d, mk.C
    qs, dve, act = p.sems["qs"], p.sems["dve"], p.sems["act"]
    a16.push()
    entry_guard(p)
    wbc = load_row_bcast(mk, p, a16, d["L1_attn_w_ln1"], D)
    p.wait("vector", qs, qs.n)
    p.v("tensor_scalar_add", wbc, wbc, 1.0, ctr=dve)
    xin = [a16.alloc32(D) for _ in range(2)]
    st8 = [a16.alloc32(12) for _ in range(2)]
    mv = [a16.alloc32(4) for _ in range(2)]
    xn32t = [a16.alloc32(D) for _ in range(2)]
    xn16t = [a16.alloc(D) for _ in range(2)]
    h0t = [a16.alloc(D) for _ in range(2)]

    x0r = d["X132"].rearrange("(n p) d -> n p d", p=128)
    xn32r = d["xn1_32"].rearrange("(n p) d -> n p d", p=128)
    xn16r = d["xn1_16"].rearrange("(n p) d -> n p d", p=128)
    h0r = d["h016"].rearrange("(n p) d -> n p d", p=128)

    free_at = {}
    p.wait("sync", dve, dve.n)
    for i in range(NT):
        bi = i % 2
        if i >= 2:
            p.wait("sync", dve, free_at[i - 2][0])
            p.wait("sync", qs, free_at[i - 2][1])
        p.dma("sync", xin[bi], x0r[i], qs)
        ld = qs.n
        p.wait("vector", qs, ld)
        p.v("bn_stats", st8[bi][:, 0:6], xin[bi][:, 0:512], edge=True)
        p.v("bn_stats", st8[bi][:, 6:12], xin[bi][:, 512:1024], edge=True)
        p.v("bn_aggr", mv[bi][:, 0:2], st8[bi][:, 0:12], edge=True)
        p.wait("scalar", dve, dve.n)
        emit_rsqrt(p, mv[bi][:, 2:3], mv[bi][:, 1:2], pre_bias=1e-5)
        rstd_done = act.n
        p.v("tensor_scalar_add", mv[bi][:, 3:4], mv[bi][:, 1:2], 1e-5, edge=True)
        p.v("reciprocal", mv[bi][:, 3:4], mv[bi][:, 3:4], edge=True)
        p.v("tensor_mul", mv[bi][:, 3:4], mv[bi][:, 3:4], mv[bi][:, 1:2], edge=True)
        p.v("tensor_scalar_add", mv[bi][:, 3:4], mv[bi][:, 3:4], 1e-6, ctr=dve)
        p.wait("scalar", dve, dve.n)
        emit_rsqrt(p, mv[bi][:, 3:4], mv[bi][:, 3:4])
        rms2_done = act.n
        p.wait("vector", act, rstd_done)
        p.v("tensor_scalar", xn32t[bi], xin[bi], mv[bi][:, 0:1], mv[bi][:, 2:3],
            ALU.subtract, ALU.mult, edge=True)
        p.v("tensor_copy", xn16t[bi], xn32t[bi], ctr=dve)
        xn_done = dve.n
        p.wait("vector", act, rms2_done)
        p.v("scalar_tensor_tensor", h0t[bi], xn32t[bi], mv[bi][:, 3:4], wbc,
            ALU.mult, ALU.mult, ctr=dve)
        h0_done = dve.n
        p.wait("sync", dve, xn_done)
        p.dma("sync", xn32r[i], xn32t[bi], qs)
        p.dma("sync", xn16r[i], xn16t[bi], qs)
        p.wait("sync", dve, h0_done)
        p.dma("sync", h0r[i], h0t[bi], qs)
        free_at[i] = (dve.n, qs.n)
    a16.pop()


def phase_F(mk, p, a16):
    d, C = mk.d, mk.C
    qs, qg = p.sems["qs"], p.sems["qg"]
    pe, act, dve = p.sems["pe"], p.sems["act"], p.sems["dve"]
    PS, PT = mk.PS, mk.PT
    a16.push()
    entry_guard(p)

    wqkvz = a16.alloc(8 * 3072)
    for t in range(8):
        p.dma("gpsimd", wqkvz[:, t * 3072:(t + 1) * 3072],
              d["L1_attn_W_qkvz"][t * 128:(t + 1) * 128, :], qg)
    wba = a16.alloc(8 * 16)
    for t in range(8):
        p.dma("gpsimd", wba[:, t * 16:(t + 1) * 16],
              d["L1_attn_W_ba"][t * 128:(t + 1) * 128, :], qg)
    wout = a16.alloc(8 * D)
    for t in range(8):
        p.dma("gpsimd", wout[:, t * D:(t + 1) * D],
              d["L1_attn_W_out"][t * 128:(t + 1) * 128, :], qg)
    w_at = qg.n
    convw = a16.alloc32(64)
    p.dma("sync", convw, d["L1_attn_conv_w"].rearrange("(t p) k -> p t k", p=128), qs)
    dtb = a16.alloc32(2)
    p.dma("sync", dtb[0:8, 0:1], d["L1_attn_dt_bias"][:, None], qs)
    alog = a16.alloc32(2)
    p.dma("sync", alog[0:8, 0:1], d["L1_attn_A_log"][:, None], qs)
    gnw = a16.alloc32(2)
    p.dma("sync", gnw[:, 0:1], d["L1_attn_gate_norm_w"][:, None], qs)
    ln2bc = load_row_bcast(mk, p, a16, d["L1_attn_w_ln2"], D)
    cw_at = qs.n
    p.wait("vector", qs, cw_at)
    p.v("tensor_scalar_add", ln2bc, ln2bc, 1.0, edge=True)
    p.wait("scalar", qs, cw_at)
    p.act(alog[0:8, 1:2], alog[0:8, 0:1], AF.Exp, edge=True)
    p.wait("vector", act, act.n)
    p.v("tensor_scalar_mul", alog[0:8, 1:2], alog[0:8, 1:2], -1.0, ctr=dve)
    dg = a16.alloc(512)
    p.wait("vector", qs, cw_at)
    diag_done = p.sems["dve"].n

    hfm = [a16.alloc(4096) for _ in range(2)]
    qkvb = a16.alloc(16 * 528)
    halo = a16.alloc(48)
    zsil = a16.alloc(8 * 512)
    bas = a16.alloc32(512)
    absb = a16.alloc32(512)
    gfm = a16.alloc32(512)
    S0 = a16.alloc32(8 * 128)
    S0b = a16.alloc(8 * 128)
    qn = a16.alloc(4 * 128)
    kn = a16.alloc(4 * 128)
    ktm = a16.alloc(4 * 128)
    vtm = a16.alloc(8 * 128)
    bcol = a16.alloc32(16)
    bwork = a16.alloc32(80)
    rows = a16.alloc32(8 * 128)
    rows16 = a16.alloc(8 * 128)
    erow16 = a16.alloc(8 * 128)
    DDa = a16.alloc32(1024)
    DDc = a16.alloc32(1024)
    EMLa = a16.alloc(1024)
    EMUa = a16.alloc(1024)
    Aa = a16.alloc(1024)
    Va = a16.alloc(1024)
    VTa = a16.alloc(1024)
    V2a = a16.alloc(1024)
    V2Ta = a16.alloc(1024)
    XAPa = a16.alloc(2048)
    MKTa = Aa
    DLTa = EMUa
    QTa = EMLa
    AQTa = a16.alloc(1024)
    OSBa = V2Ta
    KPa = V2a
    sqv = a16.alloc(128)
    rr = a16.alloc32(1056)
    core = a16.alloc(8 * 128)
    xres = a16.alloc32(D)
    x2t = a16.alloc32(D)
    xht = a16.alloc(D)
    sq2 = xres
    mv2 = a16.alloc32(4)
    halo_init = False

    b0, b1 = PS[:, 0:512], PS[:, 512:1024]
    b2, b3 = PS[:, 1024:1536], PS[:, 1536:2048]
    b4, b5 = PS[:, 2048:2560], PS[:, 2560:3072]
    tpA, tpB = PT[:, 0:1024], PT[:, 1024:2048]

    src16 = d["h016"]
    xn1r = d["xn1_32"].rearrange("(n p) d -> n p d", p=128)
    x2r = d["x232"].rearrange("(n p) d -> n p d", p=128)
    xhr = d["xh16"].rearrange("(n p) d -> n p d", p=128)

    p.v("memset", S0, 0.0, edge=True)
    p.v("memset", S0b, 0.0, edge=True)
    p.v("memset", qkvb, 0.0, ctr=dve)
    init_done = dve.n
    p.wait("tensor", dve, max(init_done, diag_done))
    p.wait("tensor", qg, w_at)

    def load_hfm(c, buf):
        for t in range(8):
            p.dma("sync", buf[:, t * 512:(t + 1) * 512],
                  src16[c * 512:(c + 1) * 512, t * 128:(t + 1) * 128],
                  qs, transpose=True)
        return qs.n

    for sc in range(NC5):
        buf = hfm[sc % 2]
        p.wait("sync", pe, pe.n)
        ld = load_hfm(sc, buf)
        p.wait("tensor", qs, ld)
        for ct in range(16):
            sl = qkvb[:, ct * 528: ct * 528 + 515]
            if sc > 0:
                p.v("tensor_copy", sl[:, 0:3], halo[:, ct * 3:(ct + 1) * 3],
                    edge=True)
            bnk = b0 if ct % 2 == 0 else b1
            p.wait("tensor", dve, dve.n)
            p.wait("tensor", act, act.n)
            for t in range(8):
                p.mm(bnk, wqkvz[:, t * 3072 + ct * 128: t * 3072 + (ct + 1) * 128],
                     buf[:, t * 512:(t + 1) * 512], start=(t == 0), stop=(t == 7),
                     ctr=pe if t == 7 else None)
            p.wait("vector", pe, pe.n)
            p.v("tensor_copy", sl[:, 3:515], bnk, edge=True)
            p.v("tensor_copy", halo[:, ct * 3:(ct + 1) * 3], sl[:, 512:515],
                ctr=dve)
        for ct in range(16):
            bnk = b0 if ct % 2 == 0 else b1
            p.wait("vector", pe, pe.n)
            for tap in range(4):
                p.v("tensor_scalar_mul", dg[:, tap * 128:(tap + 1) * 128],
                    C["ident16"], convw[:, ct * 4 + tap: ct * 4 + tap + 1],
                    ctr=dve if tap == 3 else None)
            p.wait("tensor", dve, dve.n)
            p.wait("tensor", act, act.n)
            for tap in range(4):
                p.mm(bnk, dg[:, tap * 128:(tap + 1) * 128],
                     qkvb[:, ct * 528 + tap: ct * 528 + tap + 512],
                     start=(tap == 0), stop=(tap == 3), ctr=pe if tap == 3 else None)
            p.wait("scalar", pe, pe.n)
            p.act(qkvb[:, ct * 528 + 3: ct * 528 + 515], bnk, AF.Silu, ctr=act)
        for zt in range(8):
            ct = 16 + zt
            bnk = b0 if ct % 2 == 0 else b1
            p.wait("tensor", dve, dve.n)
            p.wait("tensor", act, act.n)
            for t in range(8):
                p.mm(bnk, wqkvz[:, t * 3072 + ct * 128: t * 3072 + (ct + 1) * 128],
                     buf[:, t * 512:(t + 1) * 512], start=(t == 0), stop=(t == 7),
                     ctr=pe if t == 7 else None)
            p.wait("scalar", pe, pe.n)
            p.act(zsil[:, zt * 512:(zt + 1) * 512], bnk, AF.Silu, ctr=act)
        p.wait("tensor", act, act.n)
        for t in range(8):
            p.mm(b0[0:8, :], wba[:, t * 16:t * 16 + 8],
                 buf[:, t * 512:(t + 1) * 512], start=(t == 0), stop=(t == 7))
        for t in range(8):
            p.mm(b1[0:8, :], wba[:, t * 16 + 8:t * 16 + 16],
                 buf[:, t * 512:(t + 1) * 512], start=(t == 0), stop=(t == 7),
                 ctr=pe if t == 7 else None)
        p.wait("vector", pe, pe.n)
        p.v("tensor_copy", bas[0:8, :], b0[0:8, :], edge=True)
        p.v("tensor_copy", absb[0:8, :], b1[0:8, :], edge=True)
        p.wait("scalar", dve, dve.n)
        p.act(bas[0:8, :], bas[0:8, :], AF.Sigmoid, edge=True)
        p.act(gfm[0:8, :], absb[0:8, :], AF.Abs, bias=dtb[0:8, 0:1], edge=True)
        p.act(gfm[0:8, :], gfm[0:8, :], AF.Exp, scale=-1.0, edge=True)
        p.act(gfm[0:8, :], gfm[0:8, :], AF.Ln, bias=1.0, ctr=act)
        sp_done = act.n
        p.wait("vector", act, sp_done)
        p.v("tensor_scalar", absb[0:8, :], absb[0:8, :], dtb[0:8, 0:1], 0.0,
            ALU.add, ALU.max, edge=True)
        p.v("tensor_add", gfm[0:8, :], gfm[0:8, :], absb[0:8, :], edge=True)
        p.v("tensor_scalar_mul", gfm[0:8, :], gfm[0:8, :], alog[0:8, 1:2], ctr=dve)
        g_done = dve.n

        for dc in range(4):
            tk = dc * 128
            ti = sc * 4 + dc
            p.wait("tensor", dve, g_done)
            p.wait("tensor", act, act.n)
            p.trans(b2[0:128, 0:8], gfm[0:8, tk:tk + 128], C["ident32"][0:8, 0:8], ctr=pe)
            p.wait("vector", pe, pe.n)
            p.v("tensor_copy", bwork[:, 0:8], b2[:, 0:8], ctr=dve)
            p.wait("tensor", dve, dve.n)
            p.mm(b3[:, 0:8], C["cum_ut32"], bwork[:, 0:8], start=True, stop=True)
            p.trans(b2[0:128, 8:16], bas[0:8, tk:tk + 128], C["ident32"][0:8, 0:8], ctr=pe)
            p.wait("vector", pe, pe.n)
            p.v("tensor_copy", bcol[:, 0:8], b3[:, 0:8], edge=True)
            p.v("tensor_copy", bwork[:, 8:16], b2[:, 8:16], edge=True)
            p.v("tensor_scalar_mul", bwork[:, 16:24], bwork[:, 8:16], -1.0, edge=True)
            p.wait("scalar", dve, dve.n)
            p.act(bwork[:, 24:32], bcol[:, 0:8], AF.Exp, ctr=act)
            p.wait("vector", act, act.n)
            p.v("tensor_mul", bwork[:, 32:40], bwork[:, 16:24], bwork[:, 24:32],
                ctr=dve)
            percol_done = dve.n
            p.wait("tensor", dve, percol_done)
            for h in range(8):
                p.trans(b2[0:1, 256:384], bcol[:, h:h + 1], C["ident32"], ctr=pe)
                p.wait("vector", pe, pe.n)
                p.v("tensor_copy", rows[0:1, h * 128:(h + 1) * 128],
                    b2[0:1, 256:384], ctr=dve)
                p.wait("tensor", dve, dve.n)
            rows_done = p.sems["dve"].n
            p.wait("scalar", dve, rows_done)
            p.act(erow16[0:1, :], rows[0:1, :], AF.Exp, ctr=act)
            p.wait("vector", act, act.n)
            for h in range(8):
                p.v("tensor_copy", rr[0:1, 256 + h:257 + h],
                    rows[0:1, h * 128 + 127: h * 128 + 128],
                    ctr=dve if h == 7 else None)
            p.wait("tensor", dve, dve.n)
            p.mm(b5[:, 0:8], C["ones32"][0:1, 0:128], rr[0:1, 256:264],
                 start=True, stop=True, ctr=pe)
            p.wait("vector", pe, pe.n)
            p.v("tensor_copy", bwork[:, 40:48], b5[:, 0:8], edge=True)
            p.v("tensor_sub", bwork[:, 48:56], bwork[:, 40:48], bcol[:, 0:8], ctr=dve)
            p.wait("scalar", dve, dve.n)
            p.act(bwork[:, 56:64], bwork[:, 48:56], AF.Exp, edge=True)
            p.act(bwork[:, 64:72], bwork[:, 40:48], AF.Exp, ctr=act)
            percol2_done = act.n

            p.wait("tensor", act, percol2_done)
            for hh in range(4):
                qsl = qkvb[:, hh * 528 + 3 + tk: hh * 528 + 3 + tk + 128]
                ksl = qkvb[:, (4 + hh) * 528 + 3 + tk: (4 + hh) * 528 + 3 + tk + 128]
                for (xi, xsl, osl, qscale) in ((0, qsl, qn, True), (1, ksl, kn, False)):
                    dst = osl[:, hh * 128:(hh + 1) * 128]
                    p.wait("vector", pe, pe.n)
                    p.v("tensor_mul", sqv, xsl, xsl, ctr=dve)
                    p.wait("tensor", dve, dve.n)
                    p.mm(b2[0:1, 256:384], C["ones16"][:, 0:1], sqv,
                         start=True, stop=True, ctr=pe)
                    p.wait("vector", pe, pe.n)
                    p.v("tensor_copy", rr[0:1, 0:128], b2[0:1, 256:384], ctr=dve)
                    p.wait("scalar", dve, dve.n)
                    sc_ = (128.0 ** -0.5) if qscale else 1.0
                    emit_rsqrt(p, rr[0:1, 128:256], rr[0:1, 0:128], pre_bias=1e-6,
                               post_scale=sc_)
                    p.wait("vector", act, act.n)
                    p.v("tensor_copy", rows16[0:1, 0:128], rr[0:1, 128:256],
                        ctr=dve)
                    p.wait("tensor", dve, dve.n)
                    p.mm(b3[:, 256:384], C["ones16"][0:1, 0:128],
                         rows16[0:1, 0:128], start=True, stop=True, ctr=pe)
                    p.wait("vector", pe, pe.n)
                    p.v("tensor_mul", dst, xsl, b3[:, 256:384], ctr=dve)
                p.wait("tensor", dve, dve.n)
                p.trans(tpB[:, 0:128], kn[:, hh * 128:(hh + 1) * 128], C["ident16"],
                        ctr=pe)
                p.wait("vector", pe, pe.n)
                p.v("tensor_copy", ktm[:, hh * 128:(hh + 1) * 128], tpB[:, 0:128],
                    ctr=dve)
            for h in range(8):
                vsl = qkvb[:, (8 + h) * 528 + 3 + tk: (8 + h) * 528 + 3 + tk + 128]
                p.wait("tensor", dve, dve.n)
                p.trans(tpB[:, 128:256], vsl, C["ident16"], ctr=pe)
                p.wait("vector", pe, pe.n)
                p.v("tensor_copy", vtm[:, h * 128:(h + 1) * 128], tpB[:, 128:256],
                    ctr=dve)
            prep_done = dve.n

            rbrow = rows16
            p.wait("tensor", dve, dve.n)
            p.wait("tensor", act, act.n)
            for h in range(8):
                p.mm(PS[:, 1024 + h * 128:1024 + (h + 1) * 128],
                     C["ones32"][0:1, 0:128], rows[0:1, h * 128:(h + 1) * 128],
                     start=True, stop=True, ctr=pe if h == 7 else None)
            p.wait("vector", pe, pe.n)
            for h in range(8):
                p.v("tensor_scalar", DDa[:, h * 128:(h + 1) * 128],
                    PS[:, 1024 + h * 128:1024 + (h + 1) * 128], bcol[:, h:h + 1],
                    None, ALU.subtract, ctr=dve if h == 7 else None)
            p.wait("vector", dve, dve.n)
            p.v("tensor_scalar", DDc, DDa, -1.0, 0.0, ALU.mult, ALU.min, ctr=dve)
            p.wait("scalar", dve, dve.n)
            p.act(DDc, DDc, AF.Exp, ctr=act)
            p.wait("vector", act, act.n)
            mlow_r = C["m_lows32"].rearrange("p (o f) -> p o f", o=1).to_broadcast([128, 8, 128])
            mupi_r = C["m_upi32"].rearrange("p (o f) -> p o f", o=1).to_broadcast([128, 8, 128])
            p.v("tensor_tensor", EMLa.rearrange("p (h f) -> p h f", f=128),
                DDc.rearrange("p (h f) -> p h f", f=128), mlow_r, ALU.mult,
                edge=True)
            p.v("tensor_scalar_min", DDc, DDa, 0.0, ctr=dve)
            p.wait("scalar", dve, dve.n)
            p.act(DDc, DDc, AF.Exp, ctr=act)
            p.wait("vector", act, act.n)
            p.v("tensor_tensor", EMUa.rearrange("p (h f) -> p h f", f=128),
                DDc.rearrange("p (h f) -> p h f", f=128), mupi_r, ALU.mult,
                ctr=dve)
            p.wait("tensor", dve, dve.n)
            for h in range(8):
                kf = kn[:, (h // 2) * 128:(h // 2 + 1) * 128]
                qf = qn[:, (h // 2) * 128:(h // 2 + 1) * 128]
                p.mm(PS[:, 1024 + h * 128:1024 + (h + 1) * 128], kf, kf,
                     start=True, stop=True)
                p.mm(PS[:, 2048 + h * 128:2048 + (h + 1) * 128], kf, qf,
                     start=True, stop=True, ctr=pe if h == 7 else None)
            p.wait("vector", pe, pe.n)
            p.v("tensor_copy", Aa, PS[:, 1024:2048], edge=True)
            p.v("tensor_tensor", AQTa, PS[:, 2048:3072], EMUa, ALU.mult, edge=True)
            for h in range(8):
                p.v("scalar_tensor_tensor", Va[:, h * 128:(h + 1) * 128],
                    Aa[:, h * 128:(h + 1) * 128], bwork[:, 16 + h:17 + h],
                    EMLa[:, h * 128:(h + 1) * 128], ALU.mult, ALU.mult,
                    ctr=dve if h == 7 else None)
            p.wait("tensor", dve, dve.n)
            for h in range(8):
                p.trans(tpA[:, h * 128:(h + 1) * 128], Va[:, h * 128:(h + 1) * 128],
                        C["ident16"], ctr=pe if h == 7 else None)
            p.wait("vector", pe, pe.n)
            p.v("tensor_copy", VTa, tpA[:, 0:1024], edge=True)
            for h in range(8):
                p.v("tensor_scalar_mul", XAPa[:, h * 256:h * 256 + 128],
                    vtm[:, h * 128:(h + 1) * 128], bwork[:, 8 + h:9 + h])
                p.v("tensor_scalar_mul", XAPa[:, h * 256 + 128:h * 256 + 256],
                    ktm[:, (h // 2) * 128:(h // 2 + 1) * 128],
                    bwork[:, 32 + h:33 + h], ctr=dve if h == 7 else None)
            p.wait("tensor", dve, dve.n)
            vc, vtc, v2c, v2tc = Va, VTa, V2a, V2Ta
            for itn in range(7):
                for h in range(8):
                    hb = PS[:, h * 256:(h + 1) * 256]
                    p.mm(hb, C["ident16"], XAPa[:, h * 256:(h + 1) * 256],
                         start=True, stop=False)
                    p.mm(hb, vtc[:, h * 128:(h + 1) * 128],
                         XAPa[:, h * 256:(h + 1) * 256],
                         start=False, stop=True,
                         ctr=pe if (h == 7 and itn == 6) else None)
                if itn < 6:
                    for h in range(8):
                        p.mm(PS[:, 2048 + h * 128:2048 + (h + 1) * 128],
                             vtc[:, h * 128:(h + 1) * 128],
                             vc[:, h * 128:(h + 1) * 128], start=True, stop=True,
                             ctr=pe if h == 7 else None)
                p.wait("vector", pe, pe.n)
                p.v("tensor_copy", XAPa, PS[:, 0:2048], edge=True)
                if itn < 6:
                    p.v("tensor_copy", v2c, PS[:, 2048:3072], ctr=dve)
                    p.wait("tensor", dve, dve.n)
                    for h in range(8):
                        p.mm(PS[:, 2048 + h * 128:2048 + (h + 1) * 128],
                             vc[:, h * 128:(h + 1) * 128],
                             vtc[:, h * 128:(h + 1) * 128], start=True, stop=True,
                             ctr=pe if h == 7 else None)
                    p.wait("vector", pe, pe.n)
                    p.v("tensor_copy", v2tc, PS[:, 2048:3072], ctr=dve)
                    vc, v2c = v2c, vc
                    vtc, v2tc = v2tc, vtc
                else:
                    p.v("engine_nop", ctr=dve)
                p.wait("tensor", dve, dve.n)
            for h in range(8):
                p.trans(tpB[:, h * 128:(h + 1) * 128],
                        XAPa[:, h * 256 + 128:h * 256 + 256], C["ident16"],
                        ctr=pe if h == 7 else None)
            p.wait("vector", pe, pe.n)
            p.v("tensor_copy", MKTa, tpB[:, 0:1024], ctr=dve)
            p.wait("tensor", dve, dve.n)
            for h in range(8):
                p.mm(PS[:, h * 128:(h + 1) * 128], MKTa[:, h * 128:(h + 1) * 128],
                     S0b[:, h * 128:(h + 1) * 128], start=True, stop=True,
                     ctr=pe if h == 7 else None)
            p.wait("vector", pe, pe.n)
            p.v("tensor_tensor", DLTa.rearrange("p (h f) -> p h f", f=128),
                PS[:, 0:1024].rearrange("p (h f) -> p h f", f=128),
                XAPa.rearrange("p (h f) -> p h f", f=256)[:, :, 0:128],
                ALU.add, ctr=dve)
            p.wait("tensor", dve, dve.n)
            for h in range(8):
                p.mm(PS[:, 1024 + h * 128:1024 + (h + 1) * 128],
                     C["ones16"][0:1, 0:128], erow16[0:1, h * 128:(h + 1) * 128],
                     start=True, stop=True, ctr=pe if h == 7 else None)
            p.wait("vector", pe, pe.n)
            p.v("tensor_tensor", QTa.rearrange("p (h r f) -> p h r f", r=2, f=128),
                qn.rearrange("p (h o f) -> p h o f", o=1, f=128)
                  .to_broadcast([128, 4, 2, 128]),
                PS[:, 1024:2048].rearrange("p (h r f) -> p h r f", r=2, f=128),
                ALU.mult, ctr=dve)
            p.wait("tensor", dve, dve.n)
            for h in range(8):
                ob = PS[:, 1024 + h * 128:1024 + (h + 1) * 128]
                p.mm(ob, S0b[:, h * 128:(h + 1) * 128],
                     QTa[:, h * 128:(h + 1) * 128], start=True, stop=False)
                p.mm(ob, DLTa[:, h * 128:(h + 1) * 128],
                     AQTa[:, h * 128:(h + 1) * 128], start=False, stop=True,
                     ctr=pe if h == 7 else None)
            p.wait("vector", pe, pe.n)
            p.v("tensor_copy", OSBa, PS[:, 1024:2048], edge=True)
            for h in range(8):
                p.v("tensor_scalar_mul", KPa[:, h * 128:(h + 1) * 128],
                    ktm[:, (h // 2) * 128:(h // 2 + 1) * 128],
                    bwork[:, 56 + h:57 + h], ctr=dve if h == 7 else None)
            p.wait("tensor", dve, dve.n)
            for h in range(8):
                p.mm(PS[:, 2048 + h * 128:2048 + (h + 1) * 128],
                     KPa[:, h * 128:(h + 1) * 128], DLTa[:, h * 128:(h + 1) * 128],
                     start=True, stop=True, ctr=pe if h == 7 else None)
            p.wait("vector", pe, pe.n)
            for h in range(8):
                p.v("scalar_tensor_tensor", S0[:, h * 128:(h + 1) * 128],
                    S0[:, h * 128:(h + 1) * 128], bwork[:, 64 + h:65 + h],
                    PS[:, 2048 + h * 128:2048 + (h + 1) * 128], ALU.mult, ALU.add,
                    ctr=dve if h == 7 else None)
            p.wait("vector", dve, dve.n)
            p.v("tensor_copy", S0b, S0, edge=True)
            p.v("tensor_mul", KPa, OSBa, OSBa, ctr=dve)
            p.wait("tensor", dve, dve.n)
            p.mm(PS[0:1, 0:512], C["ones16"][:, 0:1], KPa[:, 0:512],
                 start=True, stop=True)
            p.mm(PS[0:1, 512:1024], C["ones16"][:, 0:1], KPa[:, 512:1024],
                 start=True, stop=True, ctr=pe)
            p.wait("scalar", pe, pe.n)
            emit_rsqrt(p, rr[0:1, 0:1024], PS[0:1, 0:1024], pre_scale=1.0 / 128,
                       pre_bias=1e-6)
            p.wait("vector", act, act.n)
            p.v("tensor_copy", rbrow[0:1, 0:1024], rr[0:1, 0:1024], ctr=dve)
            p.wait("tensor", dve, dve.n)
            for h in range(8):
                p.mm(PS[:, 1024 + h * 128:1024 + (h + 1) * 128],
                     C["ones16"][0:1, 0:128], rbrow[0:1, h * 128:(h + 1) * 128],
                     start=True, stop=True, ctr=pe if h == 7 else None)
            p.wait("vector", pe, pe.n)
            p.v("scalar_tensor_tensor", OSBa, OSBa, gnw[:, 0:1], PS[:, 1024:2048],
                ALU.mult, ALU.mult, edge=True)
            p.v("tensor_tensor", core.rearrange("p (h f) -> p h f", f=128),
                OSBa.rearrange("p (h f) -> p h f", f=128),
                zsil.rearrange("p (h f) -> p h f", f=512)[:, :, tk:tk + 128],
                ALU.mult, ctr=dve)
            core_done = dve.n

            p.dma("sync", xres, xn1r[ti], qs)
            ldx = qs.n
            p.wait("tensor", dve, core_done)
            for half in range(2):
                bnk = b0 if half == 0 else b1
                for t in range(8):
                    p.mm(bnk, core[:, t * 128:(t + 1) * 128],
                         wout[:, t * D + half * 512: t * D + half * 512 + 512],
                         start=(t == 0), stop=(t == 7), ctr=pe if t == 7 else None)
                p.wait("vector", pe, pe.n)
                p.wait("vector", qs, ldx)
                p.v("tensor_add", x2t[:, half * 512:(half + 1) * 512],
                    xres[:, half * 512:(half + 1) * 512], bnk, ctr=dve)
            x2_done = dve.n
            p.wait("sync", dve, x2_done)
            p.dma("sync", x2r[ti], x2t, qs)
            p.wait("scalar", dve, x2_done)
            p.act(sq2, x2t, AF.Square, accum_out=mv2[:, 0:1], edge=True)
            emit_rsqrt(p, mv2[:, 1:2], mv2[:, 0:1], pre_scale=1.0 / D,
                       pre_bias=1e-6)
            p.wait("vector", act, act.n)
            p.v("scalar_tensor_tensor", xht, x2t, mv2[:, 1:2], ln2bc,
                ALU.mult, ALU.mult, ctr=dve)
            p.wait("sync", dve, dve.n)
            p.dma("sync", xhr[ti], xht, qs)
    a16.pop()


def phase_prepass(mk, p, a16):
    d = mk.d
    qs, qg = p.sems["qs"], p.sems["qg"]
    a16.push()
    entry_guard(p)
    stg = [a16.alloc(4096) for _ in range(2)]
    jobs = [
        (d["L1_ffn_shared_w1"], d["sw1_16"], 4096, 1024),
        (d["L1_ffn_shared_w2"], d["sw2_16"], 1024, 4096),
        (d["L1_ffn_experts_w1"].rearrange("e f d -> (e f) d"), d["ew1_16"], 8 * 4096, 1024),
        (d["L1_ffn_experts_w2"].rearrange("e d f -> (e d) f"), d["ew2_16"], 8 * 1024, 4096),
        (d["L1_ffn_gate_w"], d["gw_16"], 8, 1024),
    ]
    pend = {}
    i = 0
    for srcT, dstT, R, W in jobs:
        r = 0
        while r < R:
            rr_ = min(128, R - r)
            bi = i % 2
            if i >= 2:
                p.wait("gpsimd", qs, pend[i - 2])
            p.dma("gpsimd", stg[bi][0:rr_, 0:W], srcT[r:r + rr_, :], qg)
            p.wait("sync", qg, p.sems["qg"].n)
            p.dma("sync", dstT[r:r + rr_, :], stg[bi][0:rr_, 0:W], qs)
            pend[i] = p.sems["qs"].n
            r += rr_
            i += 1
    a16.pop()


def phase_moe2(mk, p, a16, which):
    d, C = mk.d, mk.C
    qs, qg = p.sems["qs"], p.sems["qg"]
    pe, act, dve = p.sems["pe"], p.sems["act"], p.sems["dve"]
    PS = mk.PS
    C = mk.C
    a16.push()
    entry_guard(p)

    if which == "G":
        src16, acc_h = d["xh16"], d["moe1"]
        NFF, glu = 4, True
    else:
        src16, acc_h = d["xn1_16"], d["ffn1"]
        NFF, glu = 32, False

    c32 = a16.alloc32(NT * 8)
    xfm = [a16.alloc(4096) for _ in range(2)]
    a16.push()
    rw = a16.alloc(8 * 8)
    if which == "G":
        for t in range(8):
            p.dma("gpsimd", rw[:, t * 8:(t + 1) * 8],
                  d["L1_attn_W_router"][t * 128:(t + 1) * 128, :], qg)
    else:
        gwrow = a16.alloc(1024)
        p.dma("sync", gwrow[0:8, :], d["gw_16"][:, :], qs)
        p.wait("tensor", qs, qs.n)
        for t in range(8):
            p.trans(mk.PT[0:128, 0:8], gwrow[0:8, t * 128:(t + 1) * 128],
                    C["ident16"][0:8, 0:8], ctr=pe)
            p.wait("vector", pe, pe.n)
            p.v("tensor_copy", rw[:, t * 8:(t + 1) * 8], mk.PT[0:128, 0:8], ctr=dve)
            p.wait("tensor", dve, dve.n)
    rw_at = (qg.n, qs.n)
    m8 = a16.alloc32(8)
    lgt = a16.alloc32(8)
    wv = a16.alloc32(4)
    ctmp = a16.alloc32(8)
    bR = PS[:, 0:512]

    def load_xfm(c, buf):
        for t in range(8):
            p.dma("sync", buf[:, t * 512:(t + 1) * 512],
                  src16[c * 512:(c + 1) * 512, t * 128:(t + 1) * 128],
                  qs, transpose=True)
        return qs.n

    xfm_free = [0, 0]
    for c in range(NC5):
        buf = xfm[c % 2]
        p.wait("sync", pe, xfm_free[c % 2])
        ld = load_xfm(c, buf)
        p.wait("tensor", qs, ld)
        if c == 0:
            p.wait("tensor", qg, rw_at[0])
            p.wait("tensor", qs, rw_at[1])
        for ts in range(4):
            it = c * 4 + ts
            p.wait("tensor", dve, dve.n)
            for t in range(8):
                p.mm(bR[:, 0:8], buf[:, t * 512 + ts * 128: t * 512 + (ts + 1) * 128],
                     rw[:, t * 8:(t + 1) * 8], start=(t == 0), stop=(t == 7),
                     ctr=pe if t == 7 else None)
            p.wait("vector", pe, pe.n)
            p.v("tensor_copy", lgt, bR[:, 0:8], edge=True)
            p.v("max", m8, lgt, edge=True)
            p.v("tensor_sub", wv[:, 0:1], m8[:, 0:1], m8[:, 1:2], ctr=dve)
            p.wait("scalar", dve, dve.n)
            p.act(wv[:, 1:2], wv[:, 0:1], AF.Sigmoid, ctr=act)
            p.wait("vector", act, act.n)
            p.v("tensor_scalar", wv[:, 2:3], wv[:, 1:2], -1.0, 1.0, ALU.mult,
                ALU.add, edge=True)
            p.v("tensor_scalar", ctmp, lgt, m8[:, 0:1], wv[:, 1:2],
                ALU.is_equal, ALU.mult, edge=True)
            p.v("tensor_scalar", c32[:, it * 8:(it + 1) * 8], lgt, m8[:, 1:2],
                wv[:, 2:3], ALU.is_equal, ALU.mult, edge=True)
            p.v("tensor_add", c32[:, it * 8:(it + 1) * 8],
                c32[:, it * 8:(it + 1) * 8], ctmp, ctr=dve)
        xfm_free[c % 2] = pe.n
    routing_done = p.sems["dve"].n
    a16.pop()
    p.wait("gpsimd", pe, pe.n)
    p.wait("gpsimd", dve, dve.n)
    p.wait("gpsimd", act, act.n)
    p.wait("sync", dve, dve.n)

    if which == "G":
        w1e = a16.alloc(8 * 512)
        wue = a16.alloc(8 * 512)
        w2e = a16.alloc(4 * 1024)
    else:
        w1e = a16.alloc(8 * 4096)
        w2e = a16.alloc(32 * 1024)
    he = a16.alloc(NFF * 512)
    hetmp = a16.alloc(512)
    ysb = a16.alloc32(4 * 512)
    bank_he = [PS[:, 2048:2560], PS[:, 2560:3072]]
    bank_y = [PS[:, 0:512], PS[:, 512:1024], PS[:, 1024:1536], PS[:, 1536:2048]]

    w_free = 0
    acc_order = {}
    ysb_free = {}
    xfm_ld = [0, 0]
    elist = list(range(8)) + (["sh"] if which == "H" else [])
    for e in elist:
        p.wait("gpsimd", pe, w_free)
        p.wait("sync", pe, w_free)
        if which == "G":
            for t in range(8):
                p.dma("gpsimd", w1e[:, t * 512:(t + 1) * 512],
                      d["L1_attn_W_gate_e"][e, t * 128:(t + 1) * 128, :], qg)
                p.dma("gpsimd", wue[:, t * 512:(t + 1) * 512],
                      d["L1_attn_W_up_e"][e, t * 128:(t + 1) * 128, :], qg)
            for t in range(4):
                p.dma("gpsimd", w2e[:, t * 1024:(t + 1) * 1024],
                      d["L1_attn_W_down_e"][e, t * 128:(t + 1) * 128, :], qg)
            w_at = (qg.n, qs.n)
        else:
            if e == "sh":
                w1src, w2src, ro1, ro2 = d["sw1_16"], d["sw2_16"], 0, 0
            else:
                w1src, w2src = d["ew1_16"], d["ew2_16"]
                ro1, ro2 = e * 4096, e * 1024
            for t in range(8):
                p.dma("sync", w1e[:, t * 4096:(t + 1) * 4096],
                      w1src[ro1:ro1 + 4096, t * 128:(t + 1) * 128], qs,
                      transpose=True)
            for t in range(32):
                p.dma("sync", w2e[:, t * 1024:(t + 1) * 1024],
                      w2src[ro2:ro2 + 1024, t * 128:(t + 1) * 128], qs,
                      transpose=True)
            w_at = (qg.n, qs.n)

        for c in range(NC5):
            buf = xfm[c % 2]
            if c == 0:
                p.wait("sync", pe, xfm_free[0])
                xfm_ld[0] = load_xfm(0, buf)
            ld = xfm_ld[c % 2]
            if c + 1 < NC5:
                p.wait("sync", pe, xfm_free[(c + 1) % 2])
                xfm_ld[(c + 1) % 2] = load_xfm(c + 1, xfm[(c + 1) % 2])
            p.wait("tensor", qs, ld)
            if c == 0:
                p.wait("tensor", qg, w_at[0])
                p.wait("tensor", qs, w_at[1])
                if e == 0:
                    p.wait("scalar", dve, routing_done)
                    p.wait("vector", dve, routing_done)
            he_evac = {}
            for ff in range(NFF):
                bnk = bank_he[ff % 2] if not glu else bank_he[0]
                if glu:
                    bup = bank_he[1]
                    if ff >= 1:
                        p.wait("tensor", dve, he_evac[ff - 1])
                    for t in range(8):
                        p.mm(bnk, w1e[:, t * 512 + ff * 128: t * 512 + (ff + 1) * 128],
                             buf[:, t * 512:(t + 1) * 512], start=(t == 0),
                             stop=(t == 7))
                    for t in range(8):
                        p.mm(bup, wue[:, t * 512 + ff * 128: t * 512 + (ff + 1) * 128],
                             buf[:, t * 512:(t + 1) * 512], start=(t == 0),
                             stop=(t == 7), ctr=pe if t == 7 else None)
                    p.wait("scalar", pe, pe.n)
                    p.act(hetmp, bnk, AF.Silu, ctr=act)
                    p.wait("vector", act, act.n)
                    p.v("tensor_mul", he[:, ff * 512:(ff + 1) * 512], hetmp, bup,
                        ctr=dve)
                    he_evac[ff] = dve.n
                else:
                    if ff >= 2:
                        p.wait("tensor", act, he_evac[ff - 2])
                    for t in range(8):
                        p.mm(bnk, w1e[:, t * 4096 + ff * 128: t * 4096 + (ff + 1) * 128],
                             buf[:, t * 512:(t + 1) * 512], start=(t == 0),
                             stop=(t == 7), ctr=pe if t == 7 else None)
                    p.wait("scalar", pe, pe.n)
                    p.act(he[:, ff * 512:(ff + 1) * 512], bnk, AF.Silu, ctr=act)
                    he_evac[ff] = act.n
            he_all = he_evac[NFF - 1]
            for dh in range(2):
                for ts in range(4):
                    it = c * 4 + ts
                    bnk = bank_y[ts]
                    if ts in ysb_free:
                        p.wait("tensor", qg, ysb_free[ts])
                        p.wait("scalar", qg, ysb_free[ts])
                        p.wait("vector", qg, ysb_free[ts])
                    for ff in range(NFF):
                        if ff == 0:
                            if glu:
                                p.wait("tensor", dve, he_all)
                            else:
                                p.wait("tensor", act, he_all)
                        p.mm(bnk, he[:, ff * 512 + ts * 128: ff * 512 + (ts + 1) * 128],
                             w2e[:, ff * 1024 + dh * 512: ff * 1024 + dh * 512 + 512],
                             start=(ff == 0), stop=(ff == NFF - 1),
                             ctr=pe if ff == NFF - 1 else None)
                    mmv = pe.n
                    ysl = ysb[:, ts * 512:(ts + 1) * 512]
                    if e == "sh":
                        p.wait("scalar", pe, mmv)
                        p.act(ysl, bnk, AF.Copy, ctr=act)
                        ev = ("act", act.n)
                    else:
                        csc = c32[:, it * 8 + e: it * 8 + e + 1]
                        if dh == 0:
                            p.wait("scalar", pe, mmv)
                            p.act(ysl, bnk, AF.Copy, scale=csc, ctr=act)
                            ev = ("act", act.n)
                        else:
                            p.wait("vector", pe, mmv)
                            p.v("tensor_scalar_mul", ysl, bnk, csc, ctr=dve)
                            ev = ("dve", dve.n)
                    p.wait("gpsimd", p.sems[ev[0]], ev[1])
                    if (it, dh) in acc_order:
                        p.wait("gpsimd", qg, acc_order[(it, dh)])
                    dst = acc_h[it * 128:(it + 1) * 128, dh * 512:(dh + 1) * 512]
                    p.dma("gpsimd", dst, ysl, qg, accum=(e != 0))
                    acc_order[(it, dh)] = qg.n
                    ysb_free[ts] = qg.n
            xfm_free[c % 2] = pe.n
        w_free = pe.n
    a16.pop()


def phase_add(mk, p, a16, out_name, a_name, b_name):
    d = mk.d
    qs, dve = p.sems["qs"], p.sems["dve"]
    a16.push()
    entry_guard(p)
    a_r = d[a_name].rearrange("(n p) d -> n p d", p=128)
    b_r = d[b_name].rearrange("(n p) d -> n p d", p=128)
    o_r = d[out_name].rearrange("(n p) d -> n p d", p=128)
    ab = [a16.alloc32(D) for _ in range(2)]
    bb = [a16.alloc32(D) for _ in range(2)]
    free3 = {}
    for i in range(NT):
        bi = i % 2
        if i >= 2:
            p.wait("sync", dve, free3[i - 2][0])
            p.wait("sync", qs, free3[i - 2][1])
        p.dma("sync", ab[bi], a_r[i], qs)
        p.dma("sync", bb[bi], b_r[i], qs)
        ld = qs.n
        p.wait("vector", qs, ld)
        p.v("tensor_add", ab[bi], ab[bi], bb[bi], ctr=dve)
        dn = dve.n
        p.wait("sync", dve, dn)
        p.dma("sync", o_r[i], ab[bi], qs)
        free3[i] = (dn, qs.n)
    a16.pop()


def phase_combine2(mk, p, a16, out_name, x_name, s_name, f_name):
    d = mk.d
    qs, dve, act = p.sems["qs"], p.sems["dve"], p.sems["act"]
    a16.push()
    entry_guard(p)
    x_r = d[x_name].rearrange("(n p) d -> n p d", p=128)
    s_r = d[s_name].rearrange("(n p) d -> n p d", p=128)
    f_r = d[f_name].rearrange("(n p) d -> n p d", p=128)
    o_r = d[out_name].rearrange("(n p) d -> n p d", p=128)
    xb = [a16.alloc32(D) for _ in range(2)]
    sb_ = [a16.alloc32(D) for _ in range(2)]
    fb = [a16.alloc32(D) for _ in range(2)]
    gb = [a16.alloc32(D) for _ in range(2)]
    free3 = {}
    for i in range(NT):
        bi = i % 2
        if i >= 2:
            p.wait("sync", dve, free3[i - 2][0])
            p.wait("sync", qs, free3[i - 2][1])
        p.dma("sync", xb[bi], x_r[i], qs)
        p.dma("sync", sb_[bi], s_r[i], qs)
        p.dma("sync", fb[bi], f_r[i], qs)
        ld = qs.n
        p.wait("scalar", qs, ld)
        p.act(gb[bi], fb[bi], AF.Silu, ctr=act)
        p.wait("vector", act, act.n)
        p.v("tensor_mul", gb[bi], gb[bi], sb_[bi], edge=True)
        p.v("tensor_add", gb[bi], gb[bi], xb[bi], ctr=dve)
        dn = dve.n
        p.wait("sync", dve, dn)
        p.dma("sync", o_r[i], gb[bi], qs)
        free3[i] = (dn, qs.n)
    a16.pop()


def kernel(**inputs):
    nc = bass.Bass("TRN2", target_bir_lowering=False)
    m = MK(nc, phases=("A", "B", "C", "D", "E", "F", "G", "H"))
    nc2 = m.build()
    in_maps = [make_inputs_per_core(inputs, b) for b in range(B)]
    res = run_bass_kernel_spmd(nc2, in_maps, core_ids=list(range(B)))
    return np.stack([res.results[b]["out"] for b in range(B)], axis=0)


# version 05
# speedup vs baseline: 1.0250x; 1.0250x over previous
import numpy as np
import concourse.bass as bass
import concourse.mybir as mybir
from concourse.bass_utils import run_bass_kernel_spmd

F32 = mybir.dt.float32
BF16 = mybir.dt.bfloat16
AF = mybir.ActivationFunctionType
ALU = mybir.AluOpType

import os
B, S, D = 8, int(os.environ.get("MK_S", "4096")), 1024
NT = S // 128
NC5 = S // 512

def host_constants():
    c = {}
    c["ident16"] = np.eye(128, dtype=np.float32)
    c["ident32"] = np.eye(128, dtype=np.float32)
    c["ones16"] = np.ones((128, 128), dtype=np.float32)
    c["ones32"] = np.ones((128, 128), dtype=np.float32)
    c["cum_ut32"] = np.triu(np.ones((128, 128), dtype=np.float32))
    for tag, base in (("g", 1e6), ("l", 1e4)):
        inv = 1.0 / base ** (np.arange(0, 64, 2, dtype=np.float32) / 64)
        fr = np.arange(S, dtype=np.float32)[:, None] * inv[None, :]
        cos, sin = np.cos(fr), np.sin(fr)
        Ct = np.zeros((S, 64), np.float32); St = np.zeros((S, 64), np.float32)
        Ct[:, 0::2] = cos; Ct[:, 1::2] = cos
        St[:, 0::2] = -sin; St[:, 1::2] = sin
        c["rope_c_" + tag] = Ct
        c["rope_s_" + tag] = St
    kr = np.arange(128)[:, None]; qf = np.arange(512)[None, :]
    mg = np.stack([(qf - kr >= o * 128) for o in range(4)], 0).astype(np.float32)
    c["maskg"] = mg.transpose(1, 0, 2).reshape(128, 4 * 512)
    ml = []
    for o in range(-3, 3):
        diff = qf - kr + o * 128
        ml.append(((diff >= 0) & (diff < 256)).astype(np.float32))
    c["maskl"] = np.stack(ml, 0).transpose(1, 0, 2).reshape(128, 6 * 512)
    fc = np.zeros((128, 16), np.float32)
    for i, v in enumerate(FCONST_VALS):
        fc[:, i] = v
    c["fconst32"] = fc
    idx = np.arange(128)
    c["m_lows32"] = (idx[:, None] > idx[None, :]).astype(np.float32)
    c["m_upi32"] = (idx[:, None] <= idx[None, :]).astype(np.float32)
    c["m_ups32"] = (idx[:, None] < idx[None, :]).astype(np.float32)
    return c

FCONST_VALS = [-8.0, 1e-6, 1.0, 1e-5, 0.5, -0.5, 2.0, -1.0, -2.4260151319598084]

CONST_SPECS = {
    "ident16": BF16, "ident32": F32, "ones16": BF16, "ones32": F32,
    "cum_ut32": F32, "fconst32": F32,
    "maskg": BF16, "maskl": BF16,
    "m_lows32": F32, "m_upi32": F32, "m_ups32": F32,
}
GLOBAL_CONSTS = ("ident16", "ident32", "ones16", "ones32", "cum_ut32", "fconst32", "m_lows32", "m_upi32", "m_ups32")
HBM_CONSTS = ["rope_c_g", "rope_s_g", "rope_c_l", "rope_s_l"]

HOSTC = host_constants()

class Ctr:
    __slots__ = ("name", "sem", "n", "step")
    def __init__(self, name, sem, step=1):
        self.name, self.sem, self.n, self.step = name, sem, 0, step

class Arena:
    def __init__(self, t, n):
        self.t, self.n, self.off, self.marks = t, n, 0, []
    def alloc(self, cols):
        cols = (cols + 15) // 16 * 16
        assert self.off + cols <= self.n, f"arena overflow {self.off}+{cols}>{self.n}"
        ap = self.t[:, self.off:self.off + cols]
        self.off += cols
        return ap
    def alloc32(self, cols):
        return self.alloc(2 * cols).bitcast(F32)
    def push(self): self.marks.append(self.off)
    def pop(self): self.off = self.marks.pop()

class P:
    def __init__(self, nc, cur, engines, sems):
        self.nc, self.cur, self.E, self.sems = nc, cur, engines, sems

    def wait(self, who, ctr, v=None):
        v = ctr.n if v is None else v
        if v > 0 and self.cur == who:
            self.E[who].wait_ge(ctr.sem, v)

    def _fin(self, who, instr, ctr):
        if ctr is not None:
            ctr.n += ctr.step
            if self.cur == who and instr is not None:
                instr.then_inc(ctr.sem, ctr.step)

    def dma(self, q, out, in_, ctr=None, transpose=False, accum=False, slow=False):
        i = None
        if self.cur == q:
            kw = {"accum_op": ALU.add} if accum else {}
            if slow:
                kw["allow_slow_non_contiguous"] = True
            i = self.E[q].dma_start(out=out, in_=in_, transpose=transpose, **kw)
        self._fin(q, i, ctr)

    def mm(self, out, lhsT, rhs, start, stop, ctr=None):
        i = None
        if self.cur == "tensor":
            i = self.E["tensor"].matmul(out, lhsT, rhs, start=start, stop=stop)
        self._fin("tensor", i, ctr)

    def trans(self, out, in_, ident, ctr=None):
        i = None
        if self.cur == "tensor":
            i = self.E["tensor"].transpose(out, in_, ident)
        self._fin("tensor", i, ctr)

    def act(self, out, in_, func, bias=0.0, scale=1.0, accum_out=None, ctr=None,
            edge=False):
        i = None
        if self.cur == "scalar":
            i = self.E["scalar"].activation(out, in_, func, bias=bias, scale=scale,
                                            accum_out=accum_out)
        c = ctr if ctr is not None else (self.sems["act"] if edge else None)
        self._fin("scalar", i, c)
        if edge:
            self.wait("scalar", self.sems["act"])

    def ve(self, eng, meth, *args, ctr=None, edge=False, **kw):
        i = None
        if self.cur == eng:
            i = getattr(self.E[eng], meth)(*args, **kw)
        nm = "dve" if eng == "vector" else "pool"
        c = ctr if ctr is not None else (self.sems[nm] if edge else None)
        self._fin(eng, i, c)
        if edge:
            self.wait(eng, self.sems[nm])

    def v(self, meth, *args, ctr=None, edge=False, **kw):
        self.ve("vector", meth, *args, ctr=ctr, edge=edge, **kw)

    def g(self, meth, *args, ctr=None, edge=False, **kw):
        self.ve("gpsimd", meth, *args, ctr=ctr, edge=edge, **kw)

    def barrier(self):
        for e in ("sync", "gpsimd", "tensor", "vector", "scalar"):
            if self.cur == e:
                for c in self.sems.values():
                    if c.n > 0:
                        self.E[e].wait_ge(c.sem, c.n)


def ev(ap):
    return ap.rearrange("p (a b) -> p a b", b=2)[:, :, 0:1]

def od(ap):
    return ap.rearrange("p (a b) -> p a b", b=2)[:, :, 1:2]


class MK:
    def __init__(self, nc, phases, dbg=()):
        self.nc, self.phases = nc, phases
        self.dbg = set(dbg)
        self.d = {}
        self.declare_dram()

    def declare_dram(self):
        nc, d = self.nc, self.d
        d["x0"] = nc.dram_tensor("x0", [S, D], F32, kind="ExternalInput")
        WSPECS = [
            ("L0_attn_norm1_w", [D]), ("L0_attn_norm2_w", [D]),
            ("L0_attn_g_wq", [D, D]), ("L0_attn_g_kv_down", [D, 256]),
            ("L0_attn_g_k_up", [256, 256]), ("L0_attn_g_v_up", [256, 256]),
            ("L0_attn_g_wo", [D, D]),
            ("L0_attn_l_wq", [D, D]), ("L0_attn_l_wk", [D, 256]),
            ("L0_attn_l_wv", [D, 256]), ("L0_attn_l_wo", [D, D]),
            ("L0_ffn_router_w", [D, 8]), ("L0_ffn_router_b", [8]),
            ("L0_ffn_W1", [8, D, 4096]), ("L0_ffn_b1", [8, 4096]),
            ("L0_ffn_W2", [8, 4096, D]), ("L0_ffn_b2", [8, D]),
            ("L1_attn_w_ln1", [D]), ("L1_attn_w_ln2", [D]),
            ("L1_attn_W_qkvz", [D, 3072]), ("L1_attn_W_ba", [D, 16]),
            ("L1_attn_conv_w", [2048, 4]), ("L1_attn_dt_bias", [8]),
            ("L1_attn_A_log", [8]), ("L1_attn_gate_norm_w", [128]),
            ("L1_attn_W_out", [D, D]), ("L1_attn_W_router", [D, 8]),
            ("L1_attn_W_gate_e", [8, D, 512]), ("L1_attn_W_up_e", [8, D, 512]),
            ("L1_attn_W_down_e", [8, 512, D]),
            ("L1_ffn_shared_w1", [4096, D]), ("L1_ffn_shared_w2", [D, 4096]),
            ("L1_ffn_experts_w1", [8, 4096, D]), ("L1_ffn_experts_w2", [8, D, 4096]),
            ("L1_ffn_gate_w", [8, D]),
        ]
        for name, shape in WSPECS:
            d[name] = nc.dram_tensor(name, shape, F32, kind="ExternalInput")
        for name, dt in CONST_SPECS.items():
            d[name] = nc.dram_tensor(name, list(HOSTC[name].shape), dt,
                                     kind="ExternalInput")
        for name in HBM_CONSTS:
            d[name] = nc.dram_tensor(name, list(HOSTC[name].shape), BF16,
                                     kind="ExternalInput")
        scr = [
            ("xfmdbg", [128, 4096], BF16),
            ("routedbg", [128, 64], F32),
            ("c32dbg", [128, NT * 8], F32), ("hedbg", [128, 32 * 512], BF16),
            ("xn32", [S, D], F32), ("xn16", [S, D], BF16), ("a016", [S, D], BF16),
            ("x132", [S, D], F32), ("a116", [S, D], BF16), ("s32", [S, D], F32),
            ("ffn0", [S, D], F32), ("X132", [S, D], F32),
            ("xn1_32", [S, D], F32), ("xn1_16", [S, D], BF16), ("h016", [S, D], BF16),
            ("x232", [S, D], F32), ("xh16", [S, D], BF16),
            ("moe1", [S, D], F32), ("ffn1", [S, D], F32), ("s132", [S, D], F32),
            ("sw1_16", [4096, D], BF16), ("sw2_16", [D, 4096], BF16),
            ("ew1_16", [8 * 4096, D], BF16), ("ew2_16", [8 * D, 4096], BF16),
            ("gw_16", [8, D], BF16),
        ]
        for name, shape, dt in scr:
            kind = "ExternalOutput" if name in self.dbg else "Internal"
            d[name] = nc.dram_tensor(name, shape, dt, kind=kind)
        d["out"] = nc.dram_tensor("out", [S, D], F32, kind="ExternalOutput")

    def build(self):
        nc = self.nc
        with (
            nc.sbuf_tensor("A16", [128, 99328], BF16) as A16,
            nc.sbuf_tensor("CB", [128, 2048], BF16) as CB,
            nc.psum_tensor("PS", [128, 3072], F32) as PS,
            nc.psum_tensor("PT", [128, 2048], BF16) as PT,
            nc.semaphore("c_pe") as s_pe,
            nc.semaphore("c_act") as s_act,
            nc.semaphore("c_dve") as s_dve,
            nc.semaphore("c_pool") as s_pool,
            nc.semaphore("q_sync") as s_qs,
            nc.semaphore("q_scal") as s_qc,
            nc.semaphore("q_gps") as s_qg,
            nc.Block() as block,
        ):
            self.A16, self.CB, self.PS, self.PT = A16, CB, PS, PT
            mk = self

            def run(cur, eng):
                sems = {
                    "pe": Ctr("pe", s_pe), "act": Ctr("act", s_act),
                    "dve": Ctr("dve", s_dve), "pool": Ctr("pool", s_pool),
                    "qs": Ctr("qs", s_qs, 16), "qc": Ctr("qc", s_qc, 16),
                    "qg": Ctr("qg", s_qg, 16),
                }
                emit_all(mk, P(self.nc, cur, {cur: eng}, sems))

            @block.sync
            def _(eng): run("sync", eng)
            @block.gpsimd
            def _(eng): run("gpsimd", eng)
            @block.tensor
            def _(eng): run("tensor", eng)
            @block.vector
            def _(eng): run("vector", eng)
            @block.scalar
            def _(eng): run("scalar", eng)
        return nc



def emit_rsqrt(p, dst, srcv, pre_scale=1.0, pre_bias=0.0, post_scale=1.0):
    import math
    act = p.sems["act"]
    p.act(dst, srcv, AF.Ln, scale=pre_scale, bias=pre_bias, edge=True)
    p.act(dst, dst, AF.Exp, scale=-0.5,
          bias=(0.0 if post_scale == 1.0 else math.log(post_scale)), ctr=act)

def entry_guard(p, queues=("sync", "gpsimd")):
    for q in queues:
        for cn in ("pe", "act", "dve", "pool", "qs", "qc", "qg"):
            p.wait(q, p.sems[cn])


def load_const(mk, p, ar, name):
    dt = CONST_SPECS[name]
    w = HOSTC[name].shape[1]
    buf = ar.alloc32(w) if dt == F32 else ar.alloc(w)
    p.dma("sync", buf, mk.d[name][:, :], p.sems["qs"])
    return buf

def load_row_bcast(mk, p, ar, src_ap, w):
    buf = ar.alloc32(w)
    p.dma("sync", buf, src_ap[None, :].to_broadcast([128, w]), p.sems["qs"])
    return buf

def emit_all(mk, p):
    a16 = Arena(mk.A16, mk.A16.shape[1])
    cst = Arena(mk.CB, mk.CB.shape[1])
    qs = p.sems["qs"]
    C = {name: load_const(mk, p, cst, name) for name in GLOBAL_CONSTS}
    C["consts_at"] = qs.n
    mk.C = C
    if not getattr(mk, "fconst_registered", False):
        for i, v in enumerate(FCONST_VALS):
            mk.nc.const_aps.register(F32, v, C["fconst32"][:, i:i + 1])
        mk.fconst_registered = True
    for e in ("tensor", "vector", "scalar", "gpsimd"):
        p.wait(e, qs, C["consts_at"])
    if "A" in mk.phases:
        phase_A(mk, p, a16)
    if "B" in mk.phases:
        phase_attn(mk, p, a16, "g")
    if "C" in mk.phases:
        phase_attn(mk, p, a16, "l")
    if "H" in mk.phases:
        phase_prepass(mk, p, a16)
    if "D" in mk.phases:
        phase_moe(mk, p, a16, "L0")
        phase_combine2(mk, p, a16, "X132", "x0", "s32", "ffn0")
    if "E" in mk.phases:
        phase_E(mk, p, a16)
    if "F" in mk.phases:
        phase_F(mk, p, a16)
    if "G" in mk.phases:
        phase_moe2(mk, p, a16, "G")
        phase_add(mk, p, a16, "s132", "x232", "moe1")
    if "H" in mk.phases:
        phase_moe2(mk, p, a16, "H")
        phase_combine2(mk, p, a16, "out", "X132", "s132", "ffn1")
    p.barrier()


def phase_A(mk, p, a16):
    d, C = mk.d, mk.C
    qs, dve, act = p.sems["qs"], p.sems["dve"], p.sems["act"]
    a16.push()
    n1bc = load_row_bcast(mk, p, a16, d["L0_attn_norm1_w"], D)
    xin = [a16.alloc32(D) for _ in range(2)]
    st8 = [a16.alloc32(12) for _ in range(2)]
    mv = [a16.alloc32(4) for _ in range(2)]
    xn32t = [a16.alloc32(D) for _ in range(2)]
    xn16t = [a16.alloc(D) for _ in range(2)]
    a016t = [a16.alloc(D) for _ in range(2)]

    x0r = d["x0"].rearrange("(n p) d -> n p d", p=128)
    xn32r = d["xn32"].rearrange("(n p) d -> n p d", p=128)
    xn16r = d["xn16"].rearrange("(n p) d -> n p d", p=128)
    a016r = d["a016"].rearrange("(n p) d -> n p d", p=128)

    free_at = {}
    p.wait("sync", qs, qs.n)
    for i in range(NT):
        bi = i % 2
        if i >= 2:
            p.wait("sync", dve, free_at[i - 2][0])
            p.wait("sync", qs, free_at[i - 2][1])
        p.dma("sync", xin[bi], x0r[i], qs)
        ld = qs.n

        p.wait("vector", qs, ld)
        p.v("bn_stats", st8[bi][:, 0:6], xin[bi][:, 0:512], edge=True)
        p.v("bn_stats", st8[bi][:, 6:12], xin[bi][:, 512:1024], edge=True)
        p.v("bn_aggr", mv[bi][:, 0:2], st8[bi][:, 0:12], edge=True)
        p.wait("scalar", dve, dve.n)
        emit_rsqrt(p, mv[bi][:, 2:3], mv[bi][:, 1:2], pre_bias=1e-5)
        rstd_done = act.n
        p.v("tensor_scalar_add", mv[bi][:, 3:4], mv[bi][:, 1:2], 1e-5, edge=True)
        p.v("reciprocal", mv[bi][:, 3:4], mv[bi][:, 3:4], edge=True)
        p.v("tensor_mul", mv[bi][:, 3:4], mv[bi][:, 3:4], mv[bi][:, 1:2], edge=True)
        p.v("tensor_scalar_add", mv[bi][:, 3:4], mv[bi][:, 3:4], 1e-6, ctr=dve)
        p.wait("scalar", dve, dve.n)
        emit_rsqrt(p, mv[bi][:, 3:4], mv[bi][:, 3:4])
        rms2_done = act.n

        p.wait("vector", act, rstd_done)
        p.v("tensor_scalar", xn32t[bi], xin[bi], mv[bi][:, 0:1], mv[bi][:, 2:3],
            ALU.subtract, ALU.mult, edge=True)
        p.v("tensor_copy", xn16t[bi], xn32t[bi], ctr=dve)
        xn_done = dve.n
        p.wait("vector", act, rms2_done)
        p.v("scalar_tensor_tensor", a016t[bi], xn32t[bi], mv[bi][:, 3:4], n1bc,
            ALU.mult, ALU.mult, ctr=dve)
        a0_done = dve.n

        p.wait("sync", dve, xn_done)
        p.dma("sync", xn32r[i], xn32t[bi], qs)
        p.dma("sync", xn16r[i], xn16t[bi], qs)
        p.wait("sync", dve, a0_done)
        p.dma("sync", a016r[i], a016t[bi], qs)
        free_at[i] = (dve.n, qs.n)
    a16.pop()


def phase_attn(mk, p, a16, kind):
    d, C = mk.d, mk.C
    qs, qg = p.sems["qs"], p.sems["qg"]
    pe, act, dve = p.sems["pe"], p.sems["act"], p.sems["dve"]
    PS, PT = mk.PS, mk.PT
    a16.push()
    entry_guard(p)

    glob = kind == "g"
    src16 = d["a016"] if glob else d["a116"]
    wq = d["L0_attn_g_wq"] if glob else d["L0_attn_l_wq"]
    wo = d["L0_attn_g_wo"] if glob else d["L0_attn_l_wo"]

    wq_sb = a16.alloc(8 * D)
    for t in range(8):
        p.dma("gpsimd", wq_sb[:, t * D:(t + 1) * D], wq[t * 128:(t + 1) * 128, :], qg)
    if glob:
        kvd_sb = a16.alloc(8 * 256)
        ku_sb = a16.alloc(2 * 256)
        vu_sb = a16.alloc(2 * 256)
        for t in range(8):
            p.dma("gpsimd", kvd_sb[:, t * 256:(t + 1) * 256],
                  d["L0_attn_g_kv_down"][t * 128:(t + 1) * 128, :], qg)
        for t in range(2):
            p.dma("gpsimd", ku_sb[:, t * 256:(t + 1) * 256],
                  d["L0_attn_g_k_up"][t * 128:(t + 1) * 128, :], qg)
            p.dma("gpsimd", vu_sb[:, t * 256:(t + 1) * 256],
                  d["L0_attn_g_v_up"][t * 128:(t + 1) * 128, :], qg)
    else:
        wk_sb = a16.alloc(8 * 256)
        wv_sb = a16.alloc(8 * 256)
        for t in range(8):
            p.dma("gpsimd", wk_sb[:, t * 256:(t + 1) * 256],
                  d["L0_attn_l_wk"][t * 128:(t + 1) * 128, :], qg)
            p.dma("gpsimd", wv_sb[:, t * 256:(t + 1) * 256],
                  d["L0_attn_l_wv"][t * 128:(t + 1) * 128, :], qg)
    w_at = qg.n

    mtag = "maskg" if glob else "maskl"
    msk = a16.alloc(HOSTC[mtag].shape[1])
    p.dma("sync", msk, d[mtag][:, :], qs)
    rc = a16.alloc(NT * 64)
    rs = a16.alloc(NT * 64)
    p.dma("sync", rc.rearrange("p (n f) -> p n f", f=64),
          d["rope_c_" + kind].rearrange("(n p) f -> p n f", p=128), qs)
    p.dma("sync", rs.rearrange("p (n f) -> p n f", f=64),
          d["rope_s_" + kind].rearrange("(n p) f -> p n f", p=128), qs)
    const_at = qs.n

    afm = [a16.alloc(4096) for _ in range(2)]
    attn = a16.alloc(8 * S)
    a16.push()
    if glob:
        lat = a16.alloc(2 * S)
    qraw = a16.alloc(2 * S)
    qfmn = a16.alloc(2 * S)
    kvraw = a16.alloc(S)
    kfmn = a16.alloc(S)
    vaug = a16.alloc(NT * 66 + 16)
    tbuf = a16.alloc(128)
    rtmp = a16.alloc(512)
    rtmp4 = a16.alloc(512)
    rtmpq = a16.alloc(512)
    kvb = a16.alloc(512)
    q0b = a16.alloc(512)
    q1b = a16.alloc(512)
    nrm16 = a16.alloc32(20)
    nrm16b = a16.alloc32(20)
    nrm = a16.alloc32(8)
    prow = a16.alloc32(512)
    psb2 = a16.alloc(1024)
    usb = a16.alloc(512)

    bP0, bP1 = PS[:, 0:512], PS[:, 512:1024]
    bS0, bS1 = PS[:, 1024:1536], PS[:, 1536:2048]
    bU = PS[:, 2048:2560]
    bB = PS[:, 2560:3072]
    tp = [PT[:, 0:1024], PT[:, 1024:2048]]
    vaug_t = vaug[:, 0:NT * 66].rearrange("p (n w) -> p n w", w=66)

    def load_afm(c, buf):
        for t in range(8):
            p.dma("sync", buf[:, t * 512:(t + 1) * 512],
                  src16[c * 512:(c + 1) * 512, t * 128:(t + 1) * 128],
                  qs, transpose=True)
        return qs.n

    p.wait("sync", qs, const_at)

    for g in range(4):
        evac = {"P0": 0, "P1": 0}
        afm_read = [0, 0]
        for c in range(NC5):
            buf = afm[c % 2]
            p.wait("sync", pe, afm_read[c % 2])
            ld = load_afm(c, buf)
            p.wait("tensor", qs, ld)
            if c == 0 and g == 0:
                p.wait("tensor", qg, w_at)
            for ct in range(2):
                col0 = g * 256 + ct * 128
                bnk, key = (bP0, "P0") if ct == 0 else (bP1, "P1")
                p.wait("tensor", dve, evac[key])
                for t in range(8):
                    p.mm(bnk, wq_sb[:, t * D + col0: t * D + col0 + 128],
                         buf[:, t * 512:(t + 1) * 512], start=(t == 0), stop=(t == 7),
                         ctr=pe if t == 7 else None)
                p.wait("vector", pe, pe.n)
                p.v("tensor_copy", qraw[:, ct * S + c * 512: ct * S + c * 512 + 512],
                    bnk, ctr=dve)
                evac[key] = dve.n
            if glob and g == 0:
                for ct in range(2):
                    bnk, key = (bP0, "P0") if ct == 0 else (bP1, "P1")
                    p.wait("tensor", dve, evac[key])
                    for t in range(8):
                        p.mm(bnk, kvd_sb[:, t * 256 + ct * 128: t * 256 + ct * 128 + 128],
                             buf[:, t * 512:(t + 1) * 512], start=(t == 0), stop=(t == 7),
                             ctr=pe if t == 7 else None)
                    p.wait("vector", pe, pe.n)
                    p.v("tensor_copy", lat[:, ct * S + c * 512: ct * S + c * 512 + 512],
                        bnk, ctr=dve)
                    evac[key] = dve.n
            afm_read[c % 2] = pe.n

        for c in range(NC5):
            p.wait("tensor", dve, max(evac["P0"], evac["P1"]))
            if glob:
                for t in range(2):
                    p.mm(bP0[0:64, :], ku_sb[:, t * 256 + g * 64: t * 256 + g * 64 + 64],
                         lat[:, t * S + c * 512: t * S + c * 512 + 512],
                         start=(t == 0), stop=(t == 1))
                for t in range(2):
                    p.mm(bP0[64:128, :], vu_sb[:, t * 256 + g * 64: t * 256 + g * 64 + 64],
                         lat[:, t * S + c * 512: t * S + c * 512 + 512],
                         start=(t == 0), stop=(t == 1), ctr=pe if t == 1 else None)
            else:
                buf = afm[c % 2]
                p.wait("sync", pe, pe.n)
                ld = load_afm(c, buf)
                p.wait("tensor", qs, ld)
                for t in range(8):
                    p.mm(bP0[0:64, :], wk_sb[:, t * 256 + g * 64: t * 256 + g * 64 + 64],
                         buf[:, t * 512:(t + 1) * 512], start=(t == 0), stop=(t == 7))
                for t in range(8):
                    p.mm(bP0[64:128, :], wv_sb[:, t * 256 + g * 64: t * 256 + g * 64 + 64],
                         buf[:, t * 512:(t + 1) * 512], start=(t == 0), stop=(t == 7),
                         ctr=pe if t == 7 else None)
            p.wait("vector", pe, pe.n)
            p.v("tensor_copy", kvraw[:, c * 512:(c + 1) * 512], bP0, ctr=dve)
            evac["P0"] = dve.n

        p.v("memset", vaug_t[:, :, 64:65], 1.0, ctr=dve)
        p.wait("tensor", dve, dve.n)

        for scn in range(NT // 4):
            i0 = scn * 4
            p.wait("tensor", dve, dve.n)
            for j in range(4):
                p.trans(tp[0][:, j * 128:(j + 1) * 128],
                        kvraw[:, (i0 + j) * 128:(i0 + j + 1) * 128], C["ident16"])
            for j in range(4):
                p.trans(tp[0][:, 512 + j * 128:512 + (j + 1) * 128],
                        qraw[:, (i0 + j) * 128:(i0 + j + 1) * 128], C["ident16"])
            for j in range(4):
                p.trans(tp[1][:, j * 128:(j + 1) * 128],
                        qraw[:, S + (i0 + j) * 128: S + (i0 + j + 1) * 128],
                        C["ident16"], ctr=pe if j == 3 else None)
            p.wait("vector", pe, pe.n)
            p.v("tensor_copy", kvb, tp[0][:, 0:512], edge=True)
            p.v("tensor_copy", q0b, tp[0][:, 512:1024], edge=True)
            p.v("tensor_copy", q1b, tp[1][:, 0:512], edge=True)
            p.v("tensor_copy",
                vaug[:, i0 * 66:(i0 + 4) * 66].rearrange("p (n w) -> p n w", w=66)[:, :, 0:64],
                kvb.rearrange("p (n w) -> p n w", w=128)[:, :, 64:128], ctr=dve)
            p.wait("vector", dve, dve.n)
            rcs = rc[:, i0 * 64:(i0 + 4) * 64].rearrange("p (n f) -> p n f", f=64)
            rss = rs[:, i0 * 64:(i0 + 4) * 64].rearrange("p (n f) -> p n f", f=64)
            kview = kvb.rearrange("p (n w) -> p n w", w=128)[:, :, 0:64]
            kev = kview.rearrange("p n (a b) -> p n a b", b=2)
            rt4 = rtmp4[:, 0:256].rearrange("p (n f) -> p n f", f=64)
            p.v("tensor_copy", rt4.rearrange("p n (a b) -> p n a b", b=2)[:, :, :, 0:1],
                kev[:, :, :, 1:2], edge=True)
            p.v("tensor_copy", rt4.rearrange("p n (a b) -> p n a b", b=2)[:, :, :, 1:2],
                kev[:, :, :, 0:1], edge=True)
            p.v("tensor_tensor", rt4, rt4, rss, ALU.mult, edge=True)
            p.v("tensor_tensor", kview, kview, rcs, ALU.mult, edge=True)
            p.v("tensor_tensor", kview, kview, rt4, ALU.add, edge=True)
            rc2 = rcs.rearrange("p n (o f) -> p n o f", o=1).to_broadcast([128, 4, 2, 64])
            rs2 = rss.rearrange("p n (o f) -> p n o f", o=1).to_broadcast([128, 4, 2, 64])
            for qb in (q0b, q1b):
                qm = qb.rearrange("p (m f) -> p m f", f=64)
                qv = qb.rearrange("p (n h f) -> p n h f", h=2, f=64)
                rqm = rtmpq.rearrange("p (m f) -> p m f", f=64)
                rqv = rtmpq.rearrange("p (n h f) -> p n h f", h=2, f=64)
                p.v("tensor_copy", rqm.rearrange("p m (a b) -> p m a b", b=2)[:, :, :, 0:1],
                    qm.rearrange("p m (a b) -> p m a b", b=2)[:, :, :, 1:2],
                    edge=True)
                p.v("tensor_copy", rqm.rearrange("p m (a b) -> p m a b", b=2)[:, :, :, 1:2],
                    qm.rearrange("p m (a b) -> p m a b", b=2)[:, :, :, 0:1],
                    edge=True)
                p.v("tensor_tensor", rqv, rqv, rs2, ALU.mult, edge=True)
                p.v("tensor_tensor", qv, qv, rc2, ALU.mult, edge=True)
                p.v("tensor_tensor", qb, qb, rtmpq, ALU.add, edge=True)
            p.wait("scalar", dve, dve.n)
            for j in range(4):
                p.act(rtmp[:, 0:64], kvb[:, j * 128:j * 128 + 64], AF.Square,
                      accum_out=nrm16[:, j:j + 1],
                      ctr=act if j == 3 else None)
            for j in range(4):
                for hh in range(2):
                    p.act(rtmp[:, 0:64], q0b[:, j * 128 + hh * 64:j * 128 + (hh + 1) * 64],
                          AF.Square, accum_out=nrm16[:, 4 + j * 2 + hh:5 + j * 2 + hh],
                          ctr=act if (j == 3 and hh == 1) else None)
            for j in range(4):
                for hh in range(2):
                    p.act(rtmp[:, 0:64], q1b[:, j * 128 + hh * 64:j * 128 + (hh + 1) * 64],
                          AF.Square, accum_out=nrm16[:, 12 + j * 2 + hh:13 + j * 2 + hh],
                          ctr=act)
            emit_rsqrt(p, nrm16b[:, 0:20], nrm16[:, 0:20], pre_scale=1.0 / 64, pre_bias=1e-6)
            p.wait("vector", act, act.n)
            for j in range(4):
                p.v("tensor_scalar_mul", kvb[:, j * 128:j * 128 + 64],
                    kvb[:, j * 128:j * 128 + 64], nrm16b[:, j:j + 1])
            for j in range(4):
                for hh in range(2):
                    p.v("tensor_scalar_mul", q0b[:, j * 128 + hh * 64:j * 128 + (hh + 1) * 64],
                        q0b[:, j * 128 + hh * 64:j * 128 + (hh + 1) * 64],
                        nrm16b[:, 4 + j * 2 + hh:5 + j * 2 + hh])
                    p.v("tensor_scalar_mul", q1b[:, j * 128 + hh * 64:j * 128 + (hh + 1) * 64],
                        q1b[:, j * 128 + hh * 64:j * 128 + (hh + 1) * 64],
                        nrm16b[:, 12 + j * 2 + hh:13 + j * 2 + hh], ctr=dve)
            p.wait("tensor", dve, dve.n)
            for j in range(4):
                p.trans(tp[0][0:64, j * 128:(j + 1) * 128],
                        kvb[:, j * 128:j * 128 + 64], C["ident16"])
                p.trans(tp[0][64:128, 512 + j * 128:512 + (j + 1) * 128],
                        kvb[:, j * 128:j * 128 + 64], C["ident16"])
                p.trans(tp[1][:, j * 128:(j + 1) * 128],
                        q0b[:, j * 128:(j + 1) * 128], C["ident16"])
                p.trans(tp[1][:, 512 + j * 128:512 + (j + 1) * 128],
                        q1b[:, j * 128:(j + 1) * 128], C["ident16"],
                        ctr=pe if j == 3 else None)
            p.wait("vector", pe, pe.n)
            p.v("tensor_copy", kfmn[0:64, i0 * 128:(i0 + 4) * 128], tp[0][0:64, 0:512],
                edge=True)
            p.v("tensor_copy", kfmn[64:128, i0 * 128:(i0 + 4) * 128],
                tp[0][64:128, 512:1024], edge=True)
            p.v("tensor_copy", qfmn[:, i0 * 128:(i0 + 4) * 128], tp[1][:, 0:512],
                edge=True)
            p.v("tensor_copy", qfmn[:, S + i0 * 128: S + (i0 + 4) * 128],
                tp[1][:, 512:1024], ctr=dve)
            p.wait("tensor", dve, dve.n)
        prep_done = dve.n

        p.wait("tensor", dve, prep_done)
        for hl in range(4):
            h = 4 * g + hl
            pt, base = hl // 2, (hl % 2) * 64
            for qc in range(NC5):
                if glob:
                    jlist = list(range(0, 4 * qc + 4))
                else:
                    jlist = [j for j in range(4 * qc - 2, 4 * qc + 4) if j >= 0]
                pairs = [jlist[i:i + 2] for i in range(0, len(jlist), 2)]
                exp_at = {}
                umm_at = {}
                for pi, pj in enumerate(pairs):
                    if pi >= 1:
                        p.wait("tensor", act, exp_at[pi - 1])
                    for jj, j in enumerate(pj):
                        p.mm((bS0, bS1)[jj],
                             kfmn[base:base + 64, j * 128:(j + 1) * 128],
                             qfmn[base:base + 64, pt * S + qc * 512: pt * S + qc * 512 + 512],
                             start=True, stop=True,
                             ctr=pe if jj == len(pj) - 1 else None)
                    p.wait("scalar", pe, pe.n)
                    if pi >= 1:
                        p.wait("scalar", pe, umm_at[pi - 1])
                    wid = 512 * len(pj)
                    p.act(psb2[:, 0:wid], PS[:, 1024:1024 + wid], AF.Exp,
                          scale=0.125, bias=-8.0, ctr=act)
                    exp_at[pi] = act.n
                    anymask = False
                    for jj, j in enumerate(pj):
                        masked = (j - 4 * qc >= 0) if glob else True
                        if masked:
                            mo = (j - 4 * qc) if glob else (4 * qc - j + 3)
                            p.wait("vector", act, act.n)
                            p.v("tensor_mul", psb2[:, jj * 512:(jj + 1) * 512],
                                psb2[:, jj * 512:(jj + 1) * 512],
                                msk[:, mo * 512:(mo + 1) * 512], ctr=dve)
                            anymask = True
                    if anymask:
                        p.wait("tensor", dve, dve.n)
                    else:
                        p.wait("tensor", act, act.n)
                    idx0 = pi * 2
                    for jj, j in enumerate(pj):
                        p.mm(bU[0:65, :], vaug[:, j * 66: j * 66 + 65],
                             psb2[:, jj * 512:(jj + 1) * 512],
                             start=(idx0 + jj == 0),
                             stop=(idx0 + jj == len(jlist) - 1),
                             ctr=pe if jj == len(pj) - 1 else None)
                    umm_at[pi] = pe.n
                p.wait("vector", pe, pe.n)
                p.v("reciprocal", prow[64:65, :], bU[64:65, :], ctr=dve)
                p.wait("tensor", dve, dve.n)
                p.mm(bB[0:64, :], C["ones32"][64:65, 0:64], prow[64:65, :],
                     start=True, stop=True, ctr=pe)
                p.wait("scalar", pe, umm_at[len(pairs) - 1])
                p.act(usb[0:64, :], bU[0:64, :], AF.Identity, ctr=act)
                p.wait("vector", act, act.n)
                p.wait("vector", pe, pe.n)
                th, bh = h // 2, (h % 2) * 64
                if bh == 0:
                    p.v("tensor_mul",
                        attn[0:64, th * S + qc * 512: th * S + qc * 512 + 512],
                        usb[0:64, :], bB[0:64, :], ctr=dve)
                    p.wait("tensor", dve, dve.n)
                else:
                    p.v("tensor_mul", usb[0:64, :], usb[0:64, :], bB[0:64, :],
                        ctr=dve)
                    p.wait("tensor", dve, dve.n)
                    p.mm(bB[64:128, :], C["ident16"][0:64, 0:64], usb[0:64, :],
                         start=True, stop=True, ctr=pe)
                    p.wait("vector", pe, pe.n)
                    p.v("tensor_copy",
                        attn[64:128, th * S + qc * 512: th * S + qc * 512 + 512],
                        bB[64:128, :], ctr=dve)
                    p.wait("tensor", dve, dve.n)

    a16.pop()
    entry_guard(p)
    wo_sb = a16.alloc(8 * D)
    for t in range(8):
        p.dma("gpsimd", wo_sb[:, t * D:(t + 1) * D], wo[t * 128:(t + 1) * 128, :], qg)
    wo_at = qg.n
    if glob:
        n2bc = load_row_bcast(mk, p, a16, d["L0_attn_norm2_w"], D)
    res32 = d["xn32"] if glob else d["x132"]
    out32 = d["x132"] if glob else d["s32"]
    res_r = res32.rearrange("(n p) d -> n p d", p=128)
    out_r = out32.rearrange("(n p) d -> n p d", p=128)
    a116r = d["a116"].rearrange("(n p) d -> n p d", p=128)

    xres = [a16.alloc32(D) for _ in range(2)]
    xout = [a16.alloc32(D) for _ in range(2)]
    a1t = [a16.alloc(D) for _ in range(2)]
    sq = a16.alloc32(D)
    mv2 = a16.alloc32(4)

    p.wait("tensor", qg, wo_at)
    free2 = {}
    for i in range(NT):
        bi = i % 2
        if i >= 2:
            p.wait("sync", dve, free2[i - 2][0])
            p.wait("sync", qs, free2[i - 2][1])
        p.dma("sync", xres[bi], res_r[i], qs)
        ld = qs.n
        for half in range(2):
            bnk = bP0 if half == 0 else bP1
            if i >= 1:
                p.wait("tensor", dve, free2.get((i - 1, half), 0))
            for t in range(8):
                p.mm(bnk, attn[:, t * S + i * 128:(t * S + (i + 1) * 128)],
                     wo_sb[:, t * D + half * 512: t * D + half * 512 + 512],
                     start=(t == 0), stop=(t == 7), ctr=pe if t == 7 else None)
            p.wait("vector", pe, pe.n)
            p.wait("vector", qs, ld)
            p.v("tensor_add", xout[bi][:, half * 512:(half + 1) * 512],
                xres[bi][:, half * 512:(half + 1) * 512], bnk, ctr=dve)
            free2[(i, half)] = dve.n
        xdone = dve.n
        p.wait("sync", dve, xdone)
        p.dma("sync", out_r[i], xout[bi], qs)
        st_at = qs.n
        if glob:
            p.wait("scalar", dve, xdone)
            p.act(sq, xout[bi], AF.Square, accum_out=mv2[:, 0:1], edge=True)
            emit_rsqrt(p, mv2[:, 1:2], mv2[:, 0:1], pre_scale=1.0 / D,
                       pre_bias=1e-6)
            p.wait("vector", act, act.n)
            p.v("scalar_tensor_tensor", a1t[bi], xout[bi], mv2[:, 1:2], n2bc,
                ALU.mult, ALU.mult, ctr=dve)
            p.wait("sync", dve, dve.n)
            p.dma("sync", a116r[i], a1t[bi], qs)
        free2[i] = (dve.n, qs.n)
    a16.pop()


def make_inputs_per_core(inputs, core_b):
    import ml_dtypes
    m = {"x0": np.ascontiguousarray(np.asarray(inputs["L0_attn_hidden"])[core_b],
                                    dtype=np.float32)}
    for k, v in inputs.items():
        if k != "L0_attn_hidden":
            m[k] = np.ascontiguousarray(np.asarray(v, dtype=np.float32))
    for name, dt in CONST_SPECS.items():
        m[name] = (HOSTC[name].astype(ml_dtypes.bfloat16) if dt == BF16
                   else HOSTC[name])
    for name in HBM_CONSTS:
        m[name] = HOSTC[name].astype(ml_dtypes.bfloat16)
    return m


def run_phases(inputs, phases, dbg=(), trace=False):
    nc = bass.Bass("TRN2", target_bir_lowering=False)
    mk = MK(nc, phases, dbg=dbg)
    nc2 = mk.build()
    in_maps = [make_inputs_per_core(inputs, b) for b in range(B)]
    res = run_bass_kernel_spmd(nc2, in_maps, core_ids=list(range(B)), trace=trace)
    return res


def phase_moe(mk, p, a16, which):
    d, C = mk.d, mk.C
    qs, qg = p.sems["qs"], p.sems["qg"]
    pe, act, dve = p.sems["pe"], p.sems["act"], p.sems["dve"]
    PS = mk.PS
    a16.push()
    entry_guard(p)

    src16 = d["xn16"]
    Wr, br = d["L0_ffn_router_w"], d["L0_ffn_router_b"]
    W1, W2 = d["L0_ffn_W1"], d["L0_ffn_W2"]
    b1, b2 = d["L0_ffn_b1"], d["L0_ffn_b2"]
    acc_h = d["ffn0"]

    c32 = a16.alloc32(NT * 8)
    xfm = [a16.alloc(4096) for _ in range(2)]
    a16.push()
    rw = a16.alloc(8 * 8)
    for t in range(8):
        p.dma("gpsimd", rw[:, t * 8:(t + 1) * 8], Wr[t * 128:(t + 1) * 128, :], qg)
    rb = a16.alloc(16)
    p.dma("gpsimd", rb[0:1, 0:8], br[None, :], qg)
    rw_at = qg.n
    m8 = a16.alloc32(8)
    lgt = a16.alloc32(8)
    wv = a16.alloc32(4)
    ctmp = a16.alloc32(8)
    lgts = a16.alloc32(32)
    m8s = a16.alloc32(32)
    bR = PS[:, 0:512]

    def load_xfm(c, buf):
        for t in range(8):
            p.dma("sync", buf[:, t * 512:(t + 1) * 512],
                  src16[c * 512:(c + 1) * 512, t * 128:(t + 1) * 128],
                  qs, transpose=True)
        return qs.n

    xfm_free = [0, 0]
    for c in range(NC5):
        buf = xfm[c % 2]
        p.wait("sync", pe, xfm_free[c % 2])
        ld = load_xfm(c, buf)
        if "xfmdbg" in mk.dbg and c == 0:
            p.dma("sync", d["xfmdbg"][:, :], buf, qs)
        p.wait("tensor", qs, ld)
        if c == 0:
            p.wait("tensor", qg, rw_at)
        for ts in range(4):
            it = c * 4 + ts
            p.wait("tensor", dve, dve.n)
            for t in range(8):
                p.mm(bR[:, 0:8], buf[:, t * 512 + ts * 128: t * 512 + (ts + 1) * 128],
                     rw[:, t * 8:(t + 1) * 8], start=(t == 0), stop=False)
            p.mm(bR[:, 0:8], C["ones16"][0:1, 0:128], rb[0:1, 0:8],
                 start=False, stop=True, ctr=pe)
            p.wait("vector", pe, pe.n)
            p.v("tensor_copy", lgt, bR[:, 0:8], edge=True)
            p.v("max", m8, lgt, edge=True)
            p.v("tensor_sub", wv[:, 0:1], m8[:, 0:1], m8[:, 1:2], ctr=dve)
            p.wait("scalar", dve, dve.n)
            p.act(wv[:, 1:2], wv[:, 0:1], AF.Sigmoid, ctr=act)
            p.wait("vector", act, act.n)
            p.v("tensor_scalar", wv[:, 2:3], wv[:, 1:2], -1.0, 1.0, ALU.mult,
                ALU.add, edge=True)
            p.v("tensor_scalar", ctmp, lgt, m8[:, 0:1], wv[:, 1:2],
                ALU.is_equal, ALU.mult, edge=True)
            p.v("tensor_scalar", c32[:, it * 8:(it + 1) * 8], lgt, m8[:, 1:2],
                wv[:, 2:3], ALU.is_equal, ALU.mult, edge=True)
            p.v("tensor_add", c32[:, it * 8:(it + 1) * 8],
                c32[:, it * 8:(it + 1) * 8], ctmp, ctr=dve)
            if "routedbg" in mk.dbg and c == 0:
                p.v("tensor_copy", lgts[:, ts * 8:(ts + 1) * 8], lgt, edge=True)
                p.v("tensor_copy", m8s[:, ts * 8:(ts + 1) * 8], m8, ctr=dve)
                if ts == 3:
                    p.wait("sync", dve, dve.n)
                    p.dma("sync", d["routedbg"][:, 0:32], lgts, qs)
                    p.dma("sync", d["routedbg"][:, 32:64], m8s, qs)
        xfm_free[c % 2] = pe.n
    routing_done = p.sems["dve"].n
    a16.pop()
    p.wait("gpsimd", pe, pe.n)
    p.wait("gpsimd", dve, dve.n)
    p.wait("gpsimd", act, act.n)
    if "c32dbg" in mk.dbg:
        p.wait("sync", p.sems["dve"], routing_done)
        p.dma("sync", d["c32dbg"][:, :], c32, qs)

    w1e = a16.alloc(8 * 4096)
    w2e = a16.alloc(32 * 1024)
    b1sb = a16.alloc32(32)
    b2sb = a16.alloc(1024)
    he = a16.alloc(32 * 512)
    ysb = a16.alloc32(4 * 512)
    bank_he = [PS[:, 2048:2560], PS[:, 2560:3072]]
    bank_y = [PS[:, 0:512], PS[:, 512:1024], PS[:, 1024:1536], PS[:, 1536:2048]]

    acc_r = acc_h.rearrange("s d -> s d")
    w_free = 0
    w1_free = 0
    acc_order = {}
    ysb_free = {}
    xfm_ld = [0, 0]
    for e in range(8):
        p.wait("gpsimd", pe, w1_free)
        for t in range(8):
            p.dma("gpsimd", w1e[:, t * 4096:(t + 1) * 4096],
                  W1[e, t * 128:(t + 1) * 128, :], qg)
        p.wait("gpsimd", pe, w_free)
        for t in range(32):
            p.dma("gpsimd", w2e[:, t * 1024:(t + 1) * 1024],
                  W2[e, t * 128:(t + 1) * 128, :], qg)
        p.dma("gpsimd", b2sb[0:1, :], b2[e][None, :], qg)
        w_at = qg.n
        p.dma("sync", b1sb, b1[e].rearrange("(t p) -> p t", p=128), qs, slow=True)
        b1_at = qs.n

        for c in range(NC5):
            buf = xfm[c % 2]
            if c == 0:
                p.wait("sync", pe, xfm_free[0])
                ld = load_xfm(0, buf)
                xfm_ld[0] = ld
            ld = xfm_ld[c % 2]
            if c + 1 < NC5:
                p.wait("sync", pe, xfm_free[(c + 1) % 2])
                xfm_ld[(c + 1) % 2] = load_xfm(c + 1, xfm[(c + 1) % 2])
            p.wait("tensor", qs, ld)
            if c == 0:
                p.wait("tensor", qg, w_at)
                p.wait("scalar", qs, b1_at)
                if e == 0:
                    p.wait("scalar", dve, routing_done)
            he_evac = {}
            for ff in range(32):
                bnk = bank_he[ff % 2]
                if ff >= 2:
                    p.wait("tensor", act, he_evac[ff - 2])
                for t in range(8):
                    p.mm(bnk, w1e[:, t * 4096 + ff * 128: t * 4096 + (ff + 1) * 128],
                         buf[:, t * 512:(t + 1) * 512], start=(t == 0), stop=(t == 7),
                         ctr=pe if t == 7 else None)
                p.wait("scalar", pe, pe.n)
                p.act(he[:, ff * 512:(ff + 1) * 512], bnk, AF.Relu,
                      bias=b1sb[:, ff:ff + 1], ctr=act)
                he_evac[ff] = act.n
            if "hedbg" in mk.dbg and e == 0 and c == 0:
                p.wait("sync", p.sems["act"], p.sems["act"].n)
                p.dma("sync", d["hedbg"][:, :], he[:, 0:32 * 512], qs)
            if c == NC5 - 1:
                w1_free = pe.n
            for dh in range(2):
                for ts in range(4):
                    it = c * 4 + ts
                    bnk = bank_y[ts]
                    if ts in ysb_free:
                        p.wait("tensor", qg, ysb_free[ts])
                        p.wait("scalar", qg, ysb_free[ts])
                        p.wait("vector", qg, ysb_free[ts])
                    for ff in range(32):
                        if ff == 0:
                            p.wait("tensor", act, he_evac[31])
                        p.mm(bnk, he[:, ff * 512 + ts * 128: ff * 512 + (ts + 1) * 128],
                             w2e[:, ff * 1024 + dh * 512: ff * 1024 + dh * 512 + 512],
                             start=(ff == 0), stop=False)
                    p.mm(bnk, C["ones16"][0:1, 0:128], b2sb[0:1, dh * 512: dh * 512 + 512],
                         start=False, stop=True, ctr=pe)
                    mmv = pe.n
                    ysl = ysb[:, ts * 512:(ts + 1) * 512]
                    csc = c32[:, it * 8 + e: it * 8 + e + 1]
                    if dh == 0:
                        p.wait("scalar", pe, mmv)
                        p.act(ysl, bnk, AF.Copy, scale=csc, ctr=act)
                        ev = ("scalar", act.n)
                    else:
                        p.wait("vector", pe, mmv)
                        p.v("tensor_scalar_mul", ysl, bnk, csc, ctr=dve)
                        ev = ("vector", dve.n)
                    p.wait("gpsimd", p.sems[{"scalar": "act", "vector": "dve"}[ev[0]]],
                           ev[1])
                    if (it, dh) in acc_order:
                        p.wait("gpsimd", qg, acc_order[(it, dh)])
                    dst = acc_h[it * 128:(it + 1) * 128, dh * 512:(dh + 1) * 512]
                    p.dma("gpsimd", dst, ysl, qg, accum=(e > 0))
                    acc_order[(it, dh)] = qg.n
                    ysb_free[ts] = qg.n
            xfm_free[c % 2] = pe.n
        w_free = pe.n
    a16.pop()


def phase_combine(mk, p, a16, which):
    d, C = mk.d, mk.C
    qs, dve, act = p.sems["qs"], p.sems["dve"], p.sems["act"]
    a16.push()
    entry_guard(p)
    x0r = d["x0"].rearrange("(n p) d -> n p d", p=128)
    s_r = d["s32"].rearrange("(n p) d -> n p d", p=128)
    f_r = d["ffn0"].rearrange("(n p) d -> n p d", p=128)
    o_r = d["X132"].rearrange("(n p) d -> n p d", p=128)
    xb = [a16.alloc32(D) for _ in range(2)]
    sb_ = [a16.alloc32(D) for _ in range(2)]
    fb = [a16.alloc32(D) for _ in range(2)]
    gb = [a16.alloc32(D) for _ in range(2)]
    free3 = {}
    for i in range(NT):
        bi = i % 2
        if i >= 2:
            p.wait("sync", dve, free3[i - 2][0])
            p.wait("sync", qs, free3[i - 2][1])
        p.dma("sync", xb[bi], x0r[i], qs)
        p.dma("sync", sb_[bi], s_r[i], qs)
        p.dma("sync", fb[bi], f_r[i], qs)
        ld = qs.n
        p.wait("scalar", qs, ld)
        p.act(gb[bi], fb[bi], AF.Silu, ctr=act)
        p.wait("vector", act, act.n)
        p.v("tensor_mul", gb[bi], gb[bi], sb_[bi], edge=True)
        p.v("tensor_add", gb[bi], gb[bi], xb[bi], ctr=dve)
        dn = dve.n
        p.wait("sync", dve, dn)
        p.dma("sync", o_r[i], gb[bi], qs)
        free3[i] = (dn, qs.n)
    a16.pop()


def phase_E(mk, p, a16):
    d, C = mk.d, mk.C
    qs, dve, act = p.sems["qs"], p.sems["dve"], p.sems["act"]
    a16.push()
    entry_guard(p)
    wbc = load_row_bcast(mk, p, a16, d["L1_attn_w_ln1"], D)
    p.wait("vector", qs, qs.n)
    p.v("tensor_scalar_add", wbc, wbc, 1.0, ctr=dve)
    xin = [a16.alloc32(D) for _ in range(2)]
    st8 = [a16.alloc32(12) for _ in range(2)]
    mv = [a16.alloc32(4) for _ in range(2)]
    xn32t = [a16.alloc32(D) for _ in range(2)]
    xn16t = [a16.alloc(D) for _ in range(2)]
    h0t = [a16.alloc(D) for _ in range(2)]

    x0r = d["X132"].rearrange("(n p) d -> n p d", p=128)
    xn32r = d["xn1_32"].rearrange("(n p) d -> n p d", p=128)
    xn16r = d["xn1_16"].rearrange("(n p) d -> n p d", p=128)
    h0r = d["h016"].rearrange("(n p) d -> n p d", p=128)

    free_at = {}
    p.wait("sync", dve, dve.n)
    for i in range(NT):
        bi = i % 2
        if i >= 2:
            p.wait("sync", dve, free_at[i - 2][0])
            p.wait("sync", qs, free_at[i - 2][1])
        p.dma("sync", xin[bi], x0r[i], qs)
        ld = qs.n
        p.wait("vector", qs, ld)
        p.v("bn_stats", st8[bi][:, 0:6], xin[bi][:, 0:512], edge=True)
        p.v("bn_stats", st8[bi][:, 6:12], xin[bi][:, 512:1024], edge=True)
        p.v("bn_aggr", mv[bi][:, 0:2], st8[bi][:, 0:12], edge=True)
        p.wait("scalar", dve, dve.n)
        emit_rsqrt(p, mv[bi][:, 2:3], mv[bi][:, 1:2], pre_bias=1e-5)
        rstd_done = act.n
        p.v("tensor_scalar_add", mv[bi][:, 3:4], mv[bi][:, 1:2], 1e-5, edge=True)
        p.v("reciprocal", mv[bi][:, 3:4], mv[bi][:, 3:4], edge=True)
        p.v("tensor_mul", mv[bi][:, 3:4], mv[bi][:, 3:4], mv[bi][:, 1:2], edge=True)
        p.v("tensor_scalar_add", mv[bi][:, 3:4], mv[bi][:, 3:4], 1e-6, ctr=dve)
        p.wait("scalar", dve, dve.n)
        emit_rsqrt(p, mv[bi][:, 3:4], mv[bi][:, 3:4])
        rms2_done = act.n
        p.wait("vector", act, rstd_done)
        p.v("tensor_scalar", xn32t[bi], xin[bi], mv[bi][:, 0:1], mv[bi][:, 2:3],
            ALU.subtract, ALU.mult, edge=True)
        p.v("tensor_copy", xn16t[bi], xn32t[bi], ctr=dve)
        xn_done = dve.n
        p.wait("vector", act, rms2_done)
        p.v("scalar_tensor_tensor", h0t[bi], xn32t[bi], mv[bi][:, 3:4], wbc,
            ALU.mult, ALU.mult, ctr=dve)
        h0_done = dve.n
        p.wait("sync", dve, xn_done)
        p.dma("sync", xn32r[i], xn32t[bi], qs)
        p.dma("sync", xn16r[i], xn16t[bi], qs)
        p.wait("sync", dve, h0_done)
        p.dma("sync", h0r[i], h0t[bi], qs)
        free_at[i] = (dve.n, qs.n)
    a16.pop()


def phase_F(mk, p, a16):
    d, C = mk.d, mk.C
    qs, qg = p.sems["qs"], p.sems["qg"]
    pe, act, dve = p.sems["pe"], p.sems["act"], p.sems["dve"]
    PS, PT = mk.PS, mk.PT
    a16.push()
    entry_guard(p)

    wqkvz = a16.alloc(8 * 3072)
    for t in range(8):
        p.dma("gpsimd", wqkvz[:, t * 3072:(t + 1) * 3072],
              d["L1_attn_W_qkvz"][t * 128:(t + 1) * 128, :], qg)
    wba = a16.alloc(8 * 16)
    for t in range(8):
        p.dma("gpsimd", wba[:, t * 16:(t + 1) * 16],
              d["L1_attn_W_ba"][t * 128:(t + 1) * 128, :], qg)
    wout = a16.alloc(8 * D)
    for t in range(8):
        p.dma("gpsimd", wout[:, t * D:(t + 1) * D],
              d["L1_attn_W_out"][t * 128:(t + 1) * 128, :], qg)
    w_at = qg.n
    convw = a16.alloc32(64)
    p.dma("sync", convw, d["L1_attn_conv_w"].rearrange("(t p) k -> p t k", p=128), qs)
    dtb = a16.alloc32(2)
    p.dma("sync", dtb[0:8, 0:1], d["L1_attn_dt_bias"][:, None], qs)
    alog = a16.alloc32(2)
    p.dma("sync", alog[0:8, 0:1], d["L1_attn_A_log"][:, None], qs)
    gnw = a16.alloc32(2)
    p.dma("sync", gnw[:, 0:1], d["L1_attn_gate_norm_w"][:, None], qs)
    ln2bc = load_row_bcast(mk, p, a16, d["L1_attn_w_ln2"], D)
    cw_at = qs.n
    p.wait("vector", qs, cw_at)
    p.v("tensor_scalar_add", ln2bc, ln2bc, 1.0, edge=True)
    p.wait("scalar", qs, cw_at)
    p.act(alog[0:8, 1:2], alog[0:8, 0:1], AF.Exp, edge=True)
    p.wait("vector", act, act.n)
    p.v("tensor_scalar_mul", alog[0:8, 1:2], alog[0:8, 1:2], -1.0, ctr=dve)
    dg = a16.alloc(512)
    p.wait("vector", qs, cw_at)
    diag_done = p.sems["dve"].n

    hfm = [a16.alloc(4096) for _ in range(2)]
    qkvb = a16.alloc(16 * 528)
    halo = a16.alloc(48)
    zsil = a16.alloc(8 * 512)
    bas = a16.alloc32(512)
    absb = a16.alloc32(512)
    gfm = a16.alloc32(512)
    S0 = a16.alloc32(8 * 128)
    S0b = a16.alloc(8 * 128)
    qn = a16.alloc(4 * 128)
    kn = a16.alloc(4 * 128)
    ktm = a16.alloc(4 * 128)
    vtm = a16.alloc(8 * 128)
    bcol = a16.alloc32(16)
    bwork = a16.alloc32(80)
    rows = a16.alloc32(8 * 128)
    rows16 = a16.alloc(8 * 128)
    erow16 = a16.alloc(8 * 128)
    DDa = a16.alloc32(1024)
    DDc = a16.alloc32(1024)
    EMLa = a16.alloc(1024)
    EMUa = a16.alloc(1024)
    Aa = a16.alloc(1024)
    Va = a16.alloc(1024)
    VTa = a16.alloc(1024)
    V2a = a16.alloc(1024)
    V2Ta = a16.alloc(1024)
    XAPa = a16.alloc(2048)
    MKTa = Aa
    DLTa = EMUa
    QTa = EMLa
    AQTa = a16.alloc(1024)
    OSBa = V2Ta
    KPa = V2a
    sqv = a16.alloc(128)
    rr = a16.alloc32(1056)
    core = a16.alloc(8 * 128)
    xres = a16.alloc32(D)
    x2t = a16.alloc32(D)
    xht = a16.alloc(D)
    sq2 = xres
    mv2 = a16.alloc32(4)
    halo_init = False

    b0, b1 = PS[:, 0:512], PS[:, 512:1024]
    b2, b3 = PS[:, 1024:1536], PS[:, 1536:2048]
    b4, b5 = PS[:, 2048:2560], PS[:, 2560:3072]
    tpA, tpB = PT[:, 0:1024], PT[:, 1024:2048]

    src16 = d["h016"]
    xn1r = d["xn1_32"].rearrange("(n p) d -> n p d", p=128)
    x2r = d["x232"].rearrange("(n p) d -> n p d", p=128)
    xhr = d["xh16"].rearrange("(n p) d -> n p d", p=128)

    p.v("memset", S0, 0.0, edge=True)
    p.v("memset", S0b, 0.0, edge=True)
    p.v("memset", qkvb, 0.0, ctr=dve)
    init_done = dve.n
    p.wait("tensor", dve, max(init_done, diag_done))
    p.wait("tensor", qg, w_at)

    def load_hfm(c, buf):
        for t in range(8):
            p.dma("sync", buf[:, t * 512:(t + 1) * 512],
                  src16[c * 512:(c + 1) * 512, t * 128:(t + 1) * 128],
                  qs, transpose=True)
        return qs.n

    for sc in range(NC5):
        buf = hfm[sc % 2]
        p.wait("sync", pe, pe.n)
        ld = load_hfm(sc, buf)
        p.wait("tensor", qs, ld)
        for ct in range(16):
            sl = qkvb[:, ct * 528: ct * 528 + 515]
            if sc > 0:
                p.v("tensor_copy", sl[:, 0:3], halo[:, ct * 3:(ct + 1) * 3],
                    edge=True)
            bnk = b0 if ct % 2 == 0 else b1
            p.wait("tensor", dve, dve.n)
            p.wait("tensor", act, act.n)
            for t in range(8):
                p.mm(bnk, wqkvz[:, t * 3072 + ct * 128: t * 3072 + (ct + 1) * 128],
                     buf[:, t * 512:(t + 1) * 512], start=(t == 0), stop=(t == 7),
                     ctr=pe if t == 7 else None)
            p.wait("vector", pe, pe.n)
            p.v("tensor_copy", sl[:, 3:515], bnk, edge=True)
            p.v("tensor_copy", halo[:, ct * 3:(ct + 1) * 3], sl[:, 512:515],
                ctr=dve)
        for ct in range(16):
            bnk = b0 if ct % 2 == 0 else b1
            p.wait("vector", pe, pe.n)
            for tap in range(4):
                p.v("tensor_scalar_mul", dg[:, tap * 128:(tap + 1) * 128],
                    C["ident16"], convw[:, ct * 4 + tap: ct * 4 + tap + 1],
                    ctr=dve if tap == 3 else None)
            p.wait("tensor", dve, dve.n)
            p.wait("tensor", act, act.n)
            for tap in range(4):
                p.mm(bnk, dg[:, tap * 128:(tap + 1) * 128],
                     qkvb[:, ct * 528 + tap: ct * 528 + tap + 512],
                     start=(tap == 0), stop=(tap == 3), ctr=pe if tap == 3 else None)
            p.wait("scalar", pe, pe.n)
            p.act(qkvb[:, ct * 528 + 3: ct * 528 + 515], bnk, AF.Silu, ctr=act)
        for zt in range(8):
            ct = 16 + zt
            bnk = b0 if ct % 2 == 0 else b1
            p.wait("tensor", dve, dve.n)
            p.wait("tensor", act, act.n)
            for t in range(8):
                p.mm(bnk, wqkvz[:, t * 3072 + ct * 128: t * 3072 + (ct + 1) * 128],
                     buf[:, t * 512:(t + 1) * 512], start=(t == 0), stop=(t == 7),
                     ctr=pe if t == 7 else None)
            p.wait("scalar", pe, pe.n)
            p.act(zsil[:, zt * 512:(zt + 1) * 512], bnk, AF.Silu, ctr=act)
        p.wait("tensor", act, act.n)
        for t in range(8):
            p.mm(b0[0:8, :], wba[:, t * 16:t * 16 + 8],
                 buf[:, t * 512:(t + 1) * 512], start=(t == 0), stop=(t == 7))
        for t in range(8):
            p.mm(b1[0:8, :], wba[:, t * 16 + 8:t * 16 + 16],
                 buf[:, t * 512:(t + 1) * 512], start=(t == 0), stop=(t == 7),
                 ctr=pe if t == 7 else None)
        p.wait("vector", pe, pe.n)
        p.v("tensor_copy", bas[0:8, :], b0[0:8, :], edge=True)
        p.v("tensor_copy", absb[0:8, :], b1[0:8, :], edge=True)
        p.wait("scalar", dve, dve.n)
        p.act(bas[0:8, :], bas[0:8, :], AF.Sigmoid, edge=True)
        p.act(gfm[0:8, :], absb[0:8, :], AF.Abs, bias=dtb[0:8, 0:1], edge=True)
        p.act(gfm[0:8, :], gfm[0:8, :], AF.Exp, scale=-1.0, edge=True)
        p.act(gfm[0:8, :], gfm[0:8, :], AF.Ln, bias=1.0, ctr=act)
        sp_done = act.n
        p.wait("vector", act, sp_done)
        p.v("tensor_scalar", absb[0:8, :], absb[0:8, :], dtb[0:8, 0:1], 0.0,
            ALU.add, ALU.max, edge=True)
        p.v("tensor_add", gfm[0:8, :], gfm[0:8, :], absb[0:8, :], edge=True)
        p.v("tensor_scalar_mul", gfm[0:8, :], gfm[0:8, :], alog[0:8, 1:2], ctr=dve)
        g_done = dve.n

        for dc in range(4):
            tk = dc * 128
            ti = sc * 4 + dc
            p.wait("tensor", dve, g_done)
            p.wait("tensor", act, act.n)
            p.trans(b2[0:128, 0:8], gfm[0:8, tk:tk + 128], C["ident32"][0:8, 0:8], ctr=pe)
            p.wait("vector", pe, pe.n)
            p.v("tensor_copy", bwork[:, 0:8], b2[:, 0:8], ctr=dve)
            p.wait("tensor", dve, dve.n)
            p.mm(b3[:, 0:8], C["cum_ut32"], bwork[:, 0:8], start=True, stop=True)
            p.trans(b2[0:128, 8:16], bas[0:8, tk:tk + 128], C["ident32"][0:8, 0:8], ctr=pe)
            p.wait("vector", pe, pe.n)
            p.v("tensor_copy", bcol[:, 0:8], b3[:, 0:8], edge=True)
            p.v("tensor_copy", bwork[:, 8:16], b2[:, 8:16], edge=True)
            p.v("tensor_scalar_mul", bwork[:, 16:24], bwork[:, 8:16], -1.0, edge=True)
            p.wait("scalar", dve, dve.n)
            p.act(bwork[:, 24:32], bcol[:, 0:8], AF.Exp, ctr=act)
            p.wait("vector", act, act.n)
            p.v("tensor_mul", bwork[:, 32:40], bwork[:, 16:24], bwork[:, 24:32],
                ctr=dve)
            percol_done = dve.n
            p.wait("tensor", dve, percol_done)
            for h in range(8):
                p.trans(b2[0:1, 256:384], bcol[:, h:h + 1], C["ident32"], ctr=pe)
                p.wait("vector", pe, pe.n)
                p.v("tensor_copy", rows[0:1, h * 128:(h + 1) * 128],
                    b2[0:1, 256:384], ctr=dve)
                p.wait("tensor", dve, dve.n)
            rows_done = p.sems["dve"].n
            p.wait("scalar", dve, rows_done)
            p.act(erow16[0:1, :], rows[0:1, :], AF.Exp, ctr=act)
            p.wait("vector", act, act.n)
            for h in range(8):
                p.v("tensor_copy", rr[0:1, 256 + h:257 + h],
                    rows[0:1, h * 128 + 127: h * 128 + 128],
                    ctr=dve if h == 7 else None)
            p.wait("tensor", dve, dve.n)
            p.mm(b5[:, 0:8], C["ones32"][0:1, 0:128], rr[0:1, 256:264],
                 start=True, stop=True, ctr=pe)
            p.wait("vector", pe, pe.n)
            p.v("tensor_copy", bwork[:, 40:48], b5[:, 0:8], edge=True)
            p.v("tensor_sub", bwork[:, 48:56], bwork[:, 40:48], bcol[:, 0:8], ctr=dve)
            p.wait("scalar", dve, dve.n)
            p.act(bwork[:, 56:64], bwork[:, 48:56], AF.Exp, edge=True)
            p.act(bwork[:, 64:72], bwork[:, 40:48], AF.Exp, ctr=act)
            percol2_done = act.n

            p.wait("tensor", act, percol2_done)
            for hh in range(4):
                qsl = qkvb[:, hh * 528 + 3 + tk: hh * 528 + 3 + tk + 128]
                ksl = qkvb[:, (4 + hh) * 528 + 3 + tk: (4 + hh) * 528 + 3 + tk + 128]
                for (xi, xsl, osl, qscale) in ((0, qsl, qn, True), (1, ksl, kn, False)):
                    dst = osl[:, hh * 128:(hh + 1) * 128]
                    p.wait("vector", pe, pe.n)
                    p.v("tensor_mul", sqv, xsl, xsl, ctr=dve)
                    p.wait("tensor", dve, dve.n)
                    p.mm(b2[0:1, 256:384], C["ones16"][:, 0:1], sqv,
                         start=True, stop=True, ctr=pe)
                    p.wait("vector", pe, pe.n)
                    p.v("tensor_copy", rr[0:1, 0:128], b2[0:1, 256:384], ctr=dve)
                    p.wait("scalar", dve, dve.n)
                    sc_ = (128.0 ** -0.5) if qscale else 1.0
                    emit_rsqrt(p, rr[0:1, 128:256], rr[0:1, 0:128], pre_bias=1e-6,
                               post_scale=sc_)
                    p.wait("vector", act, act.n)
                    p.v("tensor_copy", rows16[0:1, 0:128], rr[0:1, 128:256],
                        ctr=dve)
                    p.wait("tensor", dve, dve.n)
                    p.mm(b3[:, 256:384], C["ones16"][0:1, 0:128],
                         rows16[0:1, 0:128], start=True, stop=True, ctr=pe)
                    p.wait("vector", pe, pe.n)
                    p.v("tensor_mul", dst, xsl, b3[:, 256:384], ctr=dve)
                p.wait("tensor", dve, dve.n)
                p.trans(tpB[:, 0:128], kn[:, hh * 128:(hh + 1) * 128], C["ident16"],
                        ctr=pe)
                p.wait("vector", pe, pe.n)
                p.v("tensor_copy", ktm[:, hh * 128:(hh + 1) * 128], tpB[:, 0:128],
                    ctr=dve)
            for h in range(8):
                vsl = qkvb[:, (8 + h) * 528 + 3 + tk: (8 + h) * 528 + 3 + tk + 128]
                p.wait("tensor", dve, dve.n)
                p.trans(tpB[:, 128:256], vsl, C["ident16"], ctr=pe)
                p.wait("vector", pe, pe.n)
                p.v("tensor_copy", vtm[:, h * 128:(h + 1) * 128], tpB[:, 128:256],
                    ctr=dve)
            prep_done = dve.n

            rbrow = rows16
            p.wait("tensor", dve, dve.n)
            p.wait("tensor", act, act.n)
            for h in range(8):
                p.mm(PS[:, 1024 + h * 128:1024 + (h + 1) * 128],
                     C["ones32"][0:1, 0:128], rows[0:1, h * 128:(h + 1) * 128],
                     start=True, stop=True, ctr=pe if h == 7 else None)
            p.wait("vector", pe, pe.n)
            for h in range(8):
                p.v("tensor_scalar", DDa[:, h * 128:(h + 1) * 128],
                    PS[:, 1024 + h * 128:1024 + (h + 1) * 128], bcol[:, h:h + 1],
                    None, ALU.subtract, ctr=dve if h == 7 else None)
            p.wait("vector", dve, dve.n)
            p.v("tensor_scalar", DDc, DDa, -1.0, 0.0, ALU.mult, ALU.min, ctr=dve)
            p.wait("scalar", dve, dve.n)
            p.act(DDc, DDc, AF.Exp, ctr=act)
            p.wait("vector", act, act.n)
            mlow_r = C["m_lows32"].rearrange("p (o f) -> p o f", o=1).to_broadcast([128, 8, 128])
            mupi_r = C["m_upi32"].rearrange("p (o f) -> p o f", o=1).to_broadcast([128, 8, 128])
            p.v("tensor_tensor", EMLa.rearrange("p (h f) -> p h f", f=128),
                DDc.rearrange("p (h f) -> p h f", f=128), mlow_r, ALU.mult,
                edge=True)
            p.v("tensor_scalar_min", DDc, DDa, 0.0, ctr=dve)
            p.wait("scalar", dve, dve.n)
            p.act(DDc, DDc, AF.Exp, ctr=act)
            p.wait("vector", act, act.n)
            p.v("tensor_tensor", EMUa.rearrange("p (h f) -> p h f", f=128),
                DDc.rearrange("p (h f) -> p h f", f=128), mupi_r, ALU.mult,
                ctr=dve)
            p.wait("tensor", dve, dve.n)
            for h in range(8):
                kf = kn[:, (h // 2) * 128:(h // 2 + 1) * 128]
                qf = qn[:, (h // 2) * 128:(h // 2 + 1) * 128]
                p.mm(PS[:, 1024 + h * 128:1024 + (h + 1) * 128], kf, kf,
                     start=True, stop=True)
                p.mm(PS[:, 2048 + h * 128:2048 + (h + 1) * 128], kf, qf,
                     start=True, stop=True, ctr=pe if h == 7 else None)
            p.wait("vector", pe, pe.n)
            p.v("tensor_copy", Aa, PS[:, 1024:2048], edge=True)
            p.v("tensor_tensor", AQTa, PS[:, 2048:3072], EMUa, ALU.mult, edge=True)
            for h in range(8):
                p.v("scalar_tensor_tensor", Va[:, h * 128:(h + 1) * 128],
                    Aa[:, h * 128:(h + 1) * 128], bwork[:, 16 + h:17 + h],
                    EMLa[:, h * 128:(h + 1) * 128], ALU.mult, ALU.mult,
                    ctr=dve if h == 7 else None)
            p.wait("tensor", dve, dve.n)
            for h in range(8):
                p.trans(tpA[:, h * 128:(h + 1) * 128], Va[:, h * 128:(h + 1) * 128],
                        C["ident16"], ctr=pe if h == 7 else None)
            p.wait("vector", pe, pe.n)
            p.v("tensor_copy", VTa, tpA[:, 0:1024], edge=True)
            for h in range(8):
                p.v("tensor_scalar_mul", XAPa[:, h * 256:h * 256 + 128],
                    vtm[:, h * 128:(h + 1) * 128], bwork[:, 8 + h:9 + h])
                p.v("tensor_scalar_mul", XAPa[:, h * 256 + 128:h * 256 + 256],
                    ktm[:, (h // 2) * 128:(h // 2 + 1) * 128],
                    bwork[:, 32 + h:33 + h], ctr=dve if h == 7 else None)
            p.wait("tensor", dve, dve.n)
            vc, vtc, v2c, v2tc = Va, VTa, V2a, V2Ta
            for itn in range(7):
                for h in range(8):
                    hb = PS[:, h * 256:(h + 1) * 256]
                    p.mm(hb, C["ident16"], XAPa[:, h * 256:(h + 1) * 256],
                         start=True, stop=False)
                    p.mm(hb, vtc[:, h * 128:(h + 1) * 128],
                         XAPa[:, h * 256:(h + 1) * 256],
                         start=False, stop=True,
                         ctr=pe if (h == 7 and itn == 6) else None)
                if itn < 6:
                    for h in range(8):
                        p.mm(PS[:, 2048 + h * 128:2048 + (h + 1) * 128],
                             vtc[:, h * 128:(h + 1) * 128],
                             vc[:, h * 128:(h + 1) * 128], start=True, stop=True,
                             ctr=pe if h == 7 else None)
                p.wait("vector", pe, pe.n)
                p.v("tensor_copy", XAPa, PS[:, 0:2048], edge=True)
                if itn < 6:
                    p.v("tensor_copy", v2c, PS[:, 2048:3072], ctr=dve)
                    p.wait("tensor", dve, dve.n)
                    for h in range(8):
                        p.mm(PS[:, 2048 + h * 128:2048 + (h + 1) * 128],
                             vc[:, h * 128:(h + 1) * 128],
                             vtc[:, h * 128:(h + 1) * 128], start=True, stop=True,
                             ctr=pe if h == 7 else None)
                    p.wait("vector", pe, pe.n)
                    p.v("tensor_copy", v2tc, PS[:, 2048:3072], ctr=dve)
                    vc, v2c = v2c, vc
                    vtc, v2tc = v2tc, vtc
                else:
                    p.v("engine_nop", ctr=dve)
                p.wait("tensor", dve, dve.n)
            for h in range(8):
                p.trans(tpB[:, h * 128:(h + 1) * 128],
                        XAPa[:, h * 256 + 128:h * 256 + 256], C["ident16"],
                        ctr=pe if h == 7 else None)
            p.wait("vector", pe, pe.n)
            p.v("tensor_copy", MKTa, tpB[:, 0:1024], ctr=dve)
            p.wait("tensor", dve, dve.n)
            for h in range(8):
                p.mm(PS[:, h * 128:(h + 1) * 128], MKTa[:, h * 128:(h + 1) * 128],
                     S0b[:, h * 128:(h + 1) * 128], start=True, stop=True,
                     ctr=pe if h == 7 else None)
            p.wait("vector", pe, pe.n)
            p.v("tensor_tensor", DLTa.rearrange("p (h f) -> p h f", f=128),
                PS[:, 0:1024].rearrange("p (h f) -> p h f", f=128),
                XAPa.rearrange("p (h f) -> p h f", f=256)[:, :, 0:128],
                ALU.add, ctr=dve)
            p.wait("tensor", dve, dve.n)
            for h in range(8):
                p.mm(PS[:, 1024 + h * 128:1024 + (h + 1) * 128],
                     C["ones16"][0:1, 0:128], erow16[0:1, h * 128:(h + 1) * 128],
                     start=True, stop=True, ctr=pe if h == 7 else None)
            p.wait("vector", pe, pe.n)
            p.v("tensor_tensor", QTa.rearrange("p (h r f) -> p h r f", r=2, f=128),
                qn.rearrange("p (h o f) -> p h o f", o=1, f=128)
                  .to_broadcast([128, 4, 2, 128]),
                PS[:, 1024:2048].rearrange("p (h r f) -> p h r f", r=2, f=128),
                ALU.mult, ctr=dve)
            p.wait("tensor", dve, dve.n)
            for h in range(8):
                ob = PS[:, 1024 + h * 128:1024 + (h + 1) * 128]
                p.mm(ob, S0b[:, h * 128:(h + 1) * 128],
                     QTa[:, h * 128:(h + 1) * 128], start=True, stop=False)
                p.mm(ob, DLTa[:, h * 128:(h + 1) * 128],
                     AQTa[:, h * 128:(h + 1) * 128], start=False, stop=True,
                     ctr=pe if h == 7 else None)
            p.wait("vector", pe, pe.n)
            p.v("tensor_copy", OSBa, PS[:, 1024:2048], edge=True)
            for h in range(8):
                p.v("tensor_scalar_mul", KPa[:, h * 128:(h + 1) * 128],
                    ktm[:, (h // 2) * 128:(h // 2 + 1) * 128],
                    bwork[:, 56 + h:57 + h], ctr=dve if h == 7 else None)
            p.wait("tensor", dve, dve.n)
            for h in range(8):
                p.mm(PS[:, 2048 + h * 128:2048 + (h + 1) * 128],
                     KPa[:, h * 128:(h + 1) * 128], DLTa[:, h * 128:(h + 1) * 128],
                     start=True, stop=True, ctr=pe if h == 7 else None)
            p.wait("vector", pe, pe.n)
            for h in range(8):
                p.v("scalar_tensor_tensor", S0[:, h * 128:(h + 1) * 128],
                    S0[:, h * 128:(h + 1) * 128], bwork[:, 64 + h:65 + h],
                    PS[:, 2048 + h * 128:2048 + (h + 1) * 128], ALU.mult, ALU.add,
                    ctr=dve if h == 7 else None)
            p.wait("vector", dve, dve.n)
            p.v("tensor_copy", S0b, S0, edge=True)
            p.v("tensor_mul", KPa, OSBa, OSBa, ctr=dve)
            p.wait("tensor", dve, dve.n)
            p.mm(PS[0:1, 0:512], C["ones16"][:, 0:1], KPa[:, 0:512],
                 start=True, stop=True)
            p.mm(PS[0:1, 512:1024], C["ones16"][:, 0:1], KPa[:, 512:1024],
                 start=True, stop=True, ctr=pe)
            p.wait("scalar", pe, pe.n)
            emit_rsqrt(p, rr[0:1, 0:1024], PS[0:1, 0:1024], pre_scale=1.0 / 128,
                       pre_bias=1e-6)
            p.wait("vector", act, act.n)
            p.v("tensor_copy", rbrow[0:1, 0:1024], rr[0:1, 0:1024], ctr=dve)
            p.wait("tensor", dve, dve.n)
            for h in range(8):
                p.mm(PS[:, 1024 + h * 128:1024 + (h + 1) * 128],
                     C["ones16"][0:1, 0:128], rbrow[0:1, h * 128:(h + 1) * 128],
                     start=True, stop=True, ctr=pe if h == 7 else None)
            p.wait("vector", pe, pe.n)
            p.v("scalar_tensor_tensor", OSBa, OSBa, gnw[:, 0:1], PS[:, 1024:2048],
                ALU.mult, ALU.mult, edge=True)
            p.v("tensor_tensor", core.rearrange("p (h f) -> p h f", f=128),
                OSBa.rearrange("p (h f) -> p h f", f=128),
                zsil.rearrange("p (h f) -> p h f", f=512)[:, :, tk:tk + 128],
                ALU.mult, ctr=dve)
            core_done = dve.n

            p.dma("sync", xres, xn1r[ti], qs)
            ldx = qs.n
            p.wait("tensor", dve, core_done)
            for half in range(2):
                bnk = b0 if half == 0 else b1
                for t in range(8):
                    p.mm(bnk, core[:, t * 128:(t + 1) * 128],
                         wout[:, t * D + half * 512: t * D + half * 512 + 512],
                         start=(t == 0), stop=(t == 7), ctr=pe if t == 7 else None)
                p.wait("vector", pe, pe.n)
                p.wait("vector", qs, ldx)
                p.v("tensor_add", x2t[:, half * 512:(half + 1) * 512],
                    xres[:, half * 512:(half + 1) * 512], bnk, ctr=dve)
            x2_done = dve.n
            p.wait("sync", dve, x2_done)
            p.dma("sync", x2r[ti], x2t, qs)
            p.wait("scalar", dve, x2_done)
            p.act(sq2, x2t, AF.Square, accum_out=mv2[:, 0:1], edge=True)
            emit_rsqrt(p, mv2[:, 1:2], mv2[:, 0:1], pre_scale=1.0 / D,
                       pre_bias=1e-6)
            p.wait("vector", act, act.n)
            p.v("scalar_tensor_tensor", xht, x2t, mv2[:, 1:2], ln2bc,
                ALU.mult, ALU.mult, ctr=dve)
            p.wait("sync", dve, dve.n)
            p.dma("sync", xhr[ti], xht, qs)
    a16.pop()


def phase_prepass(mk, p, a16):
    d = mk.d
    qg = p.sems["qg"]
    jobs = [
        (d["L1_ffn_shared_w1"], d["sw1_16"], 4096),
        (d["L1_ffn_shared_w2"], d["sw2_16"], 1024),
        (d["L1_ffn_experts_w1"].rearrange("e f d -> (e f) d"), d["ew1_16"], 8 * 4096),
        (d["L1_ffn_experts_w2"].rearrange("e d f -> (e d) f"), d["ew2_16"], 8 * 1024),
        (d["L1_ffn_gate_w"], d["gw_16"], 8),
    ]
    for srcT, dstT, R in jobs:
        r = 0
        while r < R:
            rr_ = min(1024, R - r)
            p.dma("gpsimd", dstT[r:r + rr_, :], srcT[r:r + rr_, :], qg)
            r += rr_


def phase_moe2(mk, p, a16, which):
    d, C = mk.d, mk.C
    qs, qg = p.sems["qs"], p.sems["qg"]
    pe, act, dve = p.sems["pe"], p.sems["act"], p.sems["dve"]
    PS = mk.PS
    C = mk.C
    a16.push()
    entry_guard(p)

    if which == "G":
        src16, acc_h = d["xh16"], d["moe1"]
        NFF, glu = 4, True
    else:
        src16, acc_h = d["xn1_16"], d["ffn1"]
        NFF, glu = 32, False

    c32 = a16.alloc32(NT * 8)
    xfm = [a16.alloc(4096) for _ in range(2)]
    a16.push()
    rw = a16.alloc(8 * 8)
    if which == "G":
        for t in range(8):
            p.dma("gpsimd", rw[:, t * 8:(t + 1) * 8],
                  d["L1_attn_W_router"][t * 128:(t + 1) * 128, :], qg)
    else:
        gwrow = a16.alloc(1024)
        p.dma("sync", gwrow[0:8, :], d["gw_16"][:, :], qs)
        p.wait("tensor", qs, qs.n)
        for t in range(8):
            p.trans(mk.PT[0:128, 0:8], gwrow[0:8, t * 128:(t + 1) * 128],
                    C["ident16"][0:8, 0:8], ctr=pe)
            p.wait("vector", pe, pe.n)
            p.v("tensor_copy", rw[:, t * 8:(t + 1) * 8], mk.PT[0:128, 0:8], ctr=dve)
            p.wait("tensor", dve, dve.n)
    rw_at = (qg.n, qs.n)
    m8 = a16.alloc32(8)
    lgt = a16.alloc32(8)
    wv = a16.alloc32(4)
    ctmp = a16.alloc32(8)
    bR = PS[:, 0:512]

    def load_xfm(c, buf):
        for t in range(8):
            p.dma("sync", buf[:, t * 512:(t + 1) * 512],
                  src16[c * 512:(c + 1) * 512, t * 128:(t + 1) * 128],
                  qs, transpose=True)
        return qs.n

    xfm_free = [0, 0]
    for c in range(NC5):
        buf = xfm[c % 2]
        p.wait("sync", pe, xfm_free[c % 2])
        ld = load_xfm(c, buf)
        p.wait("tensor", qs, ld)
        if c == 0:
            p.wait("tensor", qg, rw_at[0])
            p.wait("tensor", qs, rw_at[1])
        for ts in range(4):
            it = c * 4 + ts
            p.wait("tensor", dve, dve.n)
            for t in range(8):
                p.mm(bR[:, 0:8], buf[:, t * 512 + ts * 128: t * 512 + (ts + 1) * 128],
                     rw[:, t * 8:(t + 1) * 8], start=(t == 0), stop=(t == 7),
                     ctr=pe if t == 7 else None)
            p.wait("vector", pe, pe.n)
            p.v("tensor_copy", lgt, bR[:, 0:8], edge=True)
            p.v("max", m8, lgt, edge=True)
            p.v("tensor_sub", wv[:, 0:1], m8[:, 0:1], m8[:, 1:2], ctr=dve)
            p.wait("scalar", dve, dve.n)
            p.act(wv[:, 1:2], wv[:, 0:1], AF.Sigmoid, ctr=act)
            p.wait("vector", act, act.n)
            p.v("tensor_scalar", wv[:, 2:3], wv[:, 1:2], -1.0, 1.0, ALU.mult,
                ALU.add, edge=True)
            p.v("tensor_scalar", ctmp, lgt, m8[:, 0:1], wv[:, 1:2],
                ALU.is_equal, ALU.mult, edge=True)
            p.v("tensor_scalar", c32[:, it * 8:(it + 1) * 8], lgt, m8[:, 1:2],
                wv[:, 2:3], ALU.is_equal, ALU.mult, edge=True)
            p.v("tensor_add", c32[:, it * 8:(it + 1) * 8],
                c32[:, it * 8:(it + 1) * 8], ctmp, ctr=dve)
        xfm_free[c % 2] = pe.n
    routing_done = p.sems["dve"].n
    a16.pop()
    p.wait("gpsimd", pe, pe.n)
    p.wait("gpsimd", dve, dve.n)
    p.wait("gpsimd", act, act.n)
    p.wait("sync", dve, dve.n)

    if which == "G":
        w1e = a16.alloc(8 * 512)
        wue = a16.alloc(8 * 512)
        w2e = a16.alloc(4 * 1024)
    else:
        w1e = a16.alloc(8 * 4096)
        w2e = a16.alloc(32 * 1024)
    he = a16.alloc(NFF * 512)
    hetmp = a16.alloc(512)
    ysb = a16.alloc32(4 * 512)
    bank_he = [PS[:, 2048:2560], PS[:, 2560:3072]]
    bank_y = [PS[:, 0:512], PS[:, 512:1024], PS[:, 1024:1536], PS[:, 1536:2048]]

    w_free = 0
    w1_free = 0
    acc_order = {}
    ysb_free = {}
    xfm_ld = [0, 0]
    elist = list(range(8)) + (["sh"] if which == "H" else [])
    for e in elist:
        p.wait("gpsimd", pe, w1_free)
        p.wait("sync", pe, w1_free)
        if which == "G":
            for t in range(8):
                p.dma("gpsimd", w1e[:, t * 512:(t + 1) * 512],
                      d["L1_attn_W_gate_e"][e, t * 128:(t + 1) * 128, :], qg)
                p.dma("gpsimd", wue[:, t * 512:(t + 1) * 512],
                      d["L1_attn_W_up_e"][e, t * 128:(t + 1) * 128, :], qg)
            p.wait("gpsimd", pe, w_free)
            for t in range(4):
                p.dma("gpsimd", w2e[:, t * 1024:(t + 1) * 1024],
                      d["L1_attn_W_down_e"][e, t * 128:(t + 1) * 128, :], qg)
            w_at = (qg.n, qs.n)
        else:
            if e == "sh":
                w1src, w2src, ro1, ro2 = d["sw1_16"], d["sw2_16"], 0, 0
            else:
                w1src, w2src = d["ew1_16"], d["ew2_16"]
                ro1, ro2 = e * 4096, e * 1024
            for t in range(8):
                p.dma("sync", w1e[:, t * 4096:(t + 1) * 4096],
                      w1src[ro1:ro1 + 4096, t * 128:(t + 1) * 128], qs,
                      transpose=True)
            p.wait("sync", pe, w_free)
            for t in range(32):
                p.dma("sync", w2e[:, t * 1024:(t + 1) * 1024],
                      w2src[ro2:ro2 + 1024, t * 128:(t + 1) * 128], qs,
                      transpose=True)
            w_at = (qg.n, qs.n)

        for c in range(NC5):
            buf = xfm[c % 2]
            if c == 0:
                p.wait("sync", pe, xfm_free[0])
                xfm_ld[0] = load_xfm(0, buf)
            ld = xfm_ld[c % 2]
            if c + 1 < NC5:
                p.wait("sync", pe, xfm_free[(c + 1) % 2])
                xfm_ld[(c + 1) % 2] = load_xfm(c + 1, xfm[(c + 1) % 2])
            p.wait("tensor", qs, ld)
            if c == 0:
                p.wait("tensor", qg, w_at[0])
                p.wait("tensor", qs, w_at[1])
                if e == 0:
                    p.wait("scalar", dve, routing_done)
                    p.wait("vector", dve, routing_done)
            he_evac = {}
            for ff in range(NFF):
                bnk = bank_he[ff % 2] if not glu else bank_he[0]
                if glu:
                    bup = bank_he[1]
                    if ff >= 1:
                        p.wait("tensor", dve, he_evac[ff - 1])
                    for t in range(8):
                        p.mm(bnk, w1e[:, t * 512 + ff * 128: t * 512 + (ff + 1) * 128],
                             buf[:, t * 512:(t + 1) * 512], start=(t == 0),
                             stop=(t == 7))
                    for t in range(8):
                        p.mm(bup, wue[:, t * 512 + ff * 128: t * 512 + (ff + 1) * 128],
                             buf[:, t * 512:(t + 1) * 512], start=(t == 0),
                             stop=(t == 7), ctr=pe if t == 7 else None)
                    p.wait("scalar", pe, pe.n)
                    p.act(hetmp, bnk, AF.Silu, ctr=act)
                    p.wait("vector", act, act.n)
                    p.v("tensor_mul", he[:, ff * 512:(ff + 1) * 512], hetmp, bup,
                        ctr=dve)
                    he_evac[ff] = dve.n
                else:
                    if ff >= 2:
                        p.wait("tensor", act, he_evac[ff - 2])
                    for t in range(8):
                        p.mm(bnk, w1e[:, t * 4096 + ff * 128: t * 4096 + (ff + 1) * 128],
                             buf[:, t * 512:(t + 1) * 512], start=(t == 0),
                             stop=(t == 7), ctr=pe if t == 7 else None)
                    p.wait("scalar", pe, pe.n)
                    p.act(he[:, ff * 512:(ff + 1) * 512], bnk, AF.Silu, ctr=act)
                    he_evac[ff] = act.n
            he_all = he_evac[NFF - 1]
            if c == NC5 - 1:
                w1_free = pe.n
            for dh in range(2):
                for ts in range(4):
                    it = c * 4 + ts
                    bnk = bank_y[ts]
                    if ts in ysb_free:
                        p.wait("tensor", qg, ysb_free[ts])
                        p.wait("scalar", qg, ysb_free[ts])
                        p.wait("vector", qg, ysb_free[ts])
                    for ff in range(NFF):
                        if ff == 0:
                            if glu:
                                p.wait("tensor", dve, he_all)
                            else:
                                p.wait("tensor", act, he_all)
                        p.mm(bnk, he[:, ff * 512 + ts * 128: ff * 512 + (ts + 1) * 128],
                             w2e[:, ff * 1024 + dh * 512: ff * 1024 + dh * 512 + 512],
                             start=(ff == 0), stop=(ff == NFF - 1),
                             ctr=pe if ff == NFF - 1 else None)
                    mmv = pe.n
                    ysl = ysb[:, ts * 512:(ts + 1) * 512]
                    if e == "sh":
                        p.wait("scalar", pe, mmv)
                        p.act(ysl, bnk, AF.Copy, ctr=act)
                        ev = ("act", act.n)
                    else:
                        csc = c32[:, it * 8 + e: it * 8 + e + 1]
                        if dh == 0:
                            p.wait("scalar", pe, mmv)
                            p.act(ysl, bnk, AF.Copy, scale=csc, ctr=act)
                            ev = ("act", act.n)
                        else:
                            p.wait("vector", pe, mmv)
                            p.v("tensor_scalar_mul", ysl, bnk, csc, ctr=dve)
                            ev = ("dve", dve.n)
                    p.wait("gpsimd", p.sems[ev[0]], ev[1])
                    if (it, dh) in acc_order:
                        p.wait("gpsimd", qg, acc_order[(it, dh)])
                    dst = acc_h[it * 128:(it + 1) * 128, dh * 512:(dh + 1) * 512]
                    p.dma("gpsimd", dst, ysl, qg, accum=(e != 0))
                    acc_order[(it, dh)] = qg.n
                    ysb_free[ts] = qg.n
            xfm_free[c % 2] = pe.n
        w_free = pe.n
    a16.pop()


def phase_add(mk, p, a16, out_name, a_name, b_name):
    d = mk.d
    qs, dve = p.sems["qs"], p.sems["dve"]
    a16.push()
    entry_guard(p)
    a_r = d[a_name].rearrange("(n p) d -> n p d", p=128)
    b_r = d[b_name].rearrange("(n p) d -> n p d", p=128)
    o_r = d[out_name].rearrange("(n p) d -> n p d", p=128)
    ab = [a16.alloc32(D) for _ in range(2)]
    bb = [a16.alloc32(D) for _ in range(2)]
    free3 = {}
    for i in range(NT):
        bi = i % 2
        if i >= 2:
            p.wait("sync", dve, free3[i - 2][0])
            p.wait("sync", qs, free3[i - 2][1])
        p.dma("sync", ab[bi], a_r[i], qs)
        p.dma("sync", bb[bi], b_r[i], qs)
        ld = qs.n
        p.wait("vector", qs, ld)
        p.v("tensor_add", ab[bi], ab[bi], bb[bi], ctr=dve)
        dn = dve.n
        p.wait("sync", dve, dn)
        p.dma("sync", o_r[i], ab[bi], qs)
        free3[i] = (dn, qs.n)
    a16.pop()


def phase_combine2(mk, p, a16, out_name, x_name, s_name, f_name):
    d = mk.d
    qs, dve, act = p.sems["qs"], p.sems["dve"], p.sems["act"]
    a16.push()
    entry_guard(p)
    x_r = d[x_name].rearrange("(n p) d -> n p d", p=128)
    s_r = d[s_name].rearrange("(n p) d -> n p d", p=128)
    f_r = d[f_name].rearrange("(n p) d -> n p d", p=128)
    o_r = d[out_name].rearrange("(n p) d -> n p d", p=128)
    xb = [a16.alloc32(D) for _ in range(2)]
    sb_ = [a16.alloc32(D) for _ in range(2)]
    fb = [a16.alloc32(D) for _ in range(2)]
    gb = [a16.alloc32(D) for _ in range(2)]
    free3 = {}
    for i in range(NT):
        bi = i % 2
        if i >= 2:
            p.wait("sync", dve, free3[i - 2][0])
            p.wait("sync", qs, free3[i - 2][1])
        p.dma("sync", xb[bi], x_r[i], qs)
        p.dma("sync", sb_[bi], s_r[i], qs)
        p.dma("sync", fb[bi], f_r[i], qs)
        ld = qs.n
        p.wait("scalar", qs, ld)
        p.act(gb[bi], fb[bi], AF.Silu, ctr=act)
        p.wait("vector", act, act.n)
        p.v("tensor_mul", gb[bi], gb[bi], sb_[bi], edge=True)
        p.v("tensor_add", gb[bi], gb[bi], xb[bi], ctr=dve)
        dn = dve.n
        p.wait("sync", dve, dn)
        p.dma("sync", o_r[i], gb[bi], qs)
        free3[i] = (dn, qs.n)
    a16.pop()


def kernel(**inputs):
    nc = bass.Bass("TRN2", target_bir_lowering=False)
    m = MK(nc, phases=("A", "B", "C", "D", "E", "F", "G", "H"))
    nc2 = m.build()
    in_maps = [make_inputs_per_core(inputs, b) for b in range(B)]
    res = run_bass_kernel_spmd(nc2, in_maps, core_ids=list(range(B)))
    return np.stack([res.results[b]["out"] for b in range(B)], axis=0)


# version 06
# speedup vs baseline: 1.0713x; 1.0452x over previous
import numpy as np
import concourse.bass as bass
import concourse.mybir as mybir
from concourse.bass_utils import run_bass_kernel_spmd

F32 = mybir.dt.float32
BF16 = mybir.dt.bfloat16
AF = mybir.ActivationFunctionType
ALU = mybir.AluOpType

import os
B, S, D = 8, int(os.environ.get("MK_S", "4096")), 1024
NT = S // 128
NC5 = S // 512

def host_constants():
    c = {}
    c["ident16"] = np.eye(128, dtype=np.float32)
    c["ident32"] = np.eye(128, dtype=np.float32)
    c["ones16"] = np.ones((128, 128), dtype=np.float32)
    c["ones32"] = np.ones((128, 128), dtype=np.float32)
    c["cum_ut32"] = np.triu(np.ones((128, 128), dtype=np.float32))
    for tag, base in (("g", 1e6), ("l", 1e4)):
        inv = 1.0 / base ** (np.arange(0, 64, 2, dtype=np.float32) / 64)
        fr = np.arange(S, dtype=np.float32)[:, None] * inv[None, :]
        cos, sin = np.cos(fr), np.sin(fr)
        Ct = np.zeros((S, 64), np.float32); St = np.zeros((S, 64), np.float32)
        Ct[:, 0::2] = cos; Ct[:, 1::2] = cos
        St[:, 0::2] = -sin; St[:, 1::2] = sin
        c["rope_c_" + tag] = Ct
        c["rope_s_" + tag] = St
    kr = np.arange(128)[:, None]; qf = np.arange(512)[None, :]
    mg = np.stack([(qf - kr >= o * 128) for o in range(4)], 0).astype(np.float32)
    c["maskg"] = mg.transpose(1, 0, 2).reshape(128, 4 * 512)
    ml = []
    for o in range(-3, 3):
        diff = qf - kr + o * 128
        ml.append(((diff >= 0) & (diff < 256)).astype(np.float32))
    c["maskl"] = np.stack(ml, 0).transpose(1, 0, 2).reshape(128, 6 * 512)
    fc = np.zeros((128, 16), np.float32)
    for i, v in enumerate(FCONST_VALS):
        fc[:, i] = v
    c["fconst32"] = fc
    idx = np.arange(128)
    c["m_lows32"] = (idx[:, None] > idx[None, :]).astype(np.float32)
    c["m_upi32"] = (idx[:, None] <= idx[None, :]).astype(np.float32)
    c["m_ups32"] = (idx[:, None] < idx[None, :]).astype(np.float32)
    return c

FCONST_VALS = [-8.0, 1e-6, 1.0, 1e-5, 0.5, -0.5, 2.0, -1.0, -2.4260151319598084]

CONST_SPECS = {
    "ident16": BF16, "ident32": F32, "ones16": BF16, "ones32": F32,
    "cum_ut32": F32, "fconst32": F32,
    "maskg": BF16, "maskl": BF16,
    "m_lows32": F32, "m_upi32": F32, "m_ups32": F32,
}
GLOBAL_CONSTS = ("ident16", "ident32", "ones16", "ones32", "cum_ut32", "fconst32", "m_lows32", "m_upi32", "m_ups32")
HBM_CONSTS = ["rope_c_g", "rope_s_g", "rope_c_l", "rope_s_l"]

HOSTC = host_constants()

class Ctr:
    __slots__ = ("name", "sem", "n", "step")
    def __init__(self, name, sem, step=1):
        self.name, self.sem, self.n, self.step = name, sem, 0, step

class Arena:
    def __init__(self, t, n):
        self.t, self.n, self.off, self.marks = t, n, 0, []
    def alloc(self, cols):
        cols = (cols + 15) // 16 * 16
        assert self.off + cols <= self.n, f"arena overflow {self.off}+{cols}>{self.n}"
        ap = self.t[:, self.off:self.off + cols]
        self.off += cols
        return ap
    def alloc32(self, cols):
        return self.alloc(2 * cols).bitcast(F32)
    def push(self): self.marks.append(self.off)
    def pop(self): self.off = self.marks.pop()

class P:
    def __init__(self, nc, cur, engines, sems):
        self.nc, self.cur, self.E, self.sems = nc, cur, engines, sems

    def wait(self, who, ctr, v=None):
        v = ctr.n if v is None else v
        if v > 0 and self.cur == who:
            self.E[who].wait_ge(ctr.sem, v)

    def _fin(self, who, instr, ctr):
        if ctr is not None:
            ctr.n += ctr.step
            if self.cur == who and instr is not None:
                instr.then_inc(ctr.sem, ctr.step)

    def dma(self, q, out, in_, ctr=None, transpose=False, accum=False, slow=False):
        i = None
        if self.cur == q:
            kw = {"accum_op": ALU.add} if accum else {}
            if slow:
                kw["allow_slow_non_contiguous"] = True
            i = self.E[q].dma_start(out=out, in_=in_, transpose=transpose, **kw)
        self._fin(q, i, ctr)

    def mm(self, out, lhsT, rhs, start, stop, ctr=None):
        i = None
        if self.cur == "tensor":
            i = self.E["tensor"].matmul(out, lhsT, rhs, start=start, stop=stop)
        self._fin("tensor", i, ctr)

    def trans(self, out, in_, ident, ctr=None):
        i = None
        if self.cur == "tensor":
            i = self.E["tensor"].transpose(out, in_, ident)
        self._fin("tensor", i, ctr)

    def act(self, out, in_, func, bias=0.0, scale=1.0, accum_out=None, ctr=None,
            edge=False):
        i = None
        if self.cur == "scalar":
            i = self.E["scalar"].activation(out, in_, func, bias=bias, scale=scale,
                                            accum_out=accum_out)
        c = ctr if ctr is not None else (self.sems["act"] if edge else None)
        self._fin("scalar", i, c)
        if edge:
            self.wait("scalar", self.sems["act"])

    def ve(self, eng, meth, *args, ctr=None, edge=False, **kw):
        i = None
        if self.cur == eng:
            i = getattr(self.E[eng], meth)(*args, **kw)
        nm = "dve" if eng == "vector" else "pool"
        c = ctr if ctr is not None else (self.sems[nm] if edge else None)
        self._fin(eng, i, c)
        if edge:
            self.wait(eng, self.sems[nm])

    def v(self, meth, *args, ctr=None, edge=False, **kw):
        self.ve("vector", meth, *args, ctr=ctr, edge=edge, **kw)

    def g(self, meth, *args, ctr=None, edge=False, **kw):
        self.ve("gpsimd", meth, *args, ctr=ctr, edge=edge, **kw)

    def barrier(self):
        for e in ("sync", "gpsimd", "tensor", "vector", "scalar"):
            if self.cur == e:
                for c in self.sems.values():
                    if c.n > 0:
                        self.E[e].wait_ge(c.sem, c.n)


def ev(ap):
    return ap.rearrange("p (a b) -> p a b", b=2)[:, :, 0:1]

def od(ap):
    return ap.rearrange("p (a b) -> p a b", b=2)[:, :, 1:2]


class MK:
    def __init__(self, nc, phases, dbg=()):
        self.nc, self.phases = nc, phases
        self.dbg = set(dbg)
        self.d = {}
        self.declare_dram()

    def declare_dram(self):
        nc, d = self.nc, self.d
        d["x0"] = nc.dram_tensor("x0", [S, D], F32, kind="ExternalInput")
        WSPECS = [
            ("L0_attn_norm1_w", [D]), ("L0_attn_norm2_w", [D]),
            ("L0_attn_g_wq", [D, D]), ("L0_attn_g_kv_down", [D, 256]),
            ("L0_attn_g_k_up", [256, 256]), ("L0_attn_g_v_up", [256, 256]),
            ("L0_attn_g_wo", [D, D]),
            ("L0_attn_l_wq", [D, D]), ("L0_attn_l_wk", [D, 256]),
            ("L0_attn_l_wv", [D, 256]), ("L0_attn_l_wo", [D, D]),
            ("L0_ffn_router_w", [D, 8]), ("L0_ffn_router_b", [8]),
            ("L0_ffn_W1", [8, D, 4096]), ("L0_ffn_b1", [8, 4096]),
            ("L0_ffn_W2", [8, 4096, D]), ("L0_ffn_b2", [8, D]),
            ("L1_attn_w_ln1", [D]), ("L1_attn_w_ln2", [D]),
            ("L1_attn_W_qkvz", [D, 3072]), ("L1_attn_W_ba", [D, 16]),
            ("L1_attn_conv_w", [2048, 4]), ("L1_attn_dt_bias", [8]),
            ("L1_attn_A_log", [8]), ("L1_attn_gate_norm_w", [128]),
            ("L1_attn_W_out", [D, D]), ("L1_attn_W_router", [D, 8]),
            ("L1_attn_W_gate_e", [8, D, 512]), ("L1_attn_W_up_e", [8, D, 512]),
            ("L1_attn_W_down_e", [8, 512, D]),
            ("L1_ffn_shared_w1", [4096, D]), ("L1_ffn_shared_w2", [D, 4096]),
            ("L1_ffn_experts_w1", [8, 4096, D]), ("L1_ffn_experts_w2", [8, D, 4096]),
            ("L1_ffn_gate_w", [8, D]),
        ]
        for name, shape in WSPECS:
            d[name] = nc.dram_tensor(name, shape, F32, kind="ExternalInput")
        for name, dt in CONST_SPECS.items():
            d[name] = nc.dram_tensor(name, list(HOSTC[name].shape), dt,
                                     kind="ExternalInput")
        for name in HBM_CONSTS:
            d[name] = nc.dram_tensor(name, list(HOSTC[name].shape), BF16,
                                     kind="ExternalInput")
        scr = [
            ("xfmdbg", [128, 4096], BF16),
            ("routedbg", [128, 64], F32),
            ("c32dbg", [128, NT * 8], F32), ("hedbg", [128, 32 * 512], BF16),
            ("xn32", [S, D], F32), ("xn16", [S, D], BF16), ("a016", [S, D], BF16),
            ("x132", [S, D], F32), ("a116", [S, D], BF16), ("s32", [S, D], F32),
            ("ffn0", [S, D], F32), ("X132", [S, D], F32),
            ("xn1_32", [S, D], F32), ("xn1_16", [S, D], BF16), ("h016", [S, D], BF16),
            ("x232", [S, D], F32), ("xh16", [S, D], BF16),
            ("moe1", [S, D], F32), ("ffn1", [S, D], F32), ("s132", [S, D], F32),
            ("sw1_16", [4096, D], BF16), ("sw2_16", [D, 4096], BF16),
            ("ew1_16", [8 * 4096, D], BF16), ("ew2_16", [8 * D, 4096], BF16),
            ("gw_16", [8, D], BF16),
        ]
        for name, shape, dt in scr:
            kind = "ExternalOutput" if name in self.dbg else "Internal"
            d[name] = nc.dram_tensor(name, shape, dt, kind=kind)
        d["out"] = nc.dram_tensor("out", [S, D], F32, kind="ExternalOutput")

    def build(self):
        nc = self.nc
        with (
            nc.sbuf_tensor("A16", [128, 99328], BF16) as A16,
            nc.sbuf_tensor("CB", [128, 2048], BF16) as CB,
            nc.psum_tensor("PS", [128, 3072], F32) as PS,
            nc.psum_tensor("PT", [128, 2048], BF16) as PT,
            nc.semaphore("c_pe") as s_pe,
            nc.semaphore("c_act") as s_act,
            nc.semaphore("c_dve") as s_dve,
            nc.semaphore("c_pool") as s_pool,
            nc.semaphore("q_sync") as s_qs,
            nc.semaphore("q_scal") as s_qc,
            nc.semaphore("q_gps") as s_qg,
            nc.Block() as block,
        ):
            self.A16, self.CB, self.PS, self.PT = A16, CB, PS, PT
            mk = self

            def run(cur, eng):
                sems = {
                    "pe": Ctr("pe", s_pe), "act": Ctr("act", s_act),
                    "dve": Ctr("dve", s_dve), "pool": Ctr("pool", s_pool),
                    "qs": Ctr("qs", s_qs, 16), "qc": Ctr("qc", s_qc, 16),
                    "qg": Ctr("qg", s_qg, 16),
                }
                emit_all(mk, P(self.nc, cur, {cur: eng}, sems))

            @block.sync
            def _(eng): run("sync", eng)
            @block.gpsimd
            def _(eng): run("gpsimd", eng)
            @block.tensor
            def _(eng): run("tensor", eng)
            @block.vector
            def _(eng): run("vector", eng)
            @block.scalar
            def _(eng): run("scalar", eng)
        return nc



def emit_rsqrt(p, dst, srcv, pre_scale=1.0, pre_bias=0.0, post_scale=1.0):
    import math
    act = p.sems["act"]
    p.act(dst, srcv, AF.Ln, scale=pre_scale, bias=pre_bias, edge=True)
    p.act(dst, dst, AF.Exp, scale=-0.5,
          bias=(0.0 if post_scale == 1.0 else math.log(post_scale)), ctr=act)

def entry_guard(p, queues=("sync", "gpsimd")):
    for q in queues:
        for cn in ("pe", "act", "dve", "pool", "qs", "qc", "qg"):
            p.wait(q, p.sems[cn])


def load_const(mk, p, ar, name):
    dt = CONST_SPECS[name]
    w = HOSTC[name].shape[1]
    buf = ar.alloc32(w) if dt == F32 else ar.alloc(w)
    p.dma("sync", buf, mk.d[name][:, :], p.sems["qs"])
    return buf

def load_row_bcast(mk, p, ar, src_ap, w):
    buf = ar.alloc32(w)
    p.dma("sync", buf, src_ap[None, :].to_broadcast([128, w]), p.sems["qs"])
    return buf

def emit_all(mk, p):
    a16 = Arena(mk.A16, mk.A16.shape[1])
    cst = Arena(mk.CB, mk.CB.shape[1])
    qs = p.sems["qs"]
    C = {name: load_const(mk, p, cst, name) for name in GLOBAL_CONSTS}
    C["consts_at"] = qs.n
    mk.C = C
    if not getattr(mk, "fconst_registered", False):
        for i, v in enumerate(FCONST_VALS):
            mk.nc.const_aps.register(F32, v, C["fconst32"][:, i:i + 1])
        mk.fconst_registered = True
    for e in ("tensor", "vector", "scalar", "gpsimd"):
        p.wait(e, qs, C["consts_at"])
    if "A" in mk.phases:
        phase_A(mk, p, a16)
    if "B" in mk.phases:
        phase_attn(mk, p, a16, "g")
    if "C" in mk.phases:
        phase_attn(mk, p, a16, "l")
    if "H" in mk.phases:
        phase_prepass(mk, p, a16)
    if "D" in mk.phases:
        phase_moe(mk, p, a16, "L0")
        phase_combine2(mk, p, a16, "X132", "x0", "s32", "ffn0")
    if "E" in mk.phases:
        phase_E(mk, p, a16)
    if "F" in mk.phases:
        phase_F(mk, p, a16)
    if "G" in mk.phases:
        phase_moe2(mk, p, a16, "G")
        phase_add(mk, p, a16, "s132", "x232", "moe1")
    if "H" in mk.phases:
        phase_moe2(mk, p, a16, "H")
        phase_combine2(mk, p, a16, "out", "X132", "s132", "ffn1")
    p.barrier()


def phase_A(mk, p, a16):
    d, C = mk.d, mk.C
    qs, dve, act = p.sems["qs"], p.sems["dve"], p.sems["act"]
    a16.push()
    n1bc = load_row_bcast(mk, p, a16, d["L0_attn_norm1_w"], D)
    xin = [a16.alloc32(D) for _ in range(2)]
    st8 = [a16.alloc32(12) for _ in range(2)]
    mv = [a16.alloc32(4) for _ in range(2)]
    xn32t = [a16.alloc32(D) for _ in range(2)]
    xn16t = [a16.alloc(D) for _ in range(2)]
    a016t = [a16.alloc(D) for _ in range(2)]

    x0r = d["x0"].rearrange("(n p) d -> n p d", p=128)
    xn32r = d["xn32"].rearrange("(n p) d -> n p d", p=128)
    xn16r = d["xn16"].rearrange("(n p) d -> n p d", p=128)
    a016r = d["a016"].rearrange("(n p) d -> n p d", p=128)

    free_at = {}
    p.wait("sync", qs, qs.n)
    for i in range(NT):
        if i == 0:
            p.wait("tensor", dve, dve.n)
            p.wait("tensor", act, act.n)
        p.mm(mk.PS[0:64, 0:64], C["ident16"][:, 0:64], C["ident16"][:, 0:64],
             start=True, stop=True)
        bi = i % 2
        if i >= 2:
            p.wait("sync", dve, free_at[i - 2][0])
            p.wait("sync", qs, free_at[i - 2][1])
        p.dma("sync", xin[bi], x0r[i], qs)
        ld = qs.n

        p.wait("vector", qs, ld)
        p.v("bn_stats", st8[bi][:, 0:6], xin[bi][:, 0:512], edge=True)
        p.v("bn_stats", st8[bi][:, 6:12], xin[bi][:, 512:1024], edge=True)
        p.v("bn_aggr", mv[bi][:, 0:2], st8[bi][:, 0:12], edge=True)
        p.wait("scalar", dve, dve.n)
        emit_rsqrt(p, mv[bi][:, 2:3], mv[bi][:, 1:2], pre_bias=1e-5)
        rstd_done = act.n
        p.v("tensor_scalar_add", mv[bi][:, 3:4], mv[bi][:, 1:2], 1e-5, edge=True)
        p.v("reciprocal", mv[bi][:, 3:4], mv[bi][:, 3:4], edge=True)
        p.v("tensor_mul", mv[bi][:, 3:4], mv[bi][:, 3:4], mv[bi][:, 1:2], edge=True)
        p.v("tensor_scalar_add", mv[bi][:, 3:4], mv[bi][:, 3:4], 1e-6, ctr=dve)
        p.wait("scalar", dve, dve.n)
        emit_rsqrt(p, mv[bi][:, 3:4], mv[bi][:, 3:4])
        rms2_done = act.n

        p.wait("vector", act, rstd_done)
        p.v("tensor_scalar", xn32t[bi], xin[bi], mv[bi][:, 0:1], mv[bi][:, 2:3],
            ALU.subtract, ALU.mult, edge=True)
        p.v("tensor_copy", xn16t[bi], xn32t[bi], ctr=dve)
        xn_done = dve.n
        p.wait("vector", act, rms2_done)
        p.v("scalar_tensor_tensor", a016t[bi], xn32t[bi], mv[bi][:, 3:4], n1bc,
            ALU.mult, ALU.mult, ctr=dve)
        a0_done = dve.n

        p.wait("sync", dve, xn_done)
        p.dma("sync", xn32r[i], xn32t[bi], qs)
        p.dma("sync", xn16r[i], xn16t[bi], qs)
        p.wait("sync", dve, a0_done)
        p.dma("sync", a016r[i], a016t[bi], qs)
        free_at[i] = (dve.n, qs.n)
    a16.pop()


def phase_attn(mk, p, a16, kind):
    d, C = mk.d, mk.C
    qs, qg = p.sems["qs"], p.sems["qg"]
    pe, act, dve = p.sems["pe"], p.sems["act"], p.sems["dve"]
    PS, PT = mk.PS, mk.PT
    a16.push()
    entry_guard(p)

    glob = kind == "g"
    src16 = d["a016"] if glob else d["a116"]
    wq = d["L0_attn_g_wq"] if glob else d["L0_attn_l_wq"]
    wo = d["L0_attn_g_wo"] if glob else d["L0_attn_l_wo"]

    wq_sb = a16.alloc(8 * D)
    for t in range(8):
        p.dma("gpsimd", wq_sb[:, t * D:(t + 1) * D], wq[t * 128:(t + 1) * 128, :], qg)
    if glob:
        kvd_sb = a16.alloc(8 * 256)
        ku_sb = a16.alloc(2 * 256)
        vu_sb = a16.alloc(2 * 256)
        for t in range(8):
            p.dma("gpsimd", kvd_sb[:, t * 256:(t + 1) * 256],
                  d["L0_attn_g_kv_down"][t * 128:(t + 1) * 128, :], qg)
        for t in range(2):
            p.dma("gpsimd", ku_sb[:, t * 256:(t + 1) * 256],
                  d["L0_attn_g_k_up"][t * 128:(t + 1) * 128, :], qg)
            p.dma("gpsimd", vu_sb[:, t * 256:(t + 1) * 256],
                  d["L0_attn_g_v_up"][t * 128:(t + 1) * 128, :], qg)
    else:
        wk_sb = a16.alloc(8 * 256)
        wv_sb = a16.alloc(8 * 256)
        for t in range(8):
            p.dma("gpsimd", wk_sb[:, t * 256:(t + 1) * 256],
                  d["L0_attn_l_wk"][t * 128:(t + 1) * 128, :], qg)
            p.dma("gpsimd", wv_sb[:, t * 256:(t + 1) * 256],
                  d["L0_attn_l_wv"][t * 128:(t + 1) * 128, :], qg)
    w_at = qg.n

    mtag = "maskg" if glob else "maskl"
    msk = a16.alloc(HOSTC[mtag].shape[1])
    p.dma("sync", msk, d[mtag][:, :], qs)
    rc = a16.alloc(NT * 64)
    rs = a16.alloc(NT * 64)
    p.dma("sync", rc.rearrange("p (n f) -> p n f", f=64),
          d["rope_c_" + kind].rearrange("(n p) f -> p n f", p=128), qs)
    p.dma("sync", rs.rearrange("p (n f) -> p n f", f=64),
          d["rope_s_" + kind].rearrange("(n p) f -> p n f", p=128), qs)
    const_at = qs.n

    afm = [a16.alloc(4096) for _ in range(2)]
    attn = a16.alloc(8 * S)
    a16.push()
    if glob:
        lat = a16.alloc(2 * S)
    qraw = a16.alloc(2 * S)
    qfmn = a16.alloc(2 * S)
    kvraw = a16.alloc(S)
    kfmn = a16.alloc(S)
    vaug = a16.alloc(NT * 66 + 16)
    tbuf = a16.alloc(128)
    rtmp = a16.alloc(512)
    rtmp4 = a16.alloc(512)
    rtmpq = a16.alloc(512)
    kvb = a16.alloc(512)
    q0b = a16.alloc(512)
    q1b = a16.alloc(512)
    nrm16 = a16.alloc32(20)
    nrm16b = a16.alloc32(20)
    nrm = a16.alloc32(8)
    prow = a16.alloc32(512)
    psb2 = a16.alloc(1024)
    usb = a16.alloc(512)

    bP0, bP1 = PS[:, 0:512], PS[:, 512:1024]
    bS0, bS1 = PS[:, 1024:1536], PS[:, 1536:2048]
    bU = PS[:, 2048:2560]
    bB = PS[:, 2560:3072]
    tp = [PT[:, 0:1024], PT[:, 1024:2048]]
    vaug_t = vaug[:, 0:NT * 66].rearrange("p (n w) -> p n w", w=66)

    def load_afm(c, buf):
        for t in range(8):
            p.dma("sync", buf[:, t * 512:(t + 1) * 512],
                  src16[c * 512:(c + 1) * 512, t * 128:(t + 1) * 128],
                  qs, transpose=True)
        return qs.n

    p.wait("sync", qs, const_at)

    for g in range(4):
        evac = {"P0": 0, "P1": 0}
        afm_read = [0, 0]
        for c in range(NC5):
            buf = afm[c % 2]
            p.wait("sync", pe, afm_read[c % 2])
            ld = load_afm(c, buf)
            p.wait("tensor", qs, ld)
            if c == 0 and g == 0:
                p.wait("tensor", qg, w_at)
            for ct in range(2):
                col0 = g * 256 + ct * 128
                bnk, key = (bP0, "P0") if ct == 0 else (bP1, "P1")
                p.wait("tensor", dve, evac[key])
                for t in range(8):
                    p.mm(bnk, wq_sb[:, t * D + col0: t * D + col0 + 128],
                         buf[:, t * 512:(t + 1) * 512], start=(t == 0), stop=(t == 7),
                         ctr=pe if t == 7 else None)
                p.wait("vector", pe, pe.n)
                p.v("tensor_copy", qraw[:, ct * S + c * 512: ct * S + c * 512 + 512],
                    bnk, ctr=dve)
                evac[key] = dve.n
            if glob and g == 0:
                for ct in range(2):
                    bnk, key = (bP0, "P0") if ct == 0 else (bP1, "P1")
                    p.wait("tensor", dve, evac[key])
                    for t in range(8):
                        p.mm(bnk, kvd_sb[:, t * 256 + ct * 128: t * 256 + ct * 128 + 128],
                             buf[:, t * 512:(t + 1) * 512], start=(t == 0), stop=(t == 7),
                             ctr=pe if t == 7 else None)
                    p.wait("vector", pe, pe.n)
                    p.v("tensor_copy", lat[:, ct * S + c * 512: ct * S + c * 512 + 512],
                        bnk, ctr=dve)
                    evac[key] = dve.n
            afm_read[c % 2] = pe.n

        for c in range(NC5):
            p.wait("tensor", dve, max(evac["P0"], evac["P1"]))
            if glob:
                for t in range(2):
                    p.mm(bP0[0:64, :], ku_sb[:, t * 256 + g * 64: t * 256 + g * 64 + 64],
                         lat[:, t * S + c * 512: t * S + c * 512 + 512],
                         start=(t == 0), stop=(t == 1))
                for t in range(2):
                    p.mm(bP0[64:128, :], vu_sb[:, t * 256 + g * 64: t * 256 + g * 64 + 64],
                         lat[:, t * S + c * 512: t * S + c * 512 + 512],
                         start=(t == 0), stop=(t == 1), ctr=pe if t == 1 else None)
            else:
                buf = afm[c % 2]
                p.wait("sync", pe, pe.n)
                ld = load_afm(c, buf)
                p.wait("tensor", qs, ld)
                for t in range(8):
                    p.mm(bP0[0:64, :], wk_sb[:, t * 256 + g * 64: t * 256 + g * 64 + 64],
                         buf[:, t * 512:(t + 1) * 512], start=(t == 0), stop=(t == 7))
                for t in range(8):
                    p.mm(bP0[64:128, :], wv_sb[:, t * 256 + g * 64: t * 256 + g * 64 + 64],
                         buf[:, t * 512:(t + 1) * 512], start=(t == 0), stop=(t == 7),
                         ctr=pe if t == 7 else None)
            p.wait("vector", pe, pe.n)
            p.v("tensor_copy", kvraw[:, c * 512:(c + 1) * 512], bP0, ctr=dve)
            evac["P0"] = dve.n

        p.v("memset", vaug_t[:, :, 64:65], 1.0, ctr=dve)
        p.wait("tensor", dve, dve.n)

        for scn in range(NT // 4):
            i0 = scn * 4
            p.wait("tensor", dve, dve.n)
            for j in range(4):
                p.trans(tp[0][:, j * 128:(j + 1) * 128],
                        kvraw[:, (i0 + j) * 128:(i0 + j + 1) * 128], C["ident16"])
            for j in range(4):
                p.trans(tp[0][:, 512 + j * 128:512 + (j + 1) * 128],
                        qraw[:, (i0 + j) * 128:(i0 + j + 1) * 128], C["ident16"])
            for j in range(4):
                p.trans(tp[1][:, j * 128:(j + 1) * 128],
                        qraw[:, S + (i0 + j) * 128: S + (i0 + j + 1) * 128],
                        C["ident16"], ctr=pe if j == 3 else None)
            p.wait("vector", pe, pe.n)
            p.v("tensor_copy", kvb, tp[0][:, 0:512], edge=True)
            p.v("tensor_copy", q0b, tp[0][:, 512:1024], edge=True)
            p.v("tensor_copy", q1b, tp[1][:, 0:512], edge=True)
            p.v("tensor_copy",
                vaug[:, i0 * 66:(i0 + 4) * 66].rearrange("p (n w) -> p n w", w=66)[:, :, 0:64],
                kvb.rearrange("p (n w) -> p n w", w=128)[:, :, 64:128], ctr=dve)
            p.wait("vector", dve, dve.n)
            rcs = rc[:, i0 * 64:(i0 + 4) * 64].rearrange("p (n f) -> p n f", f=64)
            rss = rs[:, i0 * 64:(i0 + 4) * 64].rearrange("p (n f) -> p n f", f=64)
            kview = kvb.rearrange("p (n w) -> p n w", w=128)[:, :, 0:64]
            kev = kview.rearrange("p n (a b) -> p n a b", b=2)
            rt4 = rtmp4[:, 0:256].rearrange("p (n f) -> p n f", f=64)
            p.v("tensor_copy", rt4.rearrange("p n (a b) -> p n a b", b=2)[:, :, :, 0:1],
                kev[:, :, :, 1:2], edge=True)
            p.v("tensor_copy", rt4.rearrange("p n (a b) -> p n a b", b=2)[:, :, :, 1:2],
                kev[:, :, :, 0:1], edge=True)
            p.v("tensor_tensor", rt4, rt4, rss, ALU.mult, edge=True)
            p.v("tensor_tensor", kview, kview, rcs, ALU.mult, edge=True)
            p.v("tensor_tensor", kview, kview, rt4, ALU.add, edge=True)
            rc2 = rcs.rearrange("p n (o f) -> p n o f", o=1).to_broadcast([128, 4, 2, 64])
            rs2 = rss.rearrange("p n (o f) -> p n o f", o=1).to_broadcast([128, 4, 2, 64])
            for qb in (q0b, q1b):
                qm = qb.rearrange("p (m f) -> p m f", f=64)
                qv = qb.rearrange("p (n h f) -> p n h f", h=2, f=64)
                rqm = rtmpq.rearrange("p (m f) -> p m f", f=64)
                rqv = rtmpq.rearrange("p (n h f) -> p n h f", h=2, f=64)
                p.v("tensor_copy", rqm.rearrange("p m (a b) -> p m a b", b=2)[:, :, :, 0:1],
                    qm.rearrange("p m (a b) -> p m a b", b=2)[:, :, :, 1:2],
                    edge=True)
                p.v("tensor_copy", rqm.rearrange("p m (a b) -> p m a b", b=2)[:, :, :, 1:2],
                    qm.rearrange("p m (a b) -> p m a b", b=2)[:, :, :, 0:1],
                    edge=True)
                p.v("tensor_tensor", rqv, rqv, rs2, ALU.mult, edge=True)
                p.v("tensor_tensor", qv, qv, rc2, ALU.mult, edge=True)
                p.v("tensor_tensor", qb, qb, rtmpq, ALU.add, edge=True)
            p.wait("scalar", dve, dve.n)
            for j in range(4):
                p.act(rtmp[:, 0:64], kvb[:, j * 128:j * 128 + 64], AF.Square,
                      accum_out=nrm16[:, j:j + 1],
                      ctr=act if j == 3 else None)
            for j in range(4):
                for hh in range(2):
                    p.act(rtmp[:, 0:64], q0b[:, j * 128 + hh * 64:j * 128 + (hh + 1) * 64],
                          AF.Square, accum_out=nrm16[:, 4 + j * 2 + hh:5 + j * 2 + hh],
                          ctr=act if (j == 3 and hh == 1) else None)
            for j in range(4):
                for hh in range(2):
                    p.act(rtmp[:, 0:64], q1b[:, j * 128 + hh * 64:j * 128 + (hh + 1) * 64],
                          AF.Square, accum_out=nrm16[:, 12 + j * 2 + hh:13 + j * 2 + hh],
                          ctr=act)
            emit_rsqrt(p, nrm16b[:, 0:20], nrm16[:, 0:20], pre_scale=1.0 / 64, pre_bias=1e-6)
            p.wait("vector", act, act.n)
            for j in range(4):
                p.v("tensor_scalar_mul", kvb[:, j * 128:j * 128 + 64],
                    kvb[:, j * 128:j * 128 + 64], nrm16b[:, j:j + 1])
            for j in range(4):
                for hh in range(2):
                    p.v("tensor_scalar_mul", q0b[:, j * 128 + hh * 64:j * 128 + (hh + 1) * 64],
                        q0b[:, j * 128 + hh * 64:j * 128 + (hh + 1) * 64],
                        nrm16b[:, 4 + j * 2 + hh:5 + j * 2 + hh])
                    p.v("tensor_scalar_mul", q1b[:, j * 128 + hh * 64:j * 128 + (hh + 1) * 64],
                        q1b[:, j * 128 + hh * 64:j * 128 + (hh + 1) * 64],
                        nrm16b[:, 12 + j * 2 + hh:13 + j * 2 + hh], ctr=dve)
            p.wait("tensor", dve, dve.n)
            for j in range(4):
                p.trans(tp[0][0:64, j * 128:(j + 1) * 128],
                        kvb[:, j * 128:j * 128 + 64], C["ident16"])
                p.trans(tp[0][64:128, 512 + j * 128:512 + (j + 1) * 128],
                        kvb[:, j * 128:j * 128 + 64], C["ident16"])
                p.trans(tp[1][:, j * 128:(j + 1) * 128],
                        q0b[:, j * 128:(j + 1) * 128], C["ident16"])
                p.trans(tp[1][:, 512 + j * 128:512 + (j + 1) * 128],
                        q1b[:, j * 128:(j + 1) * 128], C["ident16"],
                        ctr=pe if j == 3 else None)
            p.wait("vector", pe, pe.n)
            p.v("tensor_copy", kfmn[0:64, i0 * 128:(i0 + 4) * 128], tp[0][0:64, 0:512],
                edge=True)
            p.v("tensor_copy", kfmn[64:128, i0 * 128:(i0 + 4) * 128],
                tp[0][64:128, 512:1024], edge=True)
            p.v("tensor_copy", qfmn[:, i0 * 128:(i0 + 4) * 128], tp[1][:, 0:512],
                edge=True)
            p.v("tensor_copy", qfmn[:, S + i0 * 128: S + (i0 + 4) * 128],
                tp[1][:, 512:1024], ctr=dve)
            p.wait("tensor", dve, dve.n)
        prep_done = dve.n

        p.wait("tensor", dve, prep_done)
        for hl in range(4):
            h = 4 * g + hl
            pt, base = hl // 2, (hl % 2) * 64
            for qc in range(NC5):
                if glob:
                    jlist = list(range(0, 4 * qc + 4))
                else:
                    jlist = [j for j in range(4 * qc - 2, 4 * qc + 4) if j >= 0]
                pairs = [jlist[i:i + 2] for i in range(0, len(jlist), 2)]
                exp_at = {}
                umm_at = {}
                for pi, pj in enumerate(pairs):
                    if pi >= 1:
                        p.wait("tensor", act, exp_at[pi - 1])
                    for jj, j in enumerate(pj):
                        p.mm((bS0, bS1)[jj],
                             kfmn[base:base + 64, j * 128:(j + 1) * 128],
                             qfmn[base:base + 64, pt * S + qc * 512: pt * S + qc * 512 + 512],
                             start=True, stop=True,
                             ctr=pe if jj == len(pj) - 1 else None)
                    p.wait("scalar", pe, pe.n)
                    if pi >= 1:
                        p.wait("scalar", pe, umm_at[pi - 1])
                    wid = 512 * len(pj)
                    p.act(psb2[:, 0:wid], PS[:, 1024:1024 + wid], AF.Exp,
                          scale=0.125, bias=-8.0, ctr=act)
                    exp_at[pi] = act.n
                    anymask = False
                    for jj, j in enumerate(pj):
                        masked = (j - 4 * qc >= 0) if glob else True
                        if masked:
                            mo = (j - 4 * qc) if glob else (4 * qc - j + 3)
                            p.wait("vector", act, act.n)
                            p.v("tensor_mul", psb2[:, jj * 512:(jj + 1) * 512],
                                psb2[:, jj * 512:(jj + 1) * 512],
                                msk[:, mo * 512:(mo + 1) * 512], ctr=dve)
                            anymask = True
                    if anymask:
                        p.wait("tensor", dve, dve.n)
                    else:
                        p.wait("tensor", act, act.n)
                    idx0 = pi * 2
                    for jj, j in enumerate(pj):
                        p.mm(bU[0:65, :], vaug[:, j * 66: j * 66 + 65],
                             psb2[:, jj * 512:(jj + 1) * 512],
                             start=(idx0 + jj == 0),
                             stop=(idx0 + jj == len(jlist) - 1),
                             ctr=pe if jj == len(pj) - 1 else None)
                    umm_at[pi] = pe.n
                p.wait("vector", pe, pe.n)
                p.v("reciprocal", prow[64:65, :], bU[64:65, :], ctr=dve)
                p.wait("tensor", dve, dve.n)
                p.mm(bB[0:64, :], C["ones32"][64:65, 0:64], prow[64:65, :],
                     start=True, stop=True, ctr=pe)
                p.wait("scalar", pe, umm_at[len(pairs) - 1])
                p.act(usb[0:64, :], bU[0:64, :], AF.Identity, ctr=act)
                p.wait("vector", act, act.n)
                p.wait("vector", pe, pe.n)
                th, bh = h // 2, (h % 2) * 64
                if bh == 0:
                    p.v("tensor_mul",
                        attn[0:64, th * S + qc * 512: th * S + qc * 512 + 512],
                        usb[0:64, :], bB[0:64, :], ctr=dve)
                    p.wait("tensor", dve, dve.n)
                else:
                    p.v("tensor_mul", usb[0:64, :], usb[0:64, :], bB[0:64, :],
                        ctr=dve)
                    p.wait("tensor", dve, dve.n)
                    p.mm(bB[64:128, :], C["ident16"][0:64, 0:64], usb[0:64, :],
                         start=True, stop=True, ctr=pe)
                    p.wait("vector", pe, pe.n)
                    p.v("tensor_copy",
                        attn[64:128, th * S + qc * 512: th * S + qc * 512 + 512],
                        bB[64:128, :], ctr=dve)
                    p.wait("tensor", dve, dve.n)

    a16.pop()
    entry_guard(p)
    wo_sb = a16.alloc(8 * D)
    for t in range(8):
        p.dma("gpsimd", wo_sb[:, t * D:(t + 1) * D], wo[t * 128:(t + 1) * 128, :], qg)
    wo_at = qg.n
    if glob:
        n2bc = load_row_bcast(mk, p, a16, d["L0_attn_norm2_w"], D)
    res32 = d["xn32"] if glob else d["x132"]
    out32 = d["x132"] if glob else d["s32"]
    res_r = res32.rearrange("(n p) d -> n p d", p=128)
    out_r = out32.rearrange("(n p) d -> n p d", p=128)
    a116r = d["a116"].rearrange("(n p) d -> n p d", p=128)

    xres = [a16.alloc32(D) for _ in range(2)]
    xout = [a16.alloc32(D) for _ in range(2)]
    a1t = [a16.alloc(D) for _ in range(2)]
    sq = a16.alloc32(D)
    mv2 = a16.alloc32(4)

    p.wait("tensor", qg, wo_at)
    free2 = {}
    for i in range(NT):
        bi = i % 2
        if i >= 2:
            p.wait("sync", dve, free2[i - 2][0])
            p.wait("sync", qs, free2[i - 2][1])
        p.dma("sync", xres[bi], res_r[i], qs)
        ld = qs.n
        for half in range(2):
            bnk = bP0 if half == 0 else bP1
            if i >= 1:
                p.wait("tensor", dve, free2.get((i - 1, half), 0))
            for t in range(8):
                p.mm(bnk, attn[:, t * S + i * 128:(t * S + (i + 1) * 128)],
                     wo_sb[:, t * D + half * 512: t * D + half * 512 + 512],
                     start=(t == 0), stop=(t == 7), ctr=pe if t == 7 else None)
            p.wait("vector", pe, pe.n)
            p.wait("vector", qs, ld)
            p.v("tensor_add", xout[bi][:, half * 512:(half + 1) * 512],
                xres[bi][:, half * 512:(half + 1) * 512], bnk, ctr=dve)
            free2[(i, half)] = dve.n
        xdone = dve.n
        p.wait("sync", dve, xdone)
        p.dma("sync", out_r[i], xout[bi], qs)
        st_at = qs.n
        if glob:
            p.wait("scalar", dve, xdone)
            p.act(sq, xout[bi], AF.Square, accum_out=mv2[:, 0:1], edge=True)
            emit_rsqrt(p, mv2[:, 1:2], mv2[:, 0:1], pre_scale=1.0 / D,
                       pre_bias=1e-6)
            p.wait("vector", act, act.n)
            p.v("scalar_tensor_tensor", a1t[bi], xout[bi], mv2[:, 1:2], n2bc,
                ALU.mult, ALU.mult, ctr=dve)
            p.wait("sync", dve, dve.n)
            p.dma("sync", a116r[i], a1t[bi], qs)
        free2[i] = (dve.n, qs.n)
    a16.pop()


def make_inputs_per_core(inputs, core_b):
    import ml_dtypes
    m = {"x0": np.ascontiguousarray(np.asarray(inputs["L0_attn_hidden"])[core_b],
                                    dtype=np.float32)}
    for k, v in inputs.items():
        if k != "L0_attn_hidden":
            m[k] = np.ascontiguousarray(np.asarray(v, dtype=np.float32))
    for name, dt in CONST_SPECS.items():
        m[name] = (HOSTC[name].astype(ml_dtypes.bfloat16) if dt == BF16
                   else HOSTC[name])
    for name in HBM_CONSTS:
        m[name] = HOSTC[name].astype(ml_dtypes.bfloat16)
    return m


def run_phases(inputs, phases, dbg=(), trace=False):
    nc = bass.Bass("TRN2", target_bir_lowering=False)
    mk = MK(nc, phases, dbg=dbg)
    nc2 = mk.build()
    in_maps = [make_inputs_per_core(inputs, b) for b in range(B)]
    res = run_bass_kernel_spmd(nc2, in_maps, core_ids=list(range(B)), trace=trace)
    return res


def phase_moe(mk, p, a16, which):
    d, C = mk.d, mk.C
    qs, qg = p.sems["qs"], p.sems["qg"]
    pe, act, dve = p.sems["pe"], p.sems["act"], p.sems["dve"]
    PS = mk.PS
    a16.push()
    entry_guard(p)

    src16 = d["xn16"]
    Wr, br = d["L0_ffn_router_w"], d["L0_ffn_router_b"]
    W1, W2 = d["L0_ffn_W1"], d["L0_ffn_W2"]
    b1, b2 = d["L0_ffn_b1"], d["L0_ffn_b2"]
    acc_h = d["ffn0"]

    c32 = a16.alloc32(NT * 8)
    xfm = [a16.alloc(4096) for _ in range(2)]
    a16.push()
    rw = a16.alloc(8 * 8)
    for t in range(8):
        p.dma("gpsimd", rw[:, t * 8:(t + 1) * 8], Wr[t * 128:(t + 1) * 128, :], qg)
    rb = a16.alloc(16)
    p.dma("gpsimd", rb[0:1, 0:8], br[None, :], qg)
    rw_at = qg.n
    m8 = a16.alloc32(8)
    lgt = a16.alloc32(8)
    wv = a16.alloc32(4)
    ctmp = a16.alloc32(8)
    lgts = a16.alloc32(32)
    m8s = a16.alloc32(32)
    bR = PS[:, 0:512]

    def load_xfm(c, buf):
        for t in range(8):
            p.dma("sync", buf[:, t * 512:(t + 1) * 512],
                  src16[c * 512:(c + 1) * 512, t * 128:(t + 1) * 128],
                  qs, transpose=True)
        return qs.n

    xfm_free = [0, 0]
    for c in range(NC5):
        buf = xfm[c % 2]
        p.wait("sync", pe, xfm_free[c % 2])
        ld = load_xfm(c, buf)
        if "xfmdbg" in mk.dbg and c == 0:
            p.dma("sync", d["xfmdbg"][:, :], buf, qs)
        p.wait("tensor", qs, ld)
        if c == 0:
            p.wait("tensor", qg, rw_at)
        for ts in range(4):
            it = c * 4 + ts
            p.wait("tensor", dve, dve.n)
            for t in range(8):
                p.mm(bR[:, 0:8], buf[:, t * 512 + ts * 128: t * 512 + (ts + 1) * 128],
                     rw[:, t * 8:(t + 1) * 8], start=(t == 0), stop=False)
            p.mm(bR[:, 0:8], C["ones16"][0:1, 0:128], rb[0:1, 0:8],
                 start=False, stop=True, ctr=pe)
            p.wait("vector", pe, pe.n)
            p.v("tensor_copy", lgt, bR[:, 0:8], edge=True)
            p.v("max", m8, lgt, edge=True)
            p.v("tensor_sub", wv[:, 0:1], m8[:, 0:1], m8[:, 1:2], ctr=dve)
            p.wait("scalar", dve, dve.n)
            p.act(wv[:, 1:2], wv[:, 0:1], AF.Sigmoid, ctr=act)
            p.wait("vector", act, act.n)
            p.v("tensor_scalar", wv[:, 2:3], wv[:, 1:2], -1.0, 1.0, ALU.mult,
                ALU.add, edge=True)
            p.v("tensor_scalar", ctmp, lgt, m8[:, 0:1], wv[:, 1:2],
                ALU.is_equal, ALU.mult, edge=True)
            p.v("tensor_scalar", c32[:, it * 8:(it + 1) * 8], lgt, m8[:, 1:2],
                wv[:, 2:3], ALU.is_equal, ALU.mult, edge=True)
            p.v("tensor_add", c32[:, it * 8:(it + 1) * 8],
                c32[:, it * 8:(it + 1) * 8], ctmp, ctr=dve)
            if "routedbg" in mk.dbg and c == 0:
                p.v("tensor_copy", lgts[:, ts * 8:(ts + 1) * 8], lgt, edge=True)
                p.v("tensor_copy", m8s[:, ts * 8:(ts + 1) * 8], m8, ctr=dve)
                if ts == 3:
                    p.wait("sync", dve, dve.n)
                    p.dma("sync", d["routedbg"][:, 0:32], lgts, qs)
                    p.dma("sync", d["routedbg"][:, 32:64], m8s, qs)
        xfm_free[c % 2] = pe.n
    routing_done = p.sems["dve"].n
    a16.pop()
    p.wait("gpsimd", pe, pe.n)
    p.wait("gpsimd", dve, dve.n)
    p.wait("gpsimd", act, act.n)
    if "c32dbg" in mk.dbg:
        p.wait("sync", p.sems["dve"], routing_done)
        p.dma("sync", d["c32dbg"][:, :], c32, qs)

    w1e = a16.alloc(8 * 4096)
    w2e = a16.alloc(32 * 1024)
    b1sb = a16.alloc32(32)
    b2sb = a16.alloc(1024)
    he = a16.alloc(32 * 512)
    ysb = a16.alloc32(4 * 512)
    bank_he = [PS[:, 2048:2560], PS[:, 2560:3072]]
    bank_y = [PS[:, 0:512], PS[:, 512:1024], PS[:, 1024:1536], PS[:, 1536:2048]]

    acc_r = acc_h.rearrange("s d -> s d")
    w_free = 0
    w1_free = 0
    acc_order = {}
    ysb_free = {}
    xfm_ld = [0, 0]
    for e in range(8):
        p.wait("gpsimd", pe, w1_free)
        for t in range(8):
            p.dma("gpsimd", w1e[:, t * 4096:(t + 1) * 4096],
                  W1[e, t * 128:(t + 1) * 128, :], qg)
        p.wait("gpsimd", pe, w_free)
        for t in range(32):
            p.dma("gpsimd", w2e[:, t * 1024:(t + 1) * 1024],
                  W2[e, t * 128:(t + 1) * 128, :], qg)
        p.dma("gpsimd", b2sb[0:1, :], b2[e][None, :], qg)
        w_at = qg.n
        p.dma("sync", b1sb, b1[e].rearrange("(t p) -> p t", p=128), qs, slow=True)
        b1_at = qs.n

        for c in range(NC5):
            buf = xfm[c % 2]
            if c == 0:
                p.wait("sync", pe, xfm_free[0])
                ld = load_xfm(0, buf)
                xfm_ld[0] = ld
            ld = xfm_ld[c % 2]
            if c + 1 < NC5:
                p.wait("sync", pe, xfm_free[(c + 1) % 2])
                xfm_ld[(c + 1) % 2] = load_xfm(c + 1, xfm[(c + 1) % 2])
            p.wait("tensor", qs, ld)
            if c == 0:
                p.wait("tensor", qg, w_at)
                p.wait("scalar", qs, b1_at)
                if e == 0:
                    p.wait("scalar", dve, routing_done)
            he_evac = {}
            for ff in range(32):
                bnk = bank_he[ff % 2]
                if ff >= 2:
                    p.wait("tensor", act, he_evac[ff - 2])
                for t in range(8):
                    p.mm(bnk, w1e[:, t * 4096 + ff * 128: t * 4096 + (ff + 1) * 128],
                         buf[:, t * 512:(t + 1) * 512], start=(t == 0), stop=(t == 7),
                         ctr=pe if t == 7 else None)
                p.wait("scalar", pe, pe.n)
                p.act(he[:, ff * 512:(ff + 1) * 512], bnk, AF.Relu,
                      bias=b1sb[:, ff:ff + 1], ctr=act)
                he_evac[ff] = act.n
            if "hedbg" in mk.dbg and e == 0 and c == 0:
                p.wait("sync", p.sems["act"], p.sems["act"].n)
                p.dma("sync", d["hedbg"][:, :], he[:, 0:32 * 512], qs)
            if c == NC5 - 1:
                w1_free = pe.n
            for dh in range(2):
                for ts in range(4):
                    it = c * 4 + ts
                    bnk = bank_y[ts]
                    if ts in ysb_free:
                        p.wait("tensor", qg, ysb_free[ts])
                        p.wait("scalar", qg, ysb_free[ts])
                        p.wait("vector", qg, ysb_free[ts])
                    for ff in range(32):
                        if ff == 0:
                            p.wait("tensor", act, he_evac[31])
                        p.mm(bnk, he[:, ff * 512 + ts * 128: ff * 512 + (ts + 1) * 128],
                             w2e[:, ff * 1024 + dh * 512: ff * 1024 + dh * 512 + 512],
                             start=(ff == 0), stop=False)
                    p.mm(bnk, C["ones16"][0:1, 0:128], b2sb[0:1, dh * 512: dh * 512 + 512],
                         start=False, stop=True, ctr=pe)
                    mmv = pe.n
                    ysl = ysb[:, ts * 512:(ts + 1) * 512]
                    csc = c32[:, it * 8 + e: it * 8 + e + 1]
                    if dh == 0:
                        p.wait("scalar", pe, mmv)
                        p.act(ysl, bnk, AF.Copy, scale=csc, ctr=act)
                        ev = ("scalar", act.n)
                    else:
                        p.wait("vector", pe, mmv)
                        p.v("tensor_scalar_mul", ysl, bnk, csc, ctr=dve)
                        ev = ("vector", dve.n)
                    p.wait("gpsimd", p.sems[{"scalar": "act", "vector": "dve"}[ev[0]]],
                           ev[1])
                    if (it, dh) in acc_order:
                        p.wait("gpsimd", qg, acc_order[(it, dh)])
                    dst = acc_h[it * 128:(it + 1) * 128, dh * 512:(dh + 1) * 512]
                    p.dma("gpsimd", dst, ysl, qg, accum=(e > 0))
                    acc_order[(it, dh)] = qg.n
                    ysb_free[ts] = qg.n
            xfm_free[c % 2] = pe.n
        w_free = pe.n
    a16.pop()


def phase_combine(mk, p, a16, which):
    d, C = mk.d, mk.C
    qs, dve, act = p.sems["qs"], p.sems["dve"], p.sems["act"]
    a16.push()
    entry_guard(p)
    x0r = d["x0"].rearrange("(n p) d -> n p d", p=128)
    s_r = d["s32"].rearrange("(n p) d -> n p d", p=128)
    f_r = d["ffn0"].rearrange("(n p) d -> n p d", p=128)
    o_r = d["X132"].rearrange("(n p) d -> n p d", p=128)
    xb = [a16.alloc32(D) for _ in range(2)]
    sb_ = [a16.alloc32(D) for _ in range(2)]
    fb = [a16.alloc32(D) for _ in range(2)]
    gb = [a16.alloc32(D) for _ in range(2)]
    free3 = {}
    for i in range(NT):
        bi = i % 2
        if i >= 2:
            p.wait("sync", dve, free3[i - 2][0])
            p.wait("sync", qs, free3[i - 2][1])
        p.dma("sync", xb[bi], x0r[i], qs)
        p.dma("sync", sb_[bi], s_r[i], qs)
        p.dma("sync", fb[bi], f_r[i], qs)
        ld = qs.n
        p.wait("scalar", qs, ld)
        p.act(gb[bi], fb[bi], AF.Silu, ctr=act)
        p.wait("vector", act, act.n)
        p.v("tensor_mul", gb[bi], gb[bi], sb_[bi], edge=True)
        p.v("tensor_add", gb[bi], gb[bi], xb[bi], ctr=dve)
        dn = dve.n
        p.wait("sync", dve, dn)
        p.dma("sync", o_r[i], gb[bi], qs)
        free3[i] = (dn, qs.n)
    a16.pop()


def phase_E(mk, p, a16):
    d, C = mk.d, mk.C
    qs, dve, act = p.sems["qs"], p.sems["dve"], p.sems["act"]
    a16.push()
    entry_guard(p)
    wbc = load_row_bcast(mk, p, a16, d["L1_attn_w_ln1"], D)
    p.wait("vector", qs, qs.n)
    p.v("tensor_scalar_add", wbc, wbc, 1.0, ctr=dve)
    xin = [a16.alloc32(D) for _ in range(2)]
    st8 = [a16.alloc32(12) for _ in range(2)]
    mv = [a16.alloc32(4) for _ in range(2)]
    xn32t = [a16.alloc32(D) for _ in range(2)]
    xn16t = [a16.alloc(D) for _ in range(2)]
    h0t = [a16.alloc(D) for _ in range(2)]

    x0r = d["X132"].rearrange("(n p) d -> n p d", p=128)
    xn32r = d["xn1_32"].rearrange("(n p) d -> n p d", p=128)
    xn16r = d["xn1_16"].rearrange("(n p) d -> n p d", p=128)
    h0r = d["h016"].rearrange("(n p) d -> n p d", p=128)

    free_at = {}
    p.wait("sync", dve, dve.n)
    for i in range(NT):
        if i == 0:
            p.wait("tensor", dve, dve.n)
            p.wait("tensor", act, act.n)
        p.mm(mk.PS[0:64, 0:64], C["ident16"][:, 0:64], C["ident16"][:, 0:64],
             start=True, stop=True)
        bi = i % 2
        if i >= 2:
            p.wait("sync", dve, free_at[i - 2][0])
            p.wait("sync", qs, free_at[i - 2][1])
        p.dma("sync", xin[bi], x0r[i], qs)
        ld = qs.n
        p.wait("vector", qs, ld)
        p.v("bn_stats", st8[bi][:, 0:6], xin[bi][:, 0:512], edge=True)
        p.v("bn_stats", st8[bi][:, 6:12], xin[bi][:, 512:1024], edge=True)
        p.v("bn_aggr", mv[bi][:, 0:2], st8[bi][:, 0:12], edge=True)
        p.wait("scalar", dve, dve.n)
        emit_rsqrt(p, mv[bi][:, 2:3], mv[bi][:, 1:2], pre_bias=1e-5)
        rstd_done = act.n
        p.v("tensor_scalar_add", mv[bi][:, 3:4], mv[bi][:, 1:2], 1e-5, edge=True)
        p.v("reciprocal", mv[bi][:, 3:4], mv[bi][:, 3:4], edge=True)
        p.v("tensor_mul", mv[bi][:, 3:4], mv[bi][:, 3:4], mv[bi][:, 1:2], edge=True)
        p.v("tensor_scalar_add", mv[bi][:, 3:4], mv[bi][:, 3:4], 1e-6, ctr=dve)
        p.wait("scalar", dve, dve.n)
        emit_rsqrt(p, mv[bi][:, 3:4], mv[bi][:, 3:4])
        rms2_done = act.n
        p.wait("vector", act, rstd_done)
        p.v("tensor_scalar", xn32t[bi], xin[bi], mv[bi][:, 0:1], mv[bi][:, 2:3],
            ALU.subtract, ALU.mult, edge=True)
        p.v("tensor_copy", xn16t[bi], xn32t[bi], ctr=dve)
        xn_done = dve.n
        p.wait("vector", act, rms2_done)
        p.v("scalar_tensor_tensor", h0t[bi], xn32t[bi], mv[bi][:, 3:4], wbc,
            ALU.mult, ALU.mult, ctr=dve)
        h0_done = dve.n
        p.wait("sync", dve, xn_done)
        p.dma("sync", xn32r[i], xn32t[bi], qs)
        p.dma("sync", xn16r[i], xn16t[bi], qs)
        p.wait("sync", dve, h0_done)
        p.dma("sync", h0r[i], h0t[bi], qs)
        free_at[i] = (dve.n, qs.n)
    a16.pop()


def phase_F(mk, p, a16):
    d, C = mk.d, mk.C
    qs, qg = p.sems["qs"], p.sems["qg"]
    pe, act, dve = p.sems["pe"], p.sems["act"], p.sems["dve"]
    PS, PT = mk.PS, mk.PT
    a16.push()
    entry_guard(p)

    wqkvz = a16.alloc(8 * 3072)
    for t in range(8):
        p.dma("gpsimd", wqkvz[:, t * 3072:(t + 1) * 3072],
              d["L1_attn_W_qkvz"][t * 128:(t + 1) * 128, :], qg)
    wba = a16.alloc(8 * 16)
    for t in range(8):
        p.dma("gpsimd", wba[:, t * 16:(t + 1) * 16],
              d["L1_attn_W_ba"][t * 128:(t + 1) * 128, :], qg)
    wout = a16.alloc(8 * D)
    for t in range(8):
        p.dma("gpsimd", wout[:, t * D:(t + 1) * D],
              d["L1_attn_W_out"][t * 128:(t + 1) * 128, :], qg)
    w_at = qg.n
    convw = a16.alloc32(64)
    p.dma("sync", convw, d["L1_attn_conv_w"].rearrange("(t p) k -> p t k", p=128), qs)
    dtb = a16.alloc32(2)
    p.dma("sync", dtb[0:8, 0:1], d["L1_attn_dt_bias"][:, None], qs)
    alog = a16.alloc32(2)
    p.dma("sync", alog[0:8, 0:1], d["L1_attn_A_log"][:, None], qs)
    gnw = a16.alloc32(2)
    p.dma("sync", gnw[:, 0:1], d["L1_attn_gate_norm_w"][:, None], qs)
    ln2bc = load_row_bcast(mk, p, a16, d["L1_attn_w_ln2"], D)
    cw_at = qs.n
    p.wait("vector", qs, cw_at)
    p.v("tensor_scalar_add", ln2bc, ln2bc, 1.0, edge=True)
    p.wait("scalar", qs, cw_at)
    p.act(alog[0:8, 1:2], alog[0:8, 0:1], AF.Exp, edge=True)
    p.wait("vector", act, act.n)
    p.v("tensor_scalar_mul", alog[0:8, 1:2], alog[0:8, 1:2], -1.0, ctr=dve)
    dg = a16.alloc(512)
    p.wait("vector", qs, cw_at)
    diag_done = p.sems["dve"].n

    hfm = [a16.alloc(4096) for _ in range(2)]
    qkvb = a16.alloc(16 * 528)
    halo = a16.alloc(48)
    zsil = a16.alloc(8 * 512)
    bas = a16.alloc32(512)
    absb = a16.alloc32(512)
    gfm = a16.alloc32(512)
    S0 = a16.alloc32(8 * 128)
    S0b = a16.alloc(8 * 128)
    qn = a16.alloc(4 * 128)
    kn = a16.alloc(4 * 128)
    ktm = a16.alloc(4 * 128)
    vtm = a16.alloc(8 * 128)
    bcol = a16.alloc32(16)
    bwork = a16.alloc32(80)
    rows = a16.alloc32(8 * 128)
    rows16 = a16.alloc(8 * 128)
    erow16 = a16.alloc(8 * 128)
    DDa = a16.alloc32(1024)
    DDc = a16.alloc32(1024)
    EMLa = a16.alloc(1024)
    EMUa = a16.alloc(1024)
    Aa = a16.alloc(1024)
    Va = a16.alloc(1024)
    VTa = a16.alloc(1024)
    V2a = a16.alloc(1024)
    V2Ta = a16.alloc(1024)
    XAPa = a16.alloc(2048)
    MKTa = Aa
    DLTa = EMUa
    QTa = EMLa
    AQTa = a16.alloc(1024)
    OSBa = V2Ta
    KPa = V2a
    sqv = a16.alloc(128)
    rr = a16.alloc32(1056)
    core = a16.alloc(8 * 128)
    xres = a16.alloc32(D)
    x2t = a16.alloc32(D)
    xht = a16.alloc(D)
    sq2 = xres
    mv2 = a16.alloc32(4)
    halo_init = False

    b0, b1 = PS[:, 0:512], PS[:, 512:1024]
    b2, b3 = PS[:, 1024:1536], PS[:, 1536:2048]
    b4, b5 = PS[:, 2048:2560], PS[:, 2560:3072]
    tpA, tpB = PT[:, 0:1024], PT[:, 1024:2048]

    src16 = d["h016"]
    xn1r = d["xn1_32"].rearrange("(n p) d -> n p d", p=128)
    x2r = d["x232"].rearrange("(n p) d -> n p d", p=128)
    xhr = d["xh16"].rearrange("(n p) d -> n p d", p=128)

    p.v("memset", S0, 0.0, edge=True)
    p.v("memset", S0b, 0.0, edge=True)
    p.v("memset", qkvb, 0.0, ctr=dve)
    init_done = dve.n
    p.wait("tensor", dve, max(init_done, diag_done))
    p.wait("tensor", qg, w_at)

    def load_hfm(c, buf):
        for t in range(8):
            p.dma("sync", buf[:, t * 512:(t + 1) * 512],
                  src16[c * 512:(c + 1) * 512, t * 128:(t + 1) * 128],
                  qs, transpose=True)
        return qs.n

    for sc in range(NC5):
        buf = hfm[sc % 2]
        p.wait("sync", pe, pe.n)
        ld = load_hfm(sc, buf)
        p.wait("tensor", qs, ld)
        for ct in range(16):
            sl = qkvb[:, ct * 528: ct * 528 + 515]
            if sc > 0:
                p.v("tensor_copy", sl[:, 0:3], halo[:, ct * 3:(ct + 1) * 3],
                    edge=True)
            bnk = b0 if ct % 2 == 0 else b1
            p.wait("tensor", dve, dve.n)
            p.wait("tensor", act, act.n)
            for t in range(8):
                p.mm(bnk, wqkvz[:, t * 3072 + ct * 128: t * 3072 + (ct + 1) * 128],
                     buf[:, t * 512:(t + 1) * 512], start=(t == 0), stop=(t == 7),
                     ctr=pe if t == 7 else None)
            p.wait("vector", pe, pe.n)
            p.v("tensor_copy", sl[:, 3:515], bnk, edge=True)
            p.v("tensor_copy", halo[:, ct * 3:(ct + 1) * 3], sl[:, 512:515],
                ctr=dve)
        for ct in range(16):
            bnk = b0 if ct % 2 == 0 else b1
            p.wait("vector", pe, pe.n)
            for tap in range(4):
                p.v("tensor_scalar_mul", dg[:, tap * 128:(tap + 1) * 128],
                    C["ident16"], convw[:, ct * 4 + tap: ct * 4 + tap + 1],
                    ctr=dve if tap == 3 else None)
            p.wait("tensor", dve, dve.n)
            p.wait("tensor", act, act.n)
            for tap in range(4):
                p.mm(bnk, dg[:, tap * 128:(tap + 1) * 128],
                     qkvb[:, ct * 528 + tap: ct * 528 + tap + 512],
                     start=(tap == 0), stop=(tap == 3), ctr=pe if tap == 3 else None)
            p.wait("scalar", pe, pe.n)
            p.act(qkvb[:, ct * 528 + 3: ct * 528 + 515], bnk, AF.Silu, ctr=act)
        for zt in range(8):
            ct = 16 + zt
            bnk = b0 if ct % 2 == 0 else b1
            p.wait("tensor", dve, dve.n)
            p.wait("tensor", act, act.n)
            for t in range(8):
                p.mm(bnk, wqkvz[:, t * 3072 + ct * 128: t * 3072 + (ct + 1) * 128],
                     buf[:, t * 512:(t + 1) * 512], start=(t == 0), stop=(t == 7),
                     ctr=pe if t == 7 else None)
            p.wait("scalar", pe, pe.n)
            p.act(zsil[:, zt * 512:(zt + 1) * 512], bnk, AF.Silu, ctr=act)
        p.wait("tensor", act, act.n)
        for t in range(8):
            p.mm(b0[0:8, :], wba[:, t * 16:t * 16 + 8],
                 buf[:, t * 512:(t + 1) * 512], start=(t == 0), stop=(t == 7))
        for t in range(8):
            p.mm(b1[0:8, :], wba[:, t * 16 + 8:t * 16 + 16],
                 buf[:, t * 512:(t + 1) * 512], start=(t == 0), stop=(t == 7),
                 ctr=pe if t == 7 else None)
        p.wait("vector", pe, pe.n)
        p.v("tensor_copy", bas[0:8, :], b0[0:8, :], edge=True)
        p.v("tensor_copy", absb[0:8, :], b1[0:8, :], edge=True)
        p.wait("scalar", dve, dve.n)
        p.act(bas[0:8, :], bas[0:8, :], AF.Sigmoid, edge=True)
        p.act(gfm[0:8, :], absb[0:8, :], AF.Abs, bias=dtb[0:8, 0:1], edge=True)
        p.act(gfm[0:8, :], gfm[0:8, :], AF.Exp, scale=-1.0, edge=True)
        p.act(gfm[0:8, :], gfm[0:8, :], AF.Ln, bias=1.0, ctr=act)
        sp_done = act.n
        p.wait("vector", act, sp_done)
        p.v("tensor_scalar", absb[0:8, :], absb[0:8, :], dtb[0:8, 0:1], 0.0,
            ALU.add, ALU.max, edge=True)
        p.v("tensor_add", gfm[0:8, :], gfm[0:8, :], absb[0:8, :], edge=True)
        p.v("tensor_scalar_mul", gfm[0:8, :], gfm[0:8, :], alog[0:8, 1:2], ctr=dve)
        g_done = dve.n

        for dc in range(4):
            tk = dc * 128
            ti = sc * 4 + dc
            p.wait("tensor", dve, g_done)
            p.wait("tensor", act, act.n)
            p.trans(b2[0:128, 0:8], gfm[0:8, tk:tk + 128], C["ident32"][0:8, 0:8], ctr=pe)
            p.wait("vector", pe, pe.n)
            p.v("tensor_copy", bwork[:, 0:8], b2[:, 0:8], ctr=dve)
            p.wait("tensor", dve, dve.n)
            p.mm(b3[:, 0:8], C["cum_ut32"], bwork[:, 0:8], start=True, stop=True)
            p.trans(b2[0:128, 8:16], bas[0:8, tk:tk + 128], C["ident32"][0:8, 0:8], ctr=pe)
            p.wait("vector", pe, pe.n)
            p.v("tensor_copy", bcol[:, 0:8], b3[:, 0:8], edge=True)
            p.v("tensor_copy", bwork[:, 8:16], b2[:, 8:16], edge=True)
            p.v("tensor_scalar_mul", bwork[:, 16:24], bwork[:, 8:16], -1.0, edge=True)
            p.wait("scalar", dve, dve.n)
            p.act(bwork[:, 24:32], bcol[:, 0:8], AF.Exp, ctr=act)
            p.wait("vector", act, act.n)
            p.v("tensor_mul", bwork[:, 32:40], bwork[:, 16:24], bwork[:, 24:32],
                ctr=dve)
            percol_done = dve.n
            p.wait("tensor", dve, percol_done)
            for h in range(8):
                p.trans(b2[0:1, 256:384], bcol[:, h:h + 1], C["ident32"], ctr=pe)
                p.wait("vector", pe, pe.n)
                p.v("tensor_copy", rows[0:1, h * 128:(h + 1) * 128],
                    b2[0:1, 256:384], ctr=dve)
                p.wait("tensor", dve, dve.n)
            rows_done = p.sems["dve"].n
            p.wait("scalar", dve, rows_done)
            p.act(erow16[0:1, :], rows[0:1, :], AF.Exp, ctr=act)
            p.wait("vector", act, act.n)
            for h in range(8):
                p.v("tensor_copy", rr[0:1, 256 + h:257 + h],
                    rows[0:1, h * 128 + 127: h * 128 + 128],
                    ctr=dve if h == 7 else None)
            p.wait("tensor", dve, dve.n)
            p.mm(b5[:, 0:8], C["ones32"][0:1, 0:128], rr[0:1, 256:264],
                 start=True, stop=True, ctr=pe)
            p.wait("vector", pe, pe.n)
            p.v("tensor_copy", bwork[:, 40:48], b5[:, 0:8], edge=True)
            p.v("tensor_sub", bwork[:, 48:56], bwork[:, 40:48], bcol[:, 0:8], ctr=dve)
            p.wait("scalar", dve, dve.n)
            p.act(bwork[:, 56:64], bwork[:, 48:56], AF.Exp, edge=True)
            p.act(bwork[:, 64:72], bwork[:, 40:48], AF.Exp, ctr=act)
            percol2_done = act.n

            p.wait("tensor", act, percol2_done)
            for hh in range(4):
                qsl = qkvb[:, hh * 528 + 3 + tk: hh * 528 + 3 + tk + 128]
                ksl = qkvb[:, (4 + hh) * 528 + 3 + tk: (4 + hh) * 528 + 3 + tk + 128]
                for (xi, xsl, osl, qscale) in ((0, qsl, qn, True), (1, ksl, kn, False)):
                    dst = osl[:, hh * 128:(hh + 1) * 128]
                    p.wait("vector", pe, pe.n)
                    p.v("tensor_mul", sqv, xsl, xsl, ctr=dve)
                    p.wait("tensor", dve, dve.n)
                    p.mm(b2[0:1, 256:384], C["ones16"][:, 0:1], sqv,
                         start=True, stop=True, ctr=pe)
                    p.wait("vector", pe, pe.n)
                    p.v("tensor_copy", rr[0:1, 0:128], b2[0:1, 256:384], ctr=dve)
                    p.wait("scalar", dve, dve.n)
                    sc_ = (128.0 ** -0.5) if qscale else 1.0
                    emit_rsqrt(p, rr[0:1, 128:256], rr[0:1, 0:128], pre_bias=1e-6,
                               post_scale=sc_)
                    p.wait("vector", act, act.n)
                    p.v("tensor_copy", rows16[0:1, 0:128], rr[0:1, 128:256],
                        ctr=dve)
                    p.wait("tensor", dve, dve.n)
                    p.mm(b3[:, 256:384], C["ones16"][0:1, 0:128],
                         rows16[0:1, 0:128], start=True, stop=True, ctr=pe)
                    p.wait("vector", pe, pe.n)
                    p.v("tensor_mul", dst, xsl, b3[:, 256:384], ctr=dve)
                p.wait("tensor", dve, dve.n)
                p.trans(tpB[:, 0:128], kn[:, hh * 128:(hh + 1) * 128], C["ident16"],
                        ctr=pe)
                p.wait("vector", pe, pe.n)
                p.v("tensor_copy", ktm[:, hh * 128:(hh + 1) * 128], tpB[:, 0:128],
                    ctr=dve)
            for h in range(8):
                vsl = qkvb[:, (8 + h) * 528 + 3 + tk: (8 + h) * 528 + 3 + tk + 128]
                p.wait("tensor", dve, dve.n)
                p.trans(tpB[:, 128:256], vsl, C["ident16"], ctr=pe)
                p.wait("vector", pe, pe.n)
                p.v("tensor_copy", vtm[:, h * 128:(h + 1) * 128], tpB[:, 128:256],
                    ctr=dve)
            prep_done = dve.n

            rbrow = rows16
            p.wait("tensor", dve, dve.n)
            p.wait("tensor", act, act.n)
            for h in range(8):
                p.mm(PS[:, 1024 + h * 128:1024 + (h + 1) * 128],
                     C["ones32"][0:1, 0:128], rows[0:1, h * 128:(h + 1) * 128],
                     start=True, stop=True, ctr=pe if h == 7 else None)
            p.wait("vector", pe, pe.n)
            for h in range(8):
                p.v("tensor_scalar", DDa[:, h * 128:(h + 1) * 128],
                    PS[:, 1024 + h * 128:1024 + (h + 1) * 128], bcol[:, h:h + 1],
                    None, ALU.subtract, ctr=dve if h == 7 else None)
            p.wait("vector", dve, dve.n)
            p.v("tensor_scalar", DDc, DDa, -1.0, 0.0, ALU.mult, ALU.min, ctr=dve)
            p.wait("scalar", dve, dve.n)
            p.act(DDc, DDc, AF.Exp, ctr=act)
            p.wait("vector", act, act.n)
            mlow_r = C["m_lows32"].rearrange("p (o f) -> p o f", o=1).to_broadcast([128, 8, 128])
            mupi_r = C["m_upi32"].rearrange("p (o f) -> p o f", o=1).to_broadcast([128, 8, 128])
            p.v("tensor_tensor", EMLa.rearrange("p (h f) -> p h f", f=128),
                DDc.rearrange("p (h f) -> p h f", f=128), mlow_r, ALU.mult,
                edge=True)
            p.v("tensor_scalar_min", DDc, DDa, 0.0, ctr=dve)
            p.wait("scalar", dve, dve.n)
            p.act(DDc, DDc, AF.Exp, ctr=act)
            p.wait("vector", act, act.n)
            p.v("tensor_tensor", EMUa.rearrange("p (h f) -> p h f", f=128),
                DDc.rearrange("p (h f) -> p h f", f=128), mupi_r, ALU.mult,
                ctr=dve)
            p.wait("tensor", dve, dve.n)
            for h in range(8):
                kf = kn[:, (h // 2) * 128:(h // 2 + 1) * 128]
                qf = qn[:, (h // 2) * 128:(h // 2 + 1) * 128]
                p.mm(PS[:, 1024 + h * 128:1024 + (h + 1) * 128], kf, kf,
                     start=True, stop=True)
                p.mm(PS[:, 2048 + h * 128:2048 + (h + 1) * 128], kf, qf,
                     start=True, stop=True, ctr=pe if h == 7 else None)
            p.wait("vector", pe, pe.n)
            p.v("tensor_copy", Aa, PS[:, 1024:2048], edge=True)
            p.v("tensor_tensor", AQTa, PS[:, 2048:3072], EMUa, ALU.mult, edge=True)
            for h in range(8):
                p.v("scalar_tensor_tensor", Va[:, h * 128:(h + 1) * 128],
                    Aa[:, h * 128:(h + 1) * 128], bwork[:, 16 + h:17 + h],
                    EMLa[:, h * 128:(h + 1) * 128], ALU.mult, ALU.mult,
                    ctr=dve if h == 7 else None)
            p.wait("tensor", dve, dve.n)
            for h in range(8):
                p.trans(tpA[:, h * 128:(h + 1) * 128], Va[:, h * 128:(h + 1) * 128],
                        C["ident16"], ctr=pe if h == 7 else None)
            p.wait("vector", pe, pe.n)
            p.v("tensor_copy", VTa, tpA[:, 0:1024], edge=True)
            for h in range(8):
                p.v("tensor_scalar_mul", XAPa[:, h * 256:h * 256 + 128],
                    vtm[:, h * 128:(h + 1) * 128], bwork[:, 8 + h:9 + h])
                p.v("tensor_scalar_mul", XAPa[:, h * 256 + 128:h * 256 + 256],
                    ktm[:, (h // 2) * 128:(h // 2 + 1) * 128],
                    bwork[:, 32 + h:33 + h], ctr=dve if h == 7 else None)
            p.wait("tensor", dve, dve.n)
            vc, vtc, v2c, v2tc = Va, VTa, V2a, V2Ta
            for itn in range(7):
                for h in range(8):
                    hb = PS[:, h * 256:(h + 1) * 256]
                    p.mm(hb, C["ident16"], XAPa[:, h * 256:(h + 1) * 256],
                         start=True, stop=False)
                    p.mm(hb, vtc[:, h * 128:(h + 1) * 128],
                         XAPa[:, h * 256:(h + 1) * 256],
                         start=False, stop=True,
                         ctr=pe if (h == 7 and itn == 6) else None)
                if itn < 6:
                    for h in range(8):
                        p.mm(PS[:, 2048 + h * 128:2048 + (h + 1) * 128],
                             vtc[:, h * 128:(h + 1) * 128],
                             vc[:, h * 128:(h + 1) * 128], start=True, stop=True,
                             ctr=pe if h == 7 else None)
                p.wait("vector", pe, pe.n)
                p.v("tensor_copy", XAPa, PS[:, 0:2048], edge=True)
                if itn < 6:
                    p.v("tensor_copy", v2c, PS[:, 2048:3072], ctr=dve)
                    p.wait("tensor", dve, dve.n)
                    for h in range(8):
                        p.mm(PS[:, 2048 + h * 128:2048 + (h + 1) * 128],
                             vc[:, h * 128:(h + 1) * 128],
                             vtc[:, h * 128:(h + 1) * 128], start=True, stop=True,
                             ctr=pe if h == 7 else None)
                    p.wait("vector", pe, pe.n)
                    p.v("tensor_copy", v2tc, PS[:, 2048:3072], ctr=dve)
                    vc, v2c = v2c, vc
                    vtc, v2tc = v2tc, vtc
                else:
                    p.v("engine_nop", ctr=dve)
                p.wait("tensor", dve, dve.n)
            for h in range(8):
                p.trans(tpB[:, h * 128:(h + 1) * 128],
                        XAPa[:, h * 256 + 128:h * 256 + 256], C["ident16"],
                        ctr=pe if h == 7 else None)
            p.wait("vector", pe, pe.n)
            p.v("tensor_copy", MKTa, tpB[:, 0:1024], ctr=dve)
            p.wait("tensor", dve, dve.n)
            for h in range(8):
                p.mm(PS[:, h * 128:(h + 1) * 128], MKTa[:, h * 128:(h + 1) * 128],
                     S0b[:, h * 128:(h + 1) * 128], start=True, stop=True,
                     ctr=pe if h == 7 else None)
            p.wait("vector", pe, pe.n)
            p.v("tensor_tensor", DLTa.rearrange("p (h f) -> p h f", f=128),
                PS[:, 0:1024].rearrange("p (h f) -> p h f", f=128),
                XAPa.rearrange("p (h f) -> p h f", f=256)[:, :, 0:128],
                ALU.add, ctr=dve)
            p.wait("tensor", dve, dve.n)
            for h in range(8):
                p.mm(PS[:, 1024 + h * 128:1024 + (h + 1) * 128],
                     C["ones16"][0:1, 0:128], erow16[0:1, h * 128:(h + 1) * 128],
                     start=True, stop=True, ctr=pe if h == 7 else None)
            p.wait("vector", pe, pe.n)
            p.v("tensor_tensor", QTa.rearrange("p (h r f) -> p h r f", r=2, f=128),
                qn.rearrange("p (h o f) -> p h o f", o=1, f=128)
                  .to_broadcast([128, 4, 2, 128]),
                PS[:, 1024:2048].rearrange("p (h r f) -> p h r f", r=2, f=128),
                ALU.mult, ctr=dve)
            p.wait("tensor", dve, dve.n)
            for h in range(8):
                ob = PS[:, 1024 + h * 128:1024 + (h + 1) * 128]
                p.mm(ob, S0b[:, h * 128:(h + 1) * 128],
                     QTa[:, h * 128:(h + 1) * 128], start=True, stop=False)
                p.mm(ob, DLTa[:, h * 128:(h + 1) * 128],
                     AQTa[:, h * 128:(h + 1) * 128], start=False, stop=True,
                     ctr=pe if h == 7 else None)
            p.wait("vector", pe, pe.n)
            p.v("tensor_copy", OSBa, PS[:, 1024:2048], edge=True)
            for h in range(8):
                p.v("tensor_scalar_mul", KPa[:, h * 128:(h + 1) * 128],
                    ktm[:, (h // 2) * 128:(h // 2 + 1) * 128],
                    bwork[:, 56 + h:57 + h], ctr=dve if h == 7 else None)
            p.wait("tensor", dve, dve.n)
            for h in range(8):
                p.mm(PS[:, 2048 + h * 128:2048 + (h + 1) * 128],
                     KPa[:, h * 128:(h + 1) * 128], DLTa[:, h * 128:(h + 1) * 128],
                     start=True, stop=True, ctr=pe if h == 7 else None)
            p.wait("vector", pe, pe.n)
            for h in range(8):
                p.v("scalar_tensor_tensor", S0[:, h * 128:(h + 1) * 128],
                    S0[:, h * 128:(h + 1) * 128], bwork[:, 64 + h:65 + h],
                    PS[:, 2048 + h * 128:2048 + (h + 1) * 128], ALU.mult, ALU.add,
                    ctr=dve if h == 7 else None)
            p.wait("vector", dve, dve.n)
            p.v("tensor_copy", S0b, S0, edge=True)
            p.v("tensor_mul", KPa, OSBa, OSBa, ctr=dve)
            p.wait("tensor", dve, dve.n)
            p.mm(PS[0:1, 0:512], C["ones16"][:, 0:1], KPa[:, 0:512],
                 start=True, stop=True)
            p.mm(PS[0:1, 512:1024], C["ones16"][:, 0:1], KPa[:, 512:1024],
                 start=True, stop=True, ctr=pe)
            p.wait("scalar", pe, pe.n)
            emit_rsqrt(p, rr[0:1, 0:1024], PS[0:1, 0:1024], pre_scale=1.0 / 128,
                       pre_bias=1e-6)
            p.wait("vector", act, act.n)
            p.v("tensor_copy", rbrow[0:1, 0:1024], rr[0:1, 0:1024], ctr=dve)
            p.wait("tensor", dve, dve.n)
            for h in range(8):
                p.mm(PS[:, 1024 + h * 128:1024 + (h + 1) * 128],
                     C["ones16"][0:1, 0:128], rbrow[0:1, h * 128:(h + 1) * 128],
                     start=True, stop=True, ctr=pe if h == 7 else None)
            p.wait("vector", pe, pe.n)
            p.v("scalar_tensor_tensor", OSBa, OSBa, gnw[:, 0:1], PS[:, 1024:2048],
                ALU.mult, ALU.mult, edge=True)
            p.v("tensor_tensor", core.rearrange("p (h f) -> p h f", f=128),
                OSBa.rearrange("p (h f) -> p h f", f=128),
                zsil.rearrange("p (h f) -> p h f", f=512)[:, :, tk:tk + 128],
                ALU.mult, ctr=dve)
            core_done = dve.n

            p.dma("sync", xres, xn1r[ti], qs)
            ldx = qs.n
            p.wait("tensor", dve, core_done)
            for half in range(2):
                bnk = b0 if half == 0 else b1
                for t in range(8):
                    p.mm(bnk, core[:, t * 128:(t + 1) * 128],
                         wout[:, t * D + half * 512: t * D + half * 512 + 512],
                         start=(t == 0), stop=(t == 7), ctr=pe if t == 7 else None)
                p.wait("vector", pe, pe.n)
                p.wait("vector", qs, ldx)
                p.v("tensor_add", x2t[:, half * 512:(half + 1) * 512],
                    xres[:, half * 512:(half + 1) * 512], bnk, ctr=dve)
            x2_done = dve.n
            p.wait("sync", dve, x2_done)
            p.dma("sync", x2r[ti], x2t, qs)
            p.wait("scalar", dve, x2_done)
            p.act(sq2, x2t, AF.Square, accum_out=mv2[:, 0:1], edge=True)
            emit_rsqrt(p, mv2[:, 1:2], mv2[:, 0:1], pre_scale=1.0 / D,
                       pre_bias=1e-6)
            p.wait("vector", act, act.n)
            p.v("scalar_tensor_tensor", xht, x2t, mv2[:, 1:2], ln2bc,
                ALU.mult, ALU.mult, ctr=dve)
            p.wait("sync", dve, dve.n)
            p.dma("sync", xhr[ti], xht, qs)
    a16.pop()


def phase_prepass(mk, p, a16):
    d = mk.d
    qg = p.sems["qg"]
    jobs = [
        (d["L1_ffn_shared_w1"], d["sw1_16"], 4096),
        (d["L1_ffn_shared_w2"], d["sw2_16"], 1024),
        (d["L1_ffn_experts_w1"].rearrange("e f d -> (e f) d"), d["ew1_16"], 8 * 4096),
        (d["L1_ffn_experts_w2"].rearrange("e d f -> (e d) f"), d["ew2_16"], 8 * 1024),
        (d["L1_ffn_gate_w"], d["gw_16"], 8),
    ]
    for srcT, dstT, R in jobs:
        r = 0
        while r < R:
            rr_ = min(1024, R - r)
            p.dma("gpsimd", dstT[r:r + rr_, :], srcT[r:r + rr_, :], qg)
            r += rr_


def phase_moe2(mk, p, a16, which):
    d, C = mk.d, mk.C
    qs, qg = p.sems["qs"], p.sems["qg"]
    pe, act, dve = p.sems["pe"], p.sems["act"], p.sems["dve"]
    PS = mk.PS
    C = mk.C
    a16.push()
    entry_guard(p)

    if which == "G":
        src16, acc_h = d["xh16"], d["moe1"]
        NFF, glu = 4, True
    else:
        src16, acc_h = d["xn1_16"], d["ffn1"]
        NFF, glu = 32, False

    c32 = a16.alloc32(NT * 8)
    xfm = [a16.alloc(4096) for _ in range(2)]
    a16.push()
    rw = a16.alloc(8 * 8)
    if which == "G":
        for t in range(8):
            p.dma("gpsimd", rw[:, t * 8:(t + 1) * 8],
                  d["L1_attn_W_router"][t * 128:(t + 1) * 128, :], qg)
    else:
        gwrow = a16.alloc(1024)
        p.dma("sync", gwrow[0:8, :], d["gw_16"][:, :], qs)
        p.wait("tensor", qs, qs.n)
        for t in range(8):
            p.trans(mk.PT[0:128, 0:8], gwrow[0:8, t * 128:(t + 1) * 128],
                    C["ident16"][0:8, 0:8], ctr=pe)
            p.wait("vector", pe, pe.n)
            p.v("tensor_copy", rw[:, t * 8:(t + 1) * 8], mk.PT[0:128, 0:8], ctr=dve)
            p.wait("tensor", dve, dve.n)
    rw_at = (qg.n, qs.n)
    m8 = a16.alloc32(8)
    lgt = a16.alloc32(8)
    wv = a16.alloc32(4)
    ctmp = a16.alloc32(8)
    bR = PS[:, 0:512]

    def load_xfm(c, buf):
        for t in range(8):
            p.dma("sync", buf[:, t * 512:(t + 1) * 512],
                  src16[c * 512:(c + 1) * 512, t * 128:(t + 1) * 128],
                  qs, transpose=True)
        return qs.n

    xfm_free = [0, 0]
    for c in range(NC5):
        buf = xfm[c % 2]
        p.wait("sync", pe, xfm_free[c % 2])
        ld = load_xfm(c, buf)
        p.wait("tensor", qs, ld)
        if c == 0:
            p.wait("tensor", qg, rw_at[0])
            p.wait("tensor", qs, rw_at[1])
        for ts in range(4):
            it = c * 4 + ts
            p.wait("tensor", dve, dve.n)
            for t in range(8):
                p.mm(bR[:, 0:8], buf[:, t * 512 + ts * 128: t * 512 + (ts + 1) * 128],
                     rw[:, t * 8:(t + 1) * 8], start=(t == 0), stop=(t == 7),
                     ctr=pe if t == 7 else None)
            p.wait("vector", pe, pe.n)
            p.v("tensor_copy", lgt, bR[:, 0:8], edge=True)
            p.v("max", m8, lgt, edge=True)
            p.v("tensor_sub", wv[:, 0:1], m8[:, 0:1], m8[:, 1:2], ctr=dve)
            p.wait("scalar", dve, dve.n)
            p.act(wv[:, 1:2], wv[:, 0:1], AF.Sigmoid, ctr=act)
            p.wait("vector", act, act.n)
            p.v("tensor_scalar", wv[:, 2:3], wv[:, 1:2], -1.0, 1.0, ALU.mult,
                ALU.add, edge=True)
            p.v("tensor_scalar", ctmp, lgt, m8[:, 0:1], wv[:, 1:2],
                ALU.is_equal, ALU.mult, edge=True)
            p.v("tensor_scalar", c32[:, it * 8:(it + 1) * 8], lgt, m8[:, 1:2],
                wv[:, 2:3], ALU.is_equal, ALU.mult, edge=True)
            p.v("tensor_add", c32[:, it * 8:(it + 1) * 8],
                c32[:, it * 8:(it + 1) * 8], ctmp, ctr=dve)
        xfm_free[c % 2] = pe.n
    routing_done = p.sems["dve"].n
    a16.pop()
    p.wait("gpsimd", pe, pe.n)
    p.wait("gpsimd", dve, dve.n)
    p.wait("gpsimd", act, act.n)
    p.wait("sync", dve, dve.n)

    if which == "G":
        w1e = a16.alloc(8 * 512)
        wue = a16.alloc(8 * 512)
        w2e = a16.alloc(4 * 1024)
    else:
        w1e = a16.alloc(8 * 4096)
        w2e = a16.alloc(32 * 1024)
    he = a16.alloc(NFF * 512)
    hetmp = a16.alloc(512)
    ysb = a16.alloc32(4 * 512)
    bank_he = [PS[:, 2048:2560], PS[:, 2560:3072]]
    bank_y = [PS[:, 0:512], PS[:, 512:1024], PS[:, 1024:1536], PS[:, 1536:2048]]

    w_free = 0
    w1_free = 0
    acc_order = {}
    ysb_free = {}
    xfm_ld = [0, 0]
    elist = list(range(8)) + (["sh"] if which == "H" else [])
    for e in elist:
        p.wait("gpsimd", pe, w1_free)
        p.wait("sync", pe, w1_free)
        if which == "G":
            for t in range(8):
                p.dma("gpsimd", w1e[:, t * 512:(t + 1) * 512],
                      d["L1_attn_W_gate_e"][e, t * 128:(t + 1) * 128, :], qg)
                p.dma("gpsimd", wue[:, t * 512:(t + 1) * 512],
                      d["L1_attn_W_up_e"][e, t * 128:(t + 1) * 128, :], qg)
            p.wait("gpsimd", pe, w_free)
            for t in range(4):
                p.dma("gpsimd", w2e[:, t * 1024:(t + 1) * 1024],
                      d["L1_attn_W_down_e"][e, t * 128:(t + 1) * 128, :], qg)
            w_at = (qg.n, qs.n)
        else:
            if e == "sh":
                w1src, w2src, ro1, ro2 = d["sw1_16"], d["sw2_16"], 0, 0
            else:
                w1src, w2src = d["ew1_16"], d["ew2_16"]
                ro1, ro2 = e * 4096, e * 1024
            for t in range(8):
                p.dma("sync", w1e[:, t * 4096:(t + 1) * 4096],
                      w1src[ro1:ro1 + 4096, t * 128:(t + 1) * 128], qs,
                      transpose=True)
            p.wait("sync", pe, w_free)
            for t in range(32):
                p.dma("sync", w2e[:, t * 1024:(t + 1) * 1024],
                      w2src[ro2:ro2 + 1024, t * 128:(t + 1) * 128], qs,
                      transpose=True)
            w_at = (qg.n, qs.n)

        for c in range(NC5):
            buf = xfm[c % 2]
            if c == 0:
                p.wait("sync", pe, xfm_free[0])
                xfm_ld[0] = load_xfm(0, buf)
            ld = xfm_ld[c % 2]
            if c + 1 < NC5:
                p.wait("sync", pe, xfm_free[(c + 1) % 2])
                xfm_ld[(c + 1) % 2] = load_xfm(c + 1, xfm[(c + 1) % 2])
            p.wait("tensor", qs, ld)
            if c == 0:
                p.wait("tensor", qg, w_at[0])
                p.wait("tensor", qs, w_at[1])
                if e == 0:
                    p.wait("scalar", dve, routing_done)
                    p.wait("vector", dve, routing_done)
            he_evac = {}
            for ff in range(NFF):
                bnk = bank_he[ff % 2] if not glu else bank_he[0]
                if glu:
                    bup = bank_he[1]
                    if ff >= 1:
                        p.wait("tensor", dve, he_evac[ff - 1])
                    for t in range(8):
                        p.mm(bnk, w1e[:, t * 512 + ff * 128: t * 512 + (ff + 1) * 128],
                             buf[:, t * 512:(t + 1) * 512], start=(t == 0),
                             stop=(t == 7))
                    for t in range(8):
                        p.mm(bup, wue[:, t * 512 + ff * 128: t * 512 + (ff + 1) * 128],
                             buf[:, t * 512:(t + 1) * 512], start=(t == 0),
                             stop=(t == 7), ctr=pe if t == 7 else None)
                    p.wait("scalar", pe, pe.n)
                    p.act(hetmp, bnk, AF.Silu, ctr=act)
                    p.wait("vector", act, act.n)
                    p.v("tensor_mul", he[:, ff * 512:(ff + 1) * 512], hetmp, bup,
                        ctr=dve)
                    he_evac[ff] = dve.n
                else:
                    if ff >= 2:
                        p.wait("tensor", act, he_evac[ff - 2])
                    for t in range(8):
                        p.mm(bnk, w1e[:, t * 4096 + ff * 128: t * 4096 + (ff + 1) * 128],
                             buf[:, t * 512:(t + 1) * 512], start=(t == 0),
                             stop=(t == 7), ctr=pe if t == 7 else None)
                    p.wait("scalar", pe, pe.n)
                    p.act(he[:, ff * 512:(ff + 1) * 512], bnk, AF.Silu, ctr=act)
                    he_evac[ff] = act.n
            he_all = he_evac[NFF - 1]
            if c == NC5 - 1:
                w1_free = pe.n
            for dh in range(2):
                for ts in range(4):
                    it = c * 4 + ts
                    bnk = bank_y[ts]
                    if ts in ysb_free:
                        p.wait("tensor", qg, ysb_free[ts])
                        p.wait("scalar", qg, ysb_free[ts])
                        p.wait("vector", qg, ysb_free[ts])
                    for ff in range(NFF):
                        if ff == 0:
                            if glu:
                                p.wait("tensor", dve, he_all)
                            else:
                                p.wait("tensor", act, he_all)
                        p.mm(bnk, he[:, ff * 512 + ts * 128: ff * 512 + (ts + 1) * 128],
                             w2e[:, ff * 1024 + dh * 512: ff * 1024 + dh * 512 + 512],
                             start=(ff == 0), stop=(ff == NFF - 1),
                             ctr=pe if ff == NFF - 1 else None)
                    mmv = pe.n
                    ysl = ysb[:, ts * 512:(ts + 1) * 512]
                    if e == "sh":
                        p.wait("scalar", pe, mmv)
                        p.act(ysl, bnk, AF.Copy, ctr=act)
                        ev = ("act", act.n)
                    else:
                        csc = c32[:, it * 8 + e: it * 8 + e + 1]
                        if dh == 0:
                            p.wait("scalar", pe, mmv)
                            p.act(ysl, bnk, AF.Copy, scale=csc, ctr=act)
                            ev = ("act", act.n)
                        else:
                            p.wait("vector", pe, mmv)
                            p.v("tensor_scalar_mul", ysl, bnk, csc, ctr=dve)
                            ev = ("dve", dve.n)
                    p.wait("gpsimd", p.sems[ev[0]], ev[1])
                    if (it, dh) in acc_order:
                        p.wait("gpsimd", qg, acc_order[(it, dh)])
                    dst = acc_h[it * 128:(it + 1) * 128, dh * 512:(dh + 1) * 512]
                    p.dma("gpsimd", dst, ysl, qg, accum=(e != 0))
                    acc_order[(it, dh)] = qg.n
                    ysb_free[ts] = qg.n
            xfm_free[c % 2] = pe.n
        w_free = pe.n
    a16.pop()


def phase_add(mk, p, a16, out_name, a_name, b_name):
    d = mk.d
    qs, dve = p.sems["qs"], p.sems["dve"]
    a16.push()
    entry_guard(p)
    a_r = d[a_name].rearrange("(n p) d -> n p d", p=128)
    b_r = d[b_name].rearrange("(n p) d -> n p d", p=128)
    o_r = d[out_name].rearrange("(n p) d -> n p d", p=128)
    ab = [a16.alloc32(D) for _ in range(2)]
    bb = [a16.alloc32(D) for _ in range(2)]
    free3 = {}
    for i in range(NT):
        bi = i % 2
        if i == 0:
            p.wait("tensor", dve, dve.n)
            p.wait("tensor", p.sems["act"], p.sems["act"].n)
        p.mm(mk.PS[0:64, 0:64], mk.C["ident16"][:, 0:64], mk.C["ident16"][:, 0:64],
             start=True, stop=True)
        if i >= 2:
            p.wait("sync", dve, free3[i - 2][0])
            p.wait("sync", qs, free3[i - 2][1])
        p.dma("sync", ab[bi], a_r[i], qs)
        p.dma("sync", bb[bi], b_r[i], qs)
        ld = qs.n
        p.wait("vector", qs, ld)
        p.v("tensor_add", ab[bi], ab[bi], bb[bi], ctr=dve)
        dn = dve.n
        p.wait("sync", dve, dn)
        p.dma("sync", o_r[i], ab[bi], qs)
        free3[i] = (dn, qs.n)
    a16.pop()


def phase_combine2(mk, p, a16, out_name, x_name, s_name, f_name):
    d = mk.d
    qs, dve, act = p.sems["qs"], p.sems["dve"], p.sems["act"]
    a16.push()
    entry_guard(p)
    x_r = d[x_name].rearrange("(n p) d -> n p d", p=128)
    s_r = d[s_name].rearrange("(n p) d -> n p d", p=128)
    f_r = d[f_name].rearrange("(n p) d -> n p d", p=128)
    o_r = d[out_name].rearrange("(n p) d -> n p d", p=128)
    xb = [a16.alloc32(D) for _ in range(2)]
    sb_ = [a16.alloc32(D) for _ in range(2)]
    fb = [a16.alloc32(D) for _ in range(2)]
    gb = [a16.alloc32(D) for _ in range(2)]
    free3 = {}
    for i in range(NT):
        bi = i % 2
        if i == 0:
            p.wait("tensor", dve, dve.n)
            p.wait("tensor", p.sems["act"], p.sems["act"].n)
        p.mm(mk.PS[0:64, 0:64], mk.C["ident16"][:, 0:64], mk.C["ident16"][:, 0:64],
             start=True, stop=True)
        if i >= 2:
            p.wait("sync", dve, free3[i - 2][0])
            p.wait("sync", qs, free3[i - 2][1])
        p.dma("sync", xb[bi], x_r[i], qs)
        p.dma("sync", sb_[bi], s_r[i], qs)
        p.dma("sync", fb[bi], f_r[i], qs)
        ld = qs.n
        p.wait("scalar", qs, ld)
        p.act(gb[bi], fb[bi], AF.Silu, ctr=act)
        p.wait("vector", act, act.n)
        p.v("tensor_mul", gb[bi], gb[bi], sb_[bi], edge=True)
        p.v("tensor_add", gb[bi], gb[bi], xb[bi], ctr=dve)
        dn = dve.n
        p.wait("sync", dve, dn)
        p.dma("sync", o_r[i], gb[bi], qs)
        free3[i] = (dn, qs.n)
    a16.pop()


def kernel(**inputs):
    nc = bass.Bass("TRN2", target_bir_lowering=False)
    m = MK(nc, phases=("A", "B", "C", "D", "E", "F", "G", "H"))
    nc2 = m.build()
    in_maps = [make_inputs_per_core(inputs, b) for b in range(B)]
    res = run_bass_kernel_spmd(nc2, in_maps, core_ids=list(range(B)))
    return np.stack([res.results[b]["out"] for b in range(B)], axis=0)


# version 07
# speedup vs baseline: 1.1260x; 1.0511x over previous
import numpy as np
import concourse.bass as bass
import concourse.mybir as mybir
from concourse.bass_utils import run_bass_kernel_spmd

F32 = mybir.dt.float32
BF16 = mybir.dt.bfloat16
AF = mybir.ActivationFunctionType
ALU = mybir.AluOpType

import os
B, S, D = 8, int(os.environ.get("MK_S", "4096")), 1024
NT = S // 128
NC5 = S // 512

def host_constants():
    c = {}
    c["ident16"] = np.eye(128, dtype=np.float32)
    c["ident32"] = np.eye(128, dtype=np.float32)
    c["ones16"] = np.ones((128, 128), dtype=np.float32)
    c["ones32"] = np.ones((128, 128), dtype=np.float32)
    c["cum_ut32"] = np.triu(np.ones((128, 128), dtype=np.float32))
    for tag, base in (("g", 1e6), ("l", 1e4)):
        inv = 1.0 / base ** (np.arange(0, 64, 2, dtype=np.float32) / 64)
        fr = np.arange(S, dtype=np.float32)[:, None] * inv[None, :]
        cos, sin = np.cos(fr), np.sin(fr)
        Ct = np.zeros((S, 64), np.float32); St = np.zeros((S, 64), np.float32)
        Ct[:, 0::2] = cos; Ct[:, 1::2] = cos
        St[:, 0::2] = -sin; St[:, 1::2] = sin
        c["rope_c_" + tag] = Ct
        c["rope_s_" + tag] = St
    kr = np.arange(128)[:, None]; qf = np.arange(512)[None, :]
    mg = np.stack([(qf - kr >= o * 128) for o in range(4)], 0).astype(np.float32)
    c["maskg"] = mg.transpose(1, 0, 2).reshape(128, 4 * 512)
    ml = []
    for o in range(-3, 3):
        diff = qf - kr + o * 128
        ml.append(((diff >= 0) & (diff < 256)).astype(np.float32))
    c["maskl"] = np.stack(ml, 0).transpose(1, 0, 2).reshape(128, 6 * 512)
    fc = np.zeros((128, 16), np.float32)
    for i, v in enumerate(FCONST_VALS):
        fc[:, i] = v
    c["fconst32"] = fc
    idx = np.arange(128)
    c["m_lows32"] = (idx[:, None] > idx[None, :]).astype(np.float32)
    c["m_upi32"] = (idx[:, None] <= idx[None, :]).astype(np.float32)
    c["m_ups32"] = (idx[:, None] < idx[None, :]).astype(np.float32)
    return c

FCONST_VALS = [-8.0, 1e-6, 1.0, 1e-5, 0.5, -0.5, 2.0, -1.0, -2.4260151319598084]

CONST_SPECS = {
    "ident16": BF16, "ident32": F32, "ones16": BF16, "ones32": F32,
    "cum_ut32": F32, "fconst32": F32,
    "maskg": BF16, "maskl": BF16,
    "m_lows32": F32, "m_upi32": F32, "m_ups32": F32,
}
GLOBAL_CONSTS = ("ident16", "ident32", "ones16", "ones32", "cum_ut32", "fconst32", "m_lows32", "m_upi32", "m_ups32")
HBM_CONSTS = ["rope_c_g", "rope_s_g", "rope_c_l", "rope_s_l"]

HOSTC = host_constants()

class Ctr:
    __slots__ = ("name", "sem", "n", "step")
    def __init__(self, name, sem, step=1):
        self.name, self.sem, self.n, self.step = name, sem, 0, step

class Arena:
    def __init__(self, t, n):
        self.t, self.n, self.off, self.marks = t, n, 0, []
    def alloc(self, cols):
        cols = (cols + 15) // 16 * 16
        assert self.off + cols <= self.n, f"arena overflow {self.off}+{cols}>{self.n}"
        ap = self.t[:, self.off:self.off + cols]
        self.off += cols
        return ap
    def alloc32(self, cols):
        return self.alloc(2 * cols).bitcast(F32)
    def push(self): self.marks.append(self.off)
    def pop(self): self.off = self.marks.pop()

class P:
    def __init__(self, nc, cur, engines, sems):
        self.nc, self.cur, self.E, self.sems = nc, cur, engines, sems

    def wait(self, who, ctr, v=None):
        v = ctr.n if v is None else v
        if v > 0 and self.cur == who:
            self.E[who].wait_ge(ctr.sem, v)

    def _fin(self, who, instr, ctr):
        if ctr is not None:
            ctr.n += ctr.step
            if self.cur == who and instr is not None:
                instr.then_inc(ctr.sem, ctr.step)

    def dma(self, q, out, in_, ctr=None, transpose=False, accum=False, slow=False):
        i = None
        if self.cur == q:
            kw = {"accum_op": ALU.add} if accum else {}
            if slow:
                kw["allow_slow_non_contiguous"] = True
            i = self.E[q].dma_start(out=out, in_=in_, transpose=transpose, **kw)
        self._fin(q, i, ctr)

    def mm(self, out, lhsT, rhs, start, stop, ctr=None):
        i = None
        if self.cur == "tensor":
            i = self.E["tensor"].matmul(out, lhsT, rhs, start=start, stop=stop)
        self._fin("tensor", i, ctr)

    def trans(self, out, in_, ident, ctr=None):
        i = None
        if self.cur == "tensor":
            i = self.E["tensor"].transpose(out, in_, ident)
        self._fin("tensor", i, ctr)

    def act(self, out, in_, func, bias=0.0, scale=1.0, accum_out=None, ctr=None,
            edge=False):
        i = None
        if self.cur == "scalar":
            i = self.E["scalar"].activation(out, in_, func, bias=bias, scale=scale,
                                            accum_out=accum_out)
        c = ctr if ctr is not None else (self.sems["act"] if edge else None)
        self._fin("scalar", i, c)
        if edge:
            self.wait("scalar", self.sems["act"])

    def ve(self, eng, meth, *args, ctr=None, edge=False, **kw):
        i = None
        if self.cur == eng:
            i = getattr(self.E[eng], meth)(*args, **kw)
        nm = "dve" if eng == "vector" else "pool"
        c = ctr if ctr is not None else (self.sems[nm] if edge else None)
        self._fin(eng, i, c)
        if edge:
            self.wait(eng, self.sems[nm])

    def v(self, meth, *args, ctr=None, edge=False, **kw):
        self.ve("vector", meth, *args, ctr=ctr, edge=edge, **kw)

    def g(self, meth, *args, ctr=None, edge=False, **kw):
        self.ve("gpsimd", meth, *args, ctr=ctr, edge=edge, **kw)

    def barrier(self):
        for e in ("sync", "gpsimd", "tensor", "vector", "scalar"):
            if self.cur == e:
                for c in self.sems.values():
                    if c.n > 0:
                        self.E[e].wait_ge(c.sem, c.n)


def ev(ap):
    return ap.rearrange("p (a b) -> p a b", b=2)[:, :, 0:1]

def od(ap):
    return ap.rearrange("p (a b) -> p a b", b=2)[:, :, 1:2]


class MK:
    def __init__(self, nc, phases, dbg=()):
        self.nc, self.phases = nc, phases
        self.dbg = set(dbg)
        self.d = {}
        self.declare_dram()

    def declare_dram(self):
        nc, d = self.nc, self.d
        d["x0"] = nc.dram_tensor("x0", [S, D], F32, kind="ExternalInput")
        WSPECS = [
            ("L0_attn_norm1_w", [D]), ("L0_attn_norm2_w", [D]),
            ("L0_attn_g_wq", [D, D]), ("L0_attn_g_kv_down", [D, 256]),
            ("L0_attn_g_k_up", [256, 256]), ("L0_attn_g_v_up", [256, 256]),
            ("L0_attn_g_wo", [D, D]),
            ("L0_attn_l_wq", [D, D]), ("L0_attn_l_wk", [D, 256]),
            ("L0_attn_l_wv", [D, 256]), ("L0_attn_l_wo", [D, D]),
            ("L0_ffn_router_w", [D, 8]), ("L0_ffn_router_b", [8]),
            ("L0_ffn_W1", [8, D, 4096]), ("L0_ffn_b1", [8, 4096]),
            ("L0_ffn_W2", [8, 4096, D]), ("L0_ffn_b2", [8, D]),
            ("L1_attn_w_ln1", [D]), ("L1_attn_w_ln2", [D]),
            ("L1_attn_W_qkvz", [D, 3072]), ("L1_attn_W_ba", [D, 16]),
            ("L1_attn_conv_w", [2048, 4]), ("L1_attn_dt_bias", [8]),
            ("L1_attn_A_log", [8]), ("L1_attn_gate_norm_w", [128]),
            ("L1_attn_W_out", [D, D]), ("L1_attn_W_router", [D, 8]),
            ("L1_attn_W_gate_e", [8, D, 512]), ("L1_attn_W_up_e", [8, D, 512]),
            ("L1_attn_W_down_e", [8, 512, D]),
            ("L1_ffn_shared_w1", [4096, D]), ("L1_ffn_shared_w2", [D, 4096]),
            ("L1_ffn_experts_w1", [8, 4096, D]), ("L1_ffn_experts_w2", [8, D, 4096]),
            ("L1_ffn_gate_w", [8, D]),
        ]
        for name, shape in WSPECS:
            d[name] = nc.dram_tensor(name, shape, F32, kind="ExternalInput")
        for name, dt in CONST_SPECS.items():
            d[name] = nc.dram_tensor(name, list(HOSTC[name].shape), dt,
                                     kind="ExternalInput")
        for name in HBM_CONSTS:
            d[name] = nc.dram_tensor(name, list(HOSTC[name].shape), BF16,
                                     kind="ExternalInput")
        scr = [
            ("xfmdbg", [128, 4096], BF16),
            ("routedbg", [128, 64], F32),
            ("c32dbg", [128, NT * 8], F32), ("hedbg", [128, 32 * 512], BF16),
            ("xn32", [S, D], F32), ("xn16", [S, D], BF16), ("a016", [S, D], BF16),
            ("x132", [S, D], F32), ("a116", [S, D], BF16), ("s32", [S, D], F32),
            ("ffn0", [S, D], F32), ("X132", [S, D], F32),
            ("xn1_32", [S, D], F32), ("xn1_16", [S, D], BF16), ("h016", [S, D], BF16),
            ("x232", [S, D], F32), ("xh16", [S, D], BF16),
            ("moe1", [S, D], F32), ("ffn1", [S, D], F32), ("s132", [S, D], F32),
            ("sw1_16", [4096, D], BF16), ("sw2_16", [D, 4096], BF16),
            ("ew1_16", [8 * 4096, D], BF16), ("ew2_16", [8 * D, 4096], BF16),
            ("gw_16", [8, D], BF16),
        ]
        for name, shape, dt in scr:
            kind = "ExternalOutput" if name in self.dbg else "Internal"
            d[name] = nc.dram_tensor(name, shape, dt, kind=kind)
        d["out"] = nc.dram_tensor("out", [S, D], F32, kind="ExternalOutput")

    def build(self):
        nc = self.nc
        with (
            nc.sbuf_tensor("A16", [128, 99328], BF16) as A16,
            nc.sbuf_tensor("CB", [128, 2048], BF16) as CB,
            nc.psum_tensor("PS", [128, 3072], F32) as PS,
            nc.psum_tensor("PT", [128, 2048], BF16) as PT,
            nc.semaphore("c_pe") as s_pe,
            nc.semaphore("c_act") as s_act,
            nc.semaphore("c_dve") as s_dve,
            nc.semaphore("c_pool") as s_pool,
            nc.semaphore("q_sync") as s_qs,
            nc.semaphore("q_scal") as s_qc,
            nc.semaphore("q_gps") as s_qg,
            nc.Block() as block,
        ):
            self.A16, self.CB, self.PS, self.PT = A16, CB, PS, PT
            mk = self

            def run(cur, eng):
                sems = {
                    "pe": Ctr("pe", s_pe), "act": Ctr("act", s_act),
                    "dve": Ctr("dve", s_dve), "pool": Ctr("pool", s_pool),
                    "qs": Ctr("qs", s_qs, 16), "qc": Ctr("qc", s_qc, 16),
                    "qg": Ctr("qg", s_qg, 16),
                }
                emit_all(mk, P(self.nc, cur, {cur: eng}, sems))

            @block.sync
            def _(eng): run("sync", eng)
            @block.gpsimd
            def _(eng): run("gpsimd", eng)
            @block.tensor
            def _(eng): run("tensor", eng)
            @block.vector
            def _(eng): run("vector", eng)
            @block.scalar
            def _(eng): run("scalar", eng)
        return nc



def emit_rsqrt(p, dst, srcv, pre_scale=1.0, pre_bias=0.0, post_scale=1.0):
    import math
    act = p.sems["act"]
    p.act(dst, srcv, AF.Ln, scale=pre_scale, bias=pre_bias, edge=True)
    p.act(dst, dst, AF.Exp, scale=-0.5,
          bias=(0.0 if post_scale == 1.0 else math.log(post_scale)), ctr=act)

def entry_guard(p, queues=("sync", "gpsimd")):
    for q in queues:
        for cn in ("pe", "act", "dve", "pool", "qs", "qc", "qg"):
            p.wait(q, p.sems[cn])


def load_const(mk, p, ar, name):
    dt = CONST_SPECS[name]
    w = HOSTC[name].shape[1]
    buf = ar.alloc32(w) if dt == F32 else ar.alloc(w)
    p.dma("sync", buf, mk.d[name][:, :], p.sems["qs"])
    return buf

def load_row_bcast(mk, p, ar, src_ap, w):
    buf = ar.alloc32(w)
    p.dma("sync", buf, src_ap[None, :].to_broadcast([128, w]), p.sems["qs"])
    return buf

def emit_all(mk, p):
    a16 = Arena(mk.A16, mk.A16.shape[1])
    cst = Arena(mk.CB, mk.CB.shape[1])
    qs = p.sems["qs"]
    C = {name: load_const(mk, p, cst, name) for name in GLOBAL_CONSTS}
    C["consts_at"] = qs.n
    mk.C = C
    if not getattr(mk, "fconst_registered", False):
        for i, v in enumerate(FCONST_VALS):
            mk.nc.const_aps.register(F32, v, C["fconst32"][:, i:i + 1])
        mk.fconst_registered = True
    for e in ("tensor", "vector", "scalar", "gpsimd"):
        p.wait(e, qs, C["consts_at"])
    if "A" in mk.phases:
        phase_A(mk, p, a16)
    if "B" in mk.phases:
        phase_attn(mk, p, a16, "g")
    if "C" in mk.phases:
        phase_attn(mk, p, a16, "l")
    mk.prepass = prepass_slices(mk, p) if "H" in mk.phases else None
    if "D" in mk.phases:
        phase_moe(mk, p, a16, "L0")
    if mk.prepass:
        for em in mk.prepass:
            em()
        mk.prepass = None
    if "D" in mk.phases:
        phase_combine2(mk, p, a16, "X132", "x0", "s32", "ffn0")
    if "E" in mk.phases:
        phase_E(mk, p, a16)
    if "F" in mk.phases:
        phase_F(mk, p, a16)
    if "G" in mk.phases:
        phase_moe2(mk, p, a16, "G")
        phase_add(mk, p, a16, "s132", "x232", "moe1")
    if "H" in mk.phases:
        phase_moe2(mk, p, a16, "H")
        phase_combine2(mk, p, a16, "out", "X132", "s132", "ffn1")
    p.barrier()


def phase_A(mk, p, a16):
    d, C = mk.d, mk.C
    qs, dve, act = p.sems["qs"], p.sems["dve"], p.sems["act"]
    a16.push()
    n1bc = load_row_bcast(mk, p, a16, d["L0_attn_norm1_w"], D)
    xin = [a16.alloc32(D) for _ in range(2)]
    st8 = [a16.alloc32(12) for _ in range(2)]
    mv = [a16.alloc32(4) for _ in range(2)]
    xn32t = [a16.alloc32(D) for _ in range(2)]
    xn16t = [a16.alloc(D) for _ in range(2)]
    a016t = [a16.alloc(D) for _ in range(2)]

    x0r = d["x0"].rearrange("(n p) d -> n p d", p=128)
    xn32r = d["xn32"].rearrange("(n p) d -> n p d", p=128)
    xn16r = d["xn16"].rearrange("(n p) d -> n p d", p=128)
    a016r = d["a016"].rearrange("(n p) d -> n p d", p=128)

    free_at = {}
    p.wait("sync", qs, qs.n)
    for i in range(NT):
        if i == 0:
            p.wait("tensor", dve, dve.n)
            p.wait("tensor", act, act.n)
        p.mm(mk.PS[0:64, 0:64], C["ident16"][:, 0:64], C["ident16"][:, 0:64],
             start=True, stop=True)
        bi = i % 2
        if i >= 2:
            p.wait("sync", dve, free_at[i - 2][0])
            p.wait("sync", qs, free_at[i - 2][1])
        p.dma("sync", xin[bi], x0r[i], qs)
        ld = qs.n

        p.wait("vector", qs, ld)
        p.v("bn_stats", st8[bi][:, 0:6], xin[bi][:, 0:512], edge=True)
        p.v("bn_stats", st8[bi][:, 6:12], xin[bi][:, 512:1024], edge=True)
        p.v("bn_aggr", mv[bi][:, 0:2], st8[bi][:, 0:12], edge=True)
        p.wait("scalar", dve, dve.n)
        emit_rsqrt(p, mv[bi][:, 2:3], mv[bi][:, 1:2], pre_bias=1e-5)
        rstd_done = act.n
        p.v("tensor_scalar_add", mv[bi][:, 3:4], mv[bi][:, 1:2], 1e-5, edge=True)
        p.v("reciprocal", mv[bi][:, 3:4], mv[bi][:, 3:4], edge=True)
        p.v("tensor_mul", mv[bi][:, 3:4], mv[bi][:, 3:4], mv[bi][:, 1:2], edge=True)
        p.v("tensor_scalar_add", mv[bi][:, 3:4], mv[bi][:, 3:4], 1e-6, ctr=dve)
        p.wait("scalar", dve, dve.n)
        emit_rsqrt(p, mv[bi][:, 3:4], mv[bi][:, 3:4])
        rms2_done = act.n

        p.wait("vector", act, rstd_done)
        p.v("tensor_scalar", xn32t[bi], xin[bi], mv[bi][:, 0:1], mv[bi][:, 2:3],
            ALU.subtract, ALU.mult, edge=True)
        p.v("tensor_copy", xn16t[bi], xn32t[bi], ctr=dve)
        xn_done = dve.n
        p.wait("vector", act, rms2_done)
        p.v("scalar_tensor_tensor", a016t[bi], xn32t[bi], mv[bi][:, 3:4], n1bc,
            ALU.mult, ALU.mult, ctr=dve)
        a0_done = dve.n

        p.wait("sync", dve, xn_done)
        p.dma("sync", xn32r[i], xn32t[bi], qs)
        p.dma("sync", xn16r[i], xn16t[bi], qs)
        p.wait("sync", dve, a0_done)
        p.dma("sync", a016r[i], a016t[bi], qs)
        free_at[i] = (dve.n, qs.n)
    a16.pop()


def phase_attn(mk, p, a16, kind):
    d, C = mk.d, mk.C
    qs, qg = p.sems["qs"], p.sems["qg"]
    pe, act, dve = p.sems["pe"], p.sems["act"], p.sems["dve"]
    PS, PT = mk.PS, mk.PT
    a16.push()
    entry_guard(p)

    glob = kind == "g"
    src16 = d["a016"] if glob else d["a116"]
    wq = d["L0_attn_g_wq"] if glob else d["L0_attn_l_wq"]
    wo = d["L0_attn_g_wo"] if glob else d["L0_attn_l_wo"]

    wq_sb = a16.alloc(8 * D)
    for t in range(8):
        p.dma("gpsimd", wq_sb[:, t * D:(t + 1) * D], wq[t * 128:(t + 1) * 128, :], qg)
    if glob:
        kvd_sb = a16.alloc(8 * 256)
        ku_sb = a16.alloc(2 * 256)
        vu_sb = a16.alloc(2 * 256)
        for t in range(8):
            p.dma("gpsimd", kvd_sb[:, t * 256:(t + 1) * 256],
                  d["L0_attn_g_kv_down"][t * 128:(t + 1) * 128, :], qg)
        for t in range(2):
            p.dma("gpsimd", ku_sb[:, t * 256:(t + 1) * 256],
                  d["L0_attn_g_k_up"][t * 128:(t + 1) * 128, :], qg)
            p.dma("gpsimd", vu_sb[:, t * 256:(t + 1) * 256],
                  d["L0_attn_g_v_up"][t * 128:(t + 1) * 128, :], qg)
    else:
        wk_sb = a16.alloc(8 * 256)
        wv_sb = a16.alloc(8 * 256)
        for t in range(8):
            p.dma("gpsimd", wk_sb[:, t * 256:(t + 1) * 256],
                  d["L0_attn_l_wk"][t * 128:(t + 1) * 128, :], qg)
            p.dma("gpsimd", wv_sb[:, t * 256:(t + 1) * 256],
                  d["L0_attn_l_wv"][t * 128:(t + 1) * 128, :], qg)
    w_at = qg.n

    mtag = "maskg" if glob else "maskl"
    msk = a16.alloc(HOSTC[mtag].shape[1])
    p.dma("sync", msk, d[mtag][:, :], qs)
    rc = a16.alloc(NT * 64)
    rs = a16.alloc(NT * 64)
    p.dma("sync", rc.rearrange("p (n f) -> p n f", f=64),
          d["rope_c_" + kind].rearrange("(n p) f -> p n f", p=128), qs)
    p.dma("sync", rs.rearrange("p (n f) -> p n f", f=64),
          d["rope_s_" + kind].rearrange("(n p) f -> p n f", p=128), qs)
    const_at = qs.n

    afm = [a16.alloc(4096) for _ in range(2)]
    attn = a16.alloc(8 * S)
    a16.push()
    if glob:
        lat = a16.alloc(2 * S)
    qraw = a16.alloc(2 * S)
    qfmn = a16.alloc(2 * S)
    kvraw = a16.alloc(S)
    kfmn = a16.alloc(S)
    vaug = a16.alloc(NT * 66 + 16)
    tbuf = a16.alloc(128)
    rtmp = a16.alloc(512)
    rtmp4 = a16.alloc(512)
    rtmpq = a16.alloc(512)
    kvb = a16.alloc(512)
    q0b = a16.alloc(512)
    q1b = a16.alloc(512)
    nrm16 = a16.alloc32(20)
    nrm16b = a16.alloc32(20)
    nrm = a16.alloc32(8)
    prow = a16.alloc32(512)
    psb2 = a16.alloc(1024)
    usb = a16.alloc(512)

    bP0, bP1 = PS[:, 0:512], PS[:, 512:1024]
    bS0, bS1 = PS[:, 1024:1536], PS[:, 1536:2048]
    bU = PS[:, 2048:2560]
    bB = PS[:, 2560:3072]
    tp = [PT[:, 0:1024], PT[:, 1024:2048]]
    vaug_t = vaug[:, 0:NT * 66].rearrange("p (n w) -> p n w", w=66)

    def load_afm(c, buf):
        for t in range(8):
            p.dma("sync", buf[:, t * 512:(t + 1) * 512],
                  src16[c * 512:(c + 1) * 512, t * 128:(t + 1) * 128],
                  qs, transpose=True)
        return qs.n

    p.wait("sync", qs, const_at)

    for g in range(4):
        evac = {"P0": 0, "P1": 0}
        afm_read = [0, 0]
        for c in range(NC5):
            buf = afm[c % 2]
            p.wait("sync", pe, afm_read[c % 2])
            ld = load_afm(c, buf)
            p.wait("tensor", qs, ld)
            if c == 0 and g == 0:
                p.wait("tensor", qg, w_at)
            for ct in range(2):
                col0 = g * 256 + ct * 128
                bnk, key = (bP0, "P0") if ct == 0 else (bP1, "P1")
                p.wait("tensor", dve, evac[key])
                for t in range(8):
                    p.mm(bnk, wq_sb[:, t * D + col0: t * D + col0 + 128],
                         buf[:, t * 512:(t + 1) * 512], start=(t == 0), stop=(t == 7),
                         ctr=pe if t == 7 else None)
                p.wait("vector", pe, pe.n)
                p.v("tensor_copy", qraw[:, ct * S + c * 512: ct * S + c * 512 + 512],
                    bnk, ctr=dve)
                evac[key] = dve.n
            if not glob:
                p.wait("tensor", dve, evac["P0"])
                for t in range(8):
                    p.mm(bP0[0:64, :], wk_sb[:, t * 256 + g * 64: t * 256 + g * 64 + 64],
                         buf[:, t * 512:(t + 1) * 512], start=(t == 0), stop=(t == 7))
                for t in range(8):
                    p.mm(bP0[64:128, :], wv_sb[:, t * 256 + g * 64: t * 256 + g * 64 + 64],
                         buf[:, t * 512:(t + 1) * 512], start=(t == 0), stop=(t == 7),
                         ctr=pe if t == 7 else None)
                p.wait("vector", pe, pe.n)
                p.v("tensor_copy", kvraw[:, c * 512:(c + 1) * 512], bP0, ctr=dve)
                evac["P0"] = dve.n
            if glob and g == 0:
                for ct in range(2):
                    bnk, key = (bP0, "P0") if ct == 0 else (bP1, "P1")
                    p.wait("tensor", dve, evac[key])
                    for t in range(8):
                        p.mm(bnk, kvd_sb[:, t * 256 + ct * 128: t * 256 + ct * 128 + 128],
                             buf[:, t * 512:(t + 1) * 512], start=(t == 0), stop=(t == 7),
                             ctr=pe if t == 7 else None)
                    p.wait("vector", pe, pe.n)
                    p.v("tensor_copy", lat[:, ct * S + c * 512: ct * S + c * 512 + 512],
                        bnk, ctr=dve)
                    evac[key] = dve.n
            afm_read[c % 2] = pe.n

        for c in range(NC5):
            if not glob:
                break
            p.wait("tensor", dve, max(evac["P0"], evac["P1"]))
            for t in range(2):
                p.mm(bP0[0:64, :], ku_sb[:, t * 256 + g * 64: t * 256 + g * 64 + 64],
                     lat[:, t * S + c * 512: t * S + c * 512 + 512],
                     start=(t == 0), stop=(t == 1))
            for t in range(2):
                p.mm(bP0[64:128, :], vu_sb[:, t * 256 + g * 64: t * 256 + g * 64 + 64],
                     lat[:, t * S + c * 512: t * S + c * 512 + 512],
                     start=(t == 0), stop=(t == 1), ctr=pe if t == 1 else None)
            p.wait("vector", pe, pe.n)
            p.v("tensor_copy", kvraw[:, c * 512:(c + 1) * 512], bP0, ctr=dve)
            evac["P0"] = dve.n

        p.v("memset", vaug_t[:, :, 64:65], 1.0, ctr=dve)
        p.wait("tensor", dve, dve.n)

        for scn in range(NT // 4):
            i0 = scn * 4
            p.wait("tensor", dve, dve.n)
            for j in range(4):
                p.trans(tp[0][:, j * 128:(j + 1) * 128],
                        kvraw[:, (i0 + j) * 128:(i0 + j + 1) * 128], C["ident16"])
            for j in range(4):
                p.trans(tp[0][:, 512 + j * 128:512 + (j + 1) * 128],
                        qraw[:, (i0 + j) * 128:(i0 + j + 1) * 128], C["ident16"])
            for j in range(4):
                p.trans(tp[1][:, j * 128:(j + 1) * 128],
                        qraw[:, S + (i0 + j) * 128: S + (i0 + j + 1) * 128],
                        C["ident16"], ctr=pe if j == 3 else None)
            p.wait("vector", pe, pe.n)
            p.v("tensor_copy", kvb, tp[0][:, 0:512], edge=True)
            p.v("tensor_copy", q0b, tp[0][:, 512:1024], edge=True)
            p.v("tensor_copy", q1b, tp[1][:, 0:512], edge=True)
            p.v("tensor_copy",
                vaug[:, i0 * 66:(i0 + 4) * 66].rearrange("p (n w) -> p n w", w=66)[:, :, 0:64],
                kvb.rearrange("p (n w) -> p n w", w=128)[:, :, 64:128], ctr=dve)
            p.wait("vector", dve, dve.n)
            rcs = rc[:, i0 * 64:(i0 + 4) * 64].rearrange("p (n f) -> p n f", f=64)
            rss = rs[:, i0 * 64:(i0 + 4) * 64].rearrange("p (n f) -> p n f", f=64)
            kview = kvb.rearrange("p (n w) -> p n w", w=128)[:, :, 0:64]
            kev = kview.rearrange("p n (a b) -> p n a b", b=2)
            rt4 = rtmp4[:, 0:256].rearrange("p (n f) -> p n f", f=64)
            p.v("tensor_copy", rt4.rearrange("p n (a b) -> p n a b", b=2)[:, :, :, 0:1],
                kev[:, :, :, 1:2], edge=True)
            p.v("tensor_copy", rt4.rearrange("p n (a b) -> p n a b", b=2)[:, :, :, 1:2],
                kev[:, :, :, 0:1], edge=True)
            p.v("tensor_tensor", rt4, rt4, rss, ALU.mult, edge=True)
            p.v("tensor_tensor", kview, kview, rcs, ALU.mult, edge=True)
            p.v("tensor_tensor", kview, kview, rt4, ALU.add, edge=True)
            rc2 = rcs.rearrange("p n (o f) -> p n o f", o=1).to_broadcast([128, 4, 2, 64])
            rs2 = rss.rearrange("p n (o f) -> p n o f", o=1).to_broadcast([128, 4, 2, 64])
            for qb in (q0b, q1b):
                qm = qb.rearrange("p (m f) -> p m f", f=64)
                qv = qb.rearrange("p (n h f) -> p n h f", h=2, f=64)
                rqm = rtmpq.rearrange("p (m f) -> p m f", f=64)
                rqv = rtmpq.rearrange("p (n h f) -> p n h f", h=2, f=64)
                p.v("tensor_copy", rqm.rearrange("p m (a b) -> p m a b", b=2)[:, :, :, 0:1],
                    qm.rearrange("p m (a b) -> p m a b", b=2)[:, :, :, 1:2],
                    edge=True)
                p.v("tensor_copy", rqm.rearrange("p m (a b) -> p m a b", b=2)[:, :, :, 1:2],
                    qm.rearrange("p m (a b) -> p m a b", b=2)[:, :, :, 0:1],
                    edge=True)
                p.v("tensor_tensor", rqv, rqv, rs2, ALU.mult, edge=True)
                p.v("tensor_tensor", qv, qv, rc2, ALU.mult, edge=True)
                p.v("tensor_tensor", qb, qb, rtmpq, ALU.add, edge=True)
            p.wait("scalar", dve, dve.n)
            for j in range(4):
                p.act(rtmp[:, 0:64], kvb[:, j * 128:j * 128 + 64], AF.Square,
                      accum_out=nrm16[:, j:j + 1],
                      ctr=act if j == 3 else None)
            for j in range(4):
                for hh in range(2):
                    p.act(rtmp[:, 0:64], q0b[:, j * 128 + hh * 64:j * 128 + (hh + 1) * 64],
                          AF.Square, accum_out=nrm16[:, 4 + j * 2 + hh:5 + j * 2 + hh],
                          ctr=act if (j == 3 and hh == 1) else None)
            for j in range(4):
                for hh in range(2):
                    p.act(rtmp[:, 0:64], q1b[:, j * 128 + hh * 64:j * 128 + (hh + 1) * 64],
                          AF.Square, accum_out=nrm16[:, 12 + j * 2 + hh:13 + j * 2 + hh],
                          ctr=act)
            emit_rsqrt(p, nrm16b[:, 0:20], nrm16[:, 0:20], pre_scale=1.0 / 64, pre_bias=1e-6)
            p.wait("vector", act, act.n)
            for j in range(4):
                p.v("tensor_scalar_mul", kvb[:, j * 128:j * 128 + 64],
                    kvb[:, j * 128:j * 128 + 64], nrm16b[:, j:j + 1])
            for j in range(4):
                for hh in range(2):
                    p.v("tensor_scalar_mul", q0b[:, j * 128 + hh * 64:j * 128 + (hh + 1) * 64],
                        q0b[:, j * 128 + hh * 64:j * 128 + (hh + 1) * 64],
                        nrm16b[:, 4 + j * 2 + hh:5 + j * 2 + hh])
                    p.v("tensor_scalar_mul", q1b[:, j * 128 + hh * 64:j * 128 + (hh + 1) * 64],
                        q1b[:, j * 128 + hh * 64:j * 128 + (hh + 1) * 64],
                        nrm16b[:, 12 + j * 2 + hh:13 + j * 2 + hh], ctr=dve)
            p.wait("tensor", dve, dve.n)
            for j in range(4):
                p.trans(tp[0][0:64, j * 128:(j + 1) * 128],
                        kvb[:, j * 128:j * 128 + 64], C["ident16"])
                p.trans(tp[0][64:128, 512 + j * 128:512 + (j + 1) * 128],
                        kvb[:, j * 128:j * 128 + 64], C["ident16"])
                p.trans(tp[1][:, j * 128:(j + 1) * 128],
                        q0b[:, j * 128:(j + 1) * 128], C["ident16"])
                p.trans(tp[1][:, 512 + j * 128:512 + (j + 1) * 128],
                        q1b[:, j * 128:(j + 1) * 128], C["ident16"],
                        ctr=pe if j == 3 else None)
            p.wait("vector", pe, pe.n)
            p.v("tensor_copy", kfmn[0:64, i0 * 128:(i0 + 4) * 128], tp[0][0:64, 0:512],
                edge=True)
            p.v("tensor_copy", kfmn[64:128, i0 * 128:(i0 + 4) * 128],
                tp[0][64:128, 512:1024], edge=True)
            p.v("tensor_copy", qfmn[:, i0 * 128:(i0 + 4) * 128], tp[1][:, 0:512],
                edge=True)
            p.v("tensor_copy", qfmn[:, S + i0 * 128: S + (i0 + 4) * 128],
                tp[1][:, 512:1024], ctr=dve)
            p.wait("tensor", dve, dve.n)
        prep_done = dve.n

        p.wait("tensor", dve, prep_done)
        for hl in range(4):
            h = 4 * g + hl
            pt, base = hl // 2, (hl % 2) * 64
            for qc in range(NC5):
                if glob:
                    jlist = list(range(0, 4 * qc + 4))
                else:
                    jlist = [j for j in range(4 * qc - 2, 4 * qc + 4) if j >= 0]
                pairs = [jlist[i:i + 2] for i in range(0, len(jlist), 2)]
                exp_at = {}
                umm_at = {}
                for pi, pj in enumerate(pairs):
                    if pi >= 1:
                        p.wait("tensor", act, exp_at[pi - 1])
                    for jj, j in enumerate(pj):
                        p.mm((bS0, bS1)[jj],
                             kfmn[base:base + 64, j * 128:(j + 1) * 128],
                             qfmn[base:base + 64, pt * S + qc * 512: pt * S + qc * 512 + 512],
                             start=True, stop=True,
                             ctr=pe if jj == len(pj) - 1 else None)
                    p.wait("scalar", pe, pe.n)
                    if pi >= 1:
                        p.wait("scalar", pe, umm_at[pi - 1])
                    wid = 512 * len(pj)
                    p.act(psb2[:, 0:wid], PS[:, 1024:1024 + wid], AF.Exp,
                          scale=0.125, bias=-8.0, ctr=act)
                    exp_at[pi] = act.n
                    anymask = False
                    for jj, j in enumerate(pj):
                        masked = (j - 4 * qc >= 0) if glob else True
                        if masked:
                            mo = (j - 4 * qc) if glob else (4 * qc - j + 3)
                            p.wait("vector", act, act.n)
                            p.v("tensor_mul", psb2[:, jj * 512:(jj + 1) * 512],
                                psb2[:, jj * 512:(jj + 1) * 512],
                                msk[:, mo * 512:(mo + 1) * 512], ctr=dve)
                            anymask = True
                    if anymask:
                        p.wait("tensor", dve, dve.n)
                    else:
                        p.wait("tensor", act, act.n)
                    idx0 = pi * 2
                    for jj, j in enumerate(pj):
                        p.mm(bU[0:65, :], vaug[:, j * 66: j * 66 + 65],
                             psb2[:, jj * 512:(jj + 1) * 512],
                             start=(idx0 + jj == 0),
                             stop=(idx0 + jj == len(jlist) - 1),
                             ctr=pe if jj == len(pj) - 1 else None)
                    umm_at[pi] = pe.n
                p.wait("vector", pe, pe.n)
                p.v("reciprocal", prow[64:65, :], bU[64:65, :], ctr=dve)
                p.wait("tensor", dve, dve.n)
                p.mm(bB[0:64, :], C["ones32"][64:65, 0:64], prow[64:65, :],
                     start=True, stop=True, ctr=pe)
                p.wait("scalar", pe, umm_at[len(pairs) - 1])
                p.act(usb[0:64, :], bU[0:64, :], AF.Identity, ctr=act)
                p.wait("vector", act, act.n)
                p.wait("vector", pe, pe.n)
                th, bh = h // 2, (h % 2) * 64
                if bh == 0:
                    p.v("tensor_mul",
                        attn[0:64, th * S + qc * 512: th * S + qc * 512 + 512],
                        usb[0:64, :], bB[0:64, :], ctr=dve)
                    p.wait("tensor", dve, dve.n)
                else:
                    p.v("tensor_mul", usb[0:64, :], usb[0:64, :], bB[0:64, :],
                        ctr=dve)
                    p.wait("tensor", dve, dve.n)
                    p.mm(bB[64:128, :], C["ident16"][0:64, 0:64], usb[0:64, :],
                         start=True, stop=True, ctr=pe)
                    p.wait("vector", pe, pe.n)
                    p.v("tensor_copy",
                        attn[64:128, th * S + qc * 512: th * S + qc * 512 + 512],
                        bB[64:128, :], ctr=dve)
                    p.wait("tensor", dve, dve.n)

    a16.pop()
    entry_guard(p)
    wo_sb = a16.alloc(8 * D)
    for t in range(8):
        p.dma("gpsimd", wo_sb[:, t * D:(t + 1) * D], wo[t * 128:(t + 1) * 128, :], qg)
    wo_at = qg.n
    if glob:
        n2bc = load_row_bcast(mk, p, a16, d["L0_attn_norm2_w"], D)
    res32 = d["xn32"] if glob else d["x132"]
    out32 = d["x132"] if glob else d["s32"]
    res_r = res32.rearrange("(n p) d -> n p d", p=128)
    out_r = out32.rearrange("(n p) d -> n p d", p=128)
    a116r = d["a116"].rearrange("(n p) d -> n p d", p=128)

    xres = [a16.alloc32(D) for _ in range(2)]
    xout = [a16.alloc32(D) for _ in range(2)]
    a1t = [a16.alloc(D) for _ in range(2)]
    sq = a16.alloc32(D)
    mv2 = a16.alloc32(4)

    p.wait("tensor", qg, wo_at)
    free2 = {}
    for i in range(NT):
        bi = i % 2
        if i >= 2:
            p.wait("sync", dve, free2[i - 2][0])
            p.wait("sync", qs, free2[i - 2][1])
        p.dma("sync", xres[bi], res_r[i], qs)
        ld = qs.n
        for half in range(2):
            bnk = bP0 if half == 0 else bP1
            if i >= 1:
                p.wait("tensor", dve, free2.get((i - 1, half), 0))
            for t in range(8):
                p.mm(bnk, attn[:, t * S + i * 128:(t * S + (i + 1) * 128)],
                     wo_sb[:, t * D + half * 512: t * D + half * 512 + 512],
                     start=(t == 0), stop=(t == 7), ctr=pe if t == 7 else None)
            p.wait("vector", pe, pe.n)
            p.wait("vector", qs, ld)
            p.v("tensor_add", xout[bi][:, half * 512:(half + 1) * 512],
                xres[bi][:, half * 512:(half + 1) * 512], bnk, ctr=dve)
            free2[(i, half)] = dve.n
        xdone = dve.n
        p.wait("sync", dve, xdone)
        p.dma("sync", out_r[i], xout[bi], qs)
        st_at = qs.n
        if glob:
            p.wait("scalar", dve, xdone)
            p.act(sq, xout[bi], AF.Square, accum_out=mv2[:, 0:1], edge=True)
            emit_rsqrt(p, mv2[:, 1:2], mv2[:, 0:1], pre_scale=1.0 / D,
                       pre_bias=1e-6)
            p.wait("vector", act, act.n)
            p.v("scalar_tensor_tensor", a1t[bi], xout[bi], mv2[:, 1:2], n2bc,
                ALU.mult, ALU.mult, ctr=dve)
            p.wait("sync", dve, dve.n)
            p.dma("sync", a116r[i], a1t[bi], qs)
        free2[i] = (dve.n, qs.n)
    a16.pop()


def make_inputs_per_core(inputs, core_b):
    import ml_dtypes
    m = {"x0": np.ascontiguousarray(np.asarray(inputs["L0_attn_hidden"])[core_b],
                                    dtype=np.float32)}
    for k, v in inputs.items():
        if k != "L0_attn_hidden":
            m[k] = np.ascontiguousarray(np.asarray(v, dtype=np.float32))
    for name, dt in CONST_SPECS.items():
        m[name] = (HOSTC[name].astype(ml_dtypes.bfloat16) if dt == BF16
                   else HOSTC[name])
    for name in HBM_CONSTS:
        m[name] = HOSTC[name].astype(ml_dtypes.bfloat16)
    return m


def run_phases(inputs, phases, dbg=(), trace=False):
    nc = bass.Bass("TRN2", target_bir_lowering=False)
    mk = MK(nc, phases, dbg=dbg)
    nc2 = mk.build()
    in_maps = [make_inputs_per_core(inputs, b) for b in range(B)]
    res = run_bass_kernel_spmd(nc2, in_maps, core_ids=list(range(B)), trace=trace)
    return res


def phase_moe(mk, p, a16, which):
    d, C = mk.d, mk.C
    qs, qg = p.sems["qs"], p.sems["qg"]
    pe, act, dve = p.sems["pe"], p.sems["act"], p.sems["dve"]
    PS = mk.PS
    a16.push()
    entry_guard(p)

    src16 = d["xn16"]
    Wr, br = d["L0_ffn_router_w"], d["L0_ffn_router_b"]
    W1, W2 = d["L0_ffn_W1"], d["L0_ffn_W2"]
    b1, b2 = d["L0_ffn_b1"], d["L0_ffn_b2"]
    acc_h = d["ffn0"]

    c32 = a16.alloc32(NT * 8)
    xfm = [a16.alloc(4096) for _ in range(2)]
    a16.push()
    rw = a16.alloc(8 * 8)
    for t in range(8):
        p.dma("gpsimd", rw[:, t * 8:(t + 1) * 8], Wr[t * 128:(t + 1) * 128, :], qg)
    rb = a16.alloc(16)
    p.dma("gpsimd", rb[0:1, 0:8], br[None, :], qg)
    rw_at = qg.n
    m8 = a16.alloc32(8)
    lgt = a16.alloc32(8)
    wv = a16.alloc32(4)
    ctmp = a16.alloc32(8)
    lgts = a16.alloc32(32)
    m8s = a16.alloc32(32)
    bR = PS[:, 0:512]

    def load_xfm(c, buf):
        for t in range(8):
            p.dma("sync", buf[:, t * 512:(t + 1) * 512],
                  src16[c * 512:(c + 1) * 512, t * 128:(t + 1) * 128],
                  qs, transpose=True)
        return qs.n

    xfm_free = [0, 0]
    for c in range(NC5):
        buf = xfm[c % 2]
        p.wait("sync", pe, xfm_free[c % 2])
        ld = load_xfm(c, buf)
        if "xfmdbg" in mk.dbg and c == 0:
            p.dma("sync", d["xfmdbg"][:, :], buf, qs)
        p.wait("tensor", qs, ld)
        if c == 0:
            p.wait("tensor", qg, rw_at)
        for ts in range(4):
            it = c * 4 + ts
            p.wait("tensor", dve, dve.n)
            for t in range(8):
                p.mm(bR[:, 0:8], buf[:, t * 512 + ts * 128: t * 512 + (ts + 1) * 128],
                     rw[:, t * 8:(t + 1) * 8], start=(t == 0), stop=False)
            p.mm(bR[:, 0:8], C["ones16"][0:1, 0:128], rb[0:1, 0:8],
                 start=False, stop=True, ctr=pe)
            p.wait("vector", pe, pe.n)
            p.v("tensor_copy", lgt, bR[:, 0:8], edge=True)
            p.v("max", m8, lgt, edge=True)
            p.v("tensor_sub", wv[:, 0:1], m8[:, 0:1], m8[:, 1:2], ctr=dve)
            p.wait("scalar", dve, dve.n)
            p.act(wv[:, 1:2], wv[:, 0:1], AF.Sigmoid, ctr=act)
            p.wait("vector", act, act.n)
            p.v("tensor_scalar", wv[:, 2:3], wv[:, 1:2], -1.0, 1.0, ALU.mult,
                ALU.add, edge=True)
            p.v("tensor_scalar", ctmp, lgt, m8[:, 0:1], wv[:, 1:2],
                ALU.is_equal, ALU.mult, edge=True)
            p.v("tensor_scalar", c32[:, it * 8:(it + 1) * 8], lgt, m8[:, 1:2],
                wv[:, 2:3], ALU.is_equal, ALU.mult, edge=True)
            p.v("tensor_add", c32[:, it * 8:(it + 1) * 8],
                c32[:, it * 8:(it + 1) * 8], ctmp, ctr=dve)
            if "routedbg" in mk.dbg and c == 0:
                p.v("tensor_copy", lgts[:, ts * 8:(ts + 1) * 8], lgt, edge=True)
                p.v("tensor_copy", m8s[:, ts * 8:(ts + 1) * 8], m8, ctr=dve)
                if ts == 3:
                    p.wait("sync", dve, dve.n)
                    p.dma("sync", d["routedbg"][:, 0:32], lgts, qs)
                    p.dma("sync", d["routedbg"][:, 32:64], m8s, qs)
        xfm_free[c % 2] = pe.n
    routing_done = p.sems["dve"].n
    a16.pop()
    p.wait("gpsimd", pe, pe.n)
    p.wait("gpsimd", dve, dve.n)
    p.wait("gpsimd", act, act.n)
    if "c32dbg" in mk.dbg:
        p.wait("sync", p.sems["dve"], routing_done)
        p.dma("sync", d["c32dbg"][:, :], c32, qs)

    w1e = a16.alloc(8 * 4096)
    w2e = a16.alloc(32 * 1024)
    b1sb = a16.alloc32(32)
    b2sb = a16.alloc(1024)
    he = a16.alloc(32 * 512)
    ysb = a16.alloc32(4 * 512)
    bank_he = [PS[:, 2048:2560], PS[:, 2560:3072]]
    bank_y = [PS[:, 0:512], PS[:, 512:1024], PS[:, 1024:1536], PS[:, 1536:2048]]

    acc_r = acc_h.rearrange("s d -> s d")
    w_free = 0
    w1_free = 0
    acc_order = {}
    ysb_free = {}
    xfm_ld = [0, 0]
    for e in range(8):
        p.wait("gpsimd", pe, w1_free)
        for t in range(8):
            p.dma("gpsimd", w1e[:, t * 4096:(t + 1) * 4096],
                  W1[e, t * 128:(t + 1) * 128, :], qg)
        p.wait("gpsimd", pe, w_free)
        for t in range(32):
            p.dma("gpsimd", w2e[:, t * 1024:(t + 1) * 1024],
                  W2[e, t * 128:(t + 1) * 128, :], qg)
        p.dma("gpsimd", b2sb[0:1, :], b2[e][None, :], qg)
        w_at = qg.n
        if getattr(mk, "prepass", None):
            mk.prepass.pop(0)()
            if not mk.prepass:
                mk.prepass = None
        p.dma("sync", b1sb, b1[e].rearrange("(t p) -> p t", p=128), qs, slow=True)
        b1_at = qs.n

        for c in range(NC5):
            buf = xfm[c % 2]
            if c == 0:
                p.wait("sync", pe, xfm_free[0])
                ld = load_xfm(0, buf)
                xfm_ld[0] = ld
            ld = xfm_ld[c % 2]
            if c + 1 < NC5:
                p.wait("sync", pe, xfm_free[(c + 1) % 2])
                xfm_ld[(c + 1) % 2] = load_xfm(c + 1, xfm[(c + 1) % 2])
            p.wait("tensor", qs, ld)
            if c == 0:
                p.wait("tensor", qg, w_at)
                p.wait("scalar", qs, b1_at)
                if e == 0:
                    p.wait("scalar", dve, routing_done)
            he_evac = {}
            for ff in range(32):
                bnk = bank_he[ff % 2]
                if ff >= 2:
                    p.wait("tensor", act, he_evac[ff - 2])
                for t in range(8):
                    p.mm(bnk, w1e[:, t * 4096 + ff * 128: t * 4096 + (ff + 1) * 128],
                         buf[:, t * 512:(t + 1) * 512], start=(t == 0), stop=(t == 7),
                         ctr=pe if t == 7 else None)
                p.wait("scalar", pe, pe.n)
                p.act(he[:, ff * 512:(ff + 1) * 512], bnk, AF.Relu,
                      bias=b1sb[:, ff:ff + 1], ctr=act)
                he_evac[ff] = act.n
            if "hedbg" in mk.dbg and e == 0 and c == 0:
                p.wait("sync", p.sems["act"], p.sems["act"].n)
                p.dma("sync", d["hedbg"][:, :], he[:, 0:32 * 512], qs)
            if c == NC5 - 1:
                w1_free = pe.n
            for dh in range(2):
                for ts in range(4):
                    it = c * 4 + ts
                    bnk = bank_y[ts]
                    if ts in ysb_free:
                        p.wait("tensor", qg, ysb_free[ts])
                        p.wait("scalar", qg, ysb_free[ts])
                        p.wait("vector", qg, ysb_free[ts])
                    for ff in range(32):
                        if ff == 0:
                            p.wait("tensor", act, he_evac[31])
                        p.mm(bnk, he[:, ff * 512 + ts * 128: ff * 512 + (ts + 1) * 128],
                             w2e[:, ff * 1024 + dh * 512: ff * 1024 + dh * 512 + 512],
                             start=(ff == 0), stop=False)
                    p.mm(bnk, C["ones16"][0:1, 0:128], b2sb[0:1, dh * 512: dh * 512 + 512],
                         start=False, stop=True, ctr=pe)
                    mmv = pe.n
                    ysl = ysb[:, ts * 512:(ts + 1) * 512]
                    csc = c32[:, it * 8 + e: it * 8 + e + 1]
                    if dh == 0:
                        p.wait("scalar", pe, mmv)
                        p.act(ysl, bnk, AF.Copy, scale=csc, ctr=act)
                        ev = ("scalar", act.n)
                    else:
                        p.wait("vector", pe, mmv)
                        p.v("tensor_scalar_mul", ysl, bnk, csc, ctr=dve)
                        ev = ("vector", dve.n)
                    p.wait("gpsimd", p.sems[{"scalar": "act", "vector": "dve"}[ev[0]]],
                           ev[1])
                    if (it, dh) in acc_order:
                        p.wait("gpsimd", qg, acc_order[(it, dh)])
                    dst = acc_h[it * 128:(it + 1) * 128, dh * 512:(dh + 1) * 512]
                    p.dma("gpsimd", dst, ysl, qg, accum=(e > 0))
                    acc_order[(it, dh)] = qg.n
                    ysb_free[ts] = qg.n
            xfm_free[c % 2] = pe.n
        w_free = pe.n
    a16.pop()


def phase_combine(mk, p, a16, which):
    d, C = mk.d, mk.C
    qs, dve, act = p.sems["qs"], p.sems["dve"], p.sems["act"]
    a16.push()
    entry_guard(p)
    x0r = d["x0"].rearrange("(n p) d -> n p d", p=128)
    s_r = d["s32"].rearrange("(n p) d -> n p d", p=128)
    f_r = d["ffn0"].rearrange("(n p) d -> n p d", p=128)
    o_r = d["X132"].rearrange("(n p) d -> n p d", p=128)
    xb = [a16.alloc32(D) for _ in range(2)]
    sb_ = [a16.alloc32(D) for _ in range(2)]
    fb = [a16.alloc32(D) for _ in range(2)]
    gb = [a16.alloc32(D) for _ in range(2)]
    free3 = {}
    for i in range(NT):
        bi = i % 2
        if i >= 2:
            p.wait("sync", dve, free3[i - 2][0])
            p.wait("sync", qs, free3[i - 2][1])
        p.dma("sync", xb[bi], x0r[i], qs)
        p.dma("sync", sb_[bi], s_r[i], qs)
        p.dma("sync", fb[bi], f_r[i], qs)
        ld = qs.n
        p.wait("scalar", qs, ld)
        p.act(gb[bi], fb[bi], AF.Silu, ctr=act)
        p.wait("vector", act, act.n)
        p.v("tensor_mul", gb[bi], gb[bi], sb_[bi], edge=True)
        p.v("tensor_add", gb[bi], gb[bi], xb[bi], ctr=dve)
        dn = dve.n
        p.wait("sync", dve, dn)
        p.dma("sync", o_r[i], gb[bi], qs)
        free3[i] = (dn, qs.n)
    a16.pop()


def phase_E(mk, p, a16):
    d, C = mk.d, mk.C
    qs, dve, act = p.sems["qs"], p.sems["dve"], p.sems["act"]
    a16.push()
    entry_guard(p)
    wbc = load_row_bcast(mk, p, a16, d["L1_attn_w_ln1"], D)
    p.wait("vector", qs, qs.n)
    p.v("tensor_scalar_add", wbc, wbc, 1.0, ctr=dve)
    xin = [a16.alloc32(D) for _ in range(2)]
    st8 = [a16.alloc32(12) for _ in range(2)]
    mv = [a16.alloc32(4) for _ in range(2)]
    xn32t = [a16.alloc32(D) for _ in range(2)]
    xn16t = [a16.alloc(D) for _ in range(2)]
    h0t = [a16.alloc(D) for _ in range(2)]

    x0r = d["X132"].rearrange("(n p) d -> n p d", p=128)
    xn32r = d["xn1_32"].rearrange("(n p) d -> n p d", p=128)
    xn16r = d["xn1_16"].rearrange("(n p) d -> n p d", p=128)
    h0r = d["h016"].rearrange("(n p) d -> n p d", p=128)

    free_at = {}
    p.wait("sync", dve, dve.n)
    for i in range(NT):
        if i == 0:
            p.wait("tensor", dve, dve.n)
            p.wait("tensor", act, act.n)
        p.mm(mk.PS[0:64, 0:64], C["ident16"][:, 0:64], C["ident16"][:, 0:64],
             start=True, stop=True)
        bi = i % 2
        if i >= 2:
            p.wait("sync", dve, free_at[i - 2][0])
            p.wait("sync", qs, free_at[i - 2][1])
        p.dma("sync", xin[bi], x0r[i], qs)
        ld = qs.n
        p.wait("vector", qs, ld)
        p.v("bn_stats", st8[bi][:, 0:6], xin[bi][:, 0:512], edge=True)
        p.v("bn_stats", st8[bi][:, 6:12], xin[bi][:, 512:1024], edge=True)
        p.v("bn_aggr", mv[bi][:, 0:2], st8[bi][:, 0:12], edge=True)
        p.wait("scalar", dve, dve.n)
        emit_rsqrt(p, mv[bi][:, 2:3], mv[bi][:, 1:2], pre_bias=1e-5)
        rstd_done = act.n
        p.v("tensor_scalar_add", mv[bi][:, 3:4], mv[bi][:, 1:2], 1e-5, edge=True)
        p.v("reciprocal", mv[bi][:, 3:4], mv[bi][:, 3:4], edge=True)
        p.v("tensor_mul", mv[bi][:, 3:4], mv[bi][:, 3:4], mv[bi][:, 1:2], edge=True)
        p.v("tensor_scalar_add", mv[bi][:, 3:4], mv[bi][:, 3:4], 1e-6, ctr=dve)
        p.wait("scalar", dve, dve.n)
        emit_rsqrt(p, mv[bi][:, 3:4], mv[bi][:, 3:4])
        rms2_done = act.n
        p.wait("vector", act, rstd_done)
        p.v("tensor_scalar", xn32t[bi], xin[bi], mv[bi][:, 0:1], mv[bi][:, 2:3],
            ALU.subtract, ALU.mult, edge=True)
        p.v("tensor_copy", xn16t[bi], xn32t[bi], ctr=dve)
        xn_done = dve.n
        p.wait("vector", act, rms2_done)
        p.v("scalar_tensor_tensor", h0t[bi], xn32t[bi], mv[bi][:, 3:4], wbc,
            ALU.mult, ALU.mult, ctr=dve)
        h0_done = dve.n
        p.wait("sync", dve, xn_done)
        p.dma("sync", xn32r[i], xn32t[bi], qs)
        p.dma("sync", xn16r[i], xn16t[bi], qs)
        p.wait("sync", dve, h0_done)
        p.dma("sync", h0r[i], h0t[bi], qs)
        free_at[i] = (dve.n, qs.n)
    a16.pop()


def phase_F(mk, p, a16):
    d, C = mk.d, mk.C
    qs, qg = p.sems["qs"], p.sems["qg"]
    pe, act, dve = p.sems["pe"], p.sems["act"], p.sems["dve"]
    PS, PT = mk.PS, mk.PT
    a16.push()
    entry_guard(p)

    wqkvz = a16.alloc(8 * 3072)
    for t in range(8):
        p.dma("gpsimd", wqkvz[:, t * 3072:(t + 1) * 3072],
              d["L1_attn_W_qkvz"][t * 128:(t + 1) * 128, :], qg)
    wba = a16.alloc(8 * 16)
    for t in range(8):
        p.dma("gpsimd", wba[:, t * 16:(t + 1) * 16],
              d["L1_attn_W_ba"][t * 128:(t + 1) * 128, :], qg)
    wout = a16.alloc(8 * D)
    for t in range(8):
        p.dma("gpsimd", wout[:, t * D:(t + 1) * D],
              d["L1_attn_W_out"][t * 128:(t + 1) * 128, :], qg)
    w_at = qg.n
    convw = a16.alloc32(64)
    p.dma("sync", convw, d["L1_attn_conv_w"].rearrange("(t p) k -> p t k", p=128), qs)
    dtb = a16.alloc32(2)
    p.dma("sync", dtb[0:8, 0:1], d["L1_attn_dt_bias"][:, None], qs)
    alog = a16.alloc32(2)
    p.dma("sync", alog[0:8, 0:1], d["L1_attn_A_log"][:, None], qs)
    gnw = a16.alloc32(2)
    p.dma("sync", gnw[:, 0:1], d["L1_attn_gate_norm_w"][:, None], qs)
    ln2bc = load_row_bcast(mk, p, a16, d["L1_attn_w_ln2"], D)
    cw_at = qs.n
    p.wait("vector", qs, cw_at)
    p.v("tensor_scalar_add", ln2bc, ln2bc, 1.0, edge=True)
    p.wait("scalar", qs, cw_at)
    p.act(alog[0:8, 1:2], alog[0:8, 0:1], AF.Exp, edge=True)
    p.wait("vector", act, act.n)
    p.v("tensor_scalar_mul", alog[0:8, 1:2], alog[0:8, 1:2], -1.0, ctr=dve)
    dg = a16.alloc(512)
    p.wait("vector", qs, cw_at)
    diag_done = p.sems["dve"].n

    hfm = [a16.alloc(4096) for _ in range(2)]
    qkvb = a16.alloc(16 * 528)
    halo = a16.alloc(48)
    zsil = a16.alloc(8 * 512)
    bas = a16.alloc32(512)
    absb = a16.alloc32(512)
    gfm = a16.alloc32(512)
    S0 = a16.alloc32(8 * 128)
    S0b = a16.alloc(8 * 128)
    qn = a16.alloc(4 * 128)
    kn = a16.alloc(4 * 128)
    ktm = a16.alloc(4 * 128)
    vtm = a16.alloc(8 * 128)
    bcol = a16.alloc32(16)
    bwork = a16.alloc32(80)
    rows = a16.alloc32(8 * 128)
    rows16 = a16.alloc(8 * 128)
    erow16 = a16.alloc(8 * 128)
    DDa = a16.alloc32(1024)
    DDc = a16.alloc32(1024)
    EMLa = a16.alloc(1024)
    EMUa = a16.alloc(1024)
    Aa = a16.alloc(1024)
    Va = a16.alloc(1024)
    VTa = a16.alloc(1024)
    V2a = a16.alloc(1024)
    V2Ta = a16.alloc(1024)
    XAPa = a16.alloc(2048)
    MKTa = Aa
    DLTa = EMUa
    QTa = EMLa
    AQTa = a16.alloc(1024)
    OSBa = V2Ta
    KPa = V2a
    sqv = a16.alloc(128)
    rr = a16.alloc32(1056)
    core = a16.alloc(8 * 128)
    xres = a16.alloc32(D)
    x2t = a16.alloc32(D)
    xht = a16.alloc(D)
    sq2 = xres
    mv2 = a16.alloc32(4)
    halo_init = False

    b0, b1 = PS[:, 0:512], PS[:, 512:1024]
    b2, b3 = PS[:, 1024:1536], PS[:, 1536:2048]
    b4, b5 = PS[:, 2048:2560], PS[:, 2560:3072]
    tpA, tpB = PT[:, 0:1024], PT[:, 1024:2048]

    src16 = d["h016"]
    xn1r = d["xn1_32"].rearrange("(n p) d -> n p d", p=128)
    x2r = d["x232"].rearrange("(n p) d -> n p d", p=128)
    xhr = d["xh16"].rearrange("(n p) d -> n p d", p=128)

    p.v("memset", S0, 0.0, edge=True)
    p.v("memset", S0b, 0.0, edge=True)
    p.v("memset", qkvb, 0.0, ctr=dve)
    init_done = dve.n
    p.wait("tensor", dve, max(init_done, diag_done))
    p.wait("tensor", qg, w_at)

    def load_hfm(c, buf):
        for t in range(8):
            p.dma("sync", buf[:, t * 512:(t + 1) * 512],
                  src16[c * 512:(c + 1) * 512, t * 128:(t + 1) * 128],
                  qs, transpose=True)
        return qs.n

    for sc in range(NC5):
        buf = hfm[sc % 2]
        p.wait("sync", pe, pe.n)
        ld = load_hfm(sc, buf)
        p.wait("tensor", qs, ld)
        for ct in range(16):
            sl = qkvb[:, ct * 528: ct * 528 + 515]
            if sc > 0:
                p.v("tensor_copy", sl[:, 0:3], halo[:, ct * 3:(ct + 1) * 3],
                    edge=True)
            bnk = b0 if ct % 2 == 0 else b1
            p.wait("tensor", dve, dve.n)
            p.wait("tensor", act, act.n)
            for t in range(8):
                p.mm(bnk, wqkvz[:, t * 3072 + ct * 128: t * 3072 + (ct + 1) * 128],
                     buf[:, t * 512:(t + 1) * 512], start=(t == 0), stop=(t == 7),
                     ctr=pe if t == 7 else None)
            p.wait("vector", pe, pe.n)
            p.v("tensor_copy", sl[:, 3:515], bnk, edge=True)
            p.v("tensor_copy", halo[:, ct * 3:(ct + 1) * 3], sl[:, 512:515],
                ctr=dve)
        for ct in range(16):
            bnk = b0 if ct % 2 == 0 else b1
            p.wait("vector", pe, pe.n)
            for tap in range(4):
                p.v("tensor_scalar_mul", dg[:, tap * 128:(tap + 1) * 128],
                    C["ident16"], convw[:, ct * 4 + tap: ct * 4 + tap + 1],
                    ctr=dve if tap == 3 else None)
            p.wait("tensor", dve, dve.n)
            p.wait("tensor", act, act.n)
            for tap in range(4):
                p.mm(bnk, dg[:, tap * 128:(tap + 1) * 128],
                     qkvb[:, ct * 528 + tap: ct * 528 + tap + 512],
                     start=(tap == 0), stop=(tap == 3), ctr=pe if tap == 3 else None)
            p.wait("scalar", pe, pe.n)
            p.act(qkvb[:, ct * 528 + 3: ct * 528 + 515], bnk, AF.Silu, ctr=act)
        for zt in range(8):
            ct = 16 + zt
            bnk = b0 if ct % 2 == 0 else b1
            p.wait("tensor", dve, dve.n)
            p.wait("tensor", act, act.n)
            for t in range(8):
                p.mm(bnk, wqkvz[:, t * 3072 + ct * 128: t * 3072 + (ct + 1) * 128],
                     buf[:, t * 512:(t + 1) * 512], start=(t == 0), stop=(t == 7),
                     ctr=pe if t == 7 else None)
            p.wait("scalar", pe, pe.n)
            p.act(zsil[:, zt * 512:(zt + 1) * 512], bnk, AF.Silu, ctr=act)
        p.wait("tensor", act, act.n)
        for t in range(8):
            p.mm(b0[0:8, :], wba[:, t * 16:t * 16 + 8],
                 buf[:, t * 512:(t + 1) * 512], start=(t == 0), stop=(t == 7))
        for t in range(8):
            p.mm(b1[0:8, :], wba[:, t * 16 + 8:t * 16 + 16],
                 buf[:, t * 512:(t + 1) * 512], start=(t == 0), stop=(t == 7),
                 ctr=pe if t == 7 else None)
        p.wait("vector", pe, pe.n)
        p.v("tensor_copy", bas[0:8, :], b0[0:8, :], edge=True)
        p.v("tensor_copy", absb[0:8, :], b1[0:8, :], edge=True)
        p.wait("scalar", dve, dve.n)
        p.act(bas[0:8, :], bas[0:8, :], AF.Sigmoid, edge=True)
        p.act(gfm[0:8, :], absb[0:8, :], AF.Abs, bias=dtb[0:8, 0:1], edge=True)
        p.act(gfm[0:8, :], gfm[0:8, :], AF.Exp, scale=-1.0, edge=True)
        p.act(gfm[0:8, :], gfm[0:8, :], AF.Ln, bias=1.0, ctr=act)
        sp_done = act.n
        p.wait("vector", act, sp_done)
        p.v("tensor_scalar", absb[0:8, :], absb[0:8, :], dtb[0:8, 0:1], 0.0,
            ALU.add, ALU.max, edge=True)
        p.v("tensor_add", gfm[0:8, :], gfm[0:8, :], absb[0:8, :], edge=True)
        p.v("tensor_scalar_mul", gfm[0:8, :], gfm[0:8, :], alog[0:8, 1:2], ctr=dve)
        g_done = dve.n

        for dc in range(4):
            tk = dc * 128
            ti = sc * 4 + dc
            p.wait("tensor", dve, g_done)
            p.wait("tensor", act, act.n)
            p.trans(b2[0:128, 0:8], gfm[0:8, tk:tk + 128], C["ident32"][0:8, 0:8], ctr=pe)
            p.wait("vector", pe, pe.n)
            p.v("tensor_copy", bwork[:, 0:8], b2[:, 0:8], ctr=dve)
            p.wait("tensor", dve, dve.n)
            p.mm(b3[:, 0:8], C["cum_ut32"], bwork[:, 0:8], start=True, stop=True)
            p.trans(b2[0:128, 8:16], bas[0:8, tk:tk + 128], C["ident32"][0:8, 0:8], ctr=pe)
            p.wait("vector", pe, pe.n)
            p.v("tensor_copy", bcol[:, 0:8], b3[:, 0:8], edge=True)
            p.v("tensor_copy", bwork[:, 8:16], b2[:, 8:16], edge=True)
            p.v("tensor_scalar_mul", bwork[:, 16:24], bwork[:, 8:16], -1.0, edge=True)
            p.wait("scalar", dve, dve.n)
            p.act(bwork[:, 24:32], bcol[:, 0:8], AF.Exp, ctr=act)
            p.wait("vector", act, act.n)
            p.v("tensor_mul", bwork[:, 32:40], bwork[:, 16:24], bwork[:, 24:32],
                ctr=dve)
            percol_done = dve.n
            p.wait("tensor", dve, percol_done)
            for h in range(8):
                p.trans(b2[0:1, 256:384], bcol[:, h:h + 1], C["ident32"], ctr=pe)
                p.wait("vector", pe, pe.n)
                p.v("tensor_copy", rows[0:1, h * 128:(h + 1) * 128],
                    b2[0:1, 256:384], ctr=dve)
                p.wait("tensor", dve, dve.n)
            rows_done = p.sems["dve"].n
            p.wait("scalar", dve, rows_done)
            p.act(erow16[0:1, :], rows[0:1, :], AF.Exp, ctr=act)
            p.wait("vector", act, act.n)
            for h in range(8):
                p.v("tensor_copy", rr[0:1, 256 + h:257 + h],
                    rows[0:1, h * 128 + 127: h * 128 + 128],
                    ctr=dve if h == 7 else None)
            p.wait("tensor", dve, dve.n)
            p.mm(b5[:, 0:8], C["ones32"][0:1, 0:128], rr[0:1, 256:264],
                 start=True, stop=True, ctr=pe)
            p.wait("vector", pe, pe.n)
            p.v("tensor_copy", bwork[:, 40:48], b5[:, 0:8], edge=True)
            p.v("tensor_sub", bwork[:, 48:56], bwork[:, 40:48], bcol[:, 0:8], ctr=dve)
            p.wait("scalar", dve, dve.n)
            p.act(bwork[:, 56:64], bwork[:, 48:56], AF.Exp, edge=True)
            p.act(bwork[:, 64:72], bwork[:, 40:48], AF.Exp, ctr=act)
            percol2_done = act.n

            p.wait("tensor", act, percol2_done)
            for hh in range(4):
                qsl = qkvb[:, hh * 528 + 3 + tk: hh * 528 + 3 + tk + 128]
                ksl = qkvb[:, (4 + hh) * 528 + 3 + tk: (4 + hh) * 528 + 3 + tk + 128]
                for (xi, xsl, osl, qscale) in ((0, qsl, qn, True), (1, ksl, kn, False)):
                    dst = osl[:, hh * 128:(hh + 1) * 128]
                    p.wait("vector", pe, pe.n)
                    p.v("tensor_mul", sqv, xsl, xsl, ctr=dve)
                    p.wait("tensor", dve, dve.n)
                    p.mm(b2[0:1, 256:384], C["ones16"][:, 0:1], sqv,
                         start=True, stop=True, ctr=pe)
                    p.wait("vector", pe, pe.n)
                    p.v("tensor_copy", rr[0:1, 0:128], b2[0:1, 256:384], ctr=dve)
                    p.wait("scalar", dve, dve.n)
                    sc_ = (128.0 ** -0.5) if qscale else 1.0
                    emit_rsqrt(p, rr[0:1, 128:256], rr[0:1, 0:128], pre_bias=1e-6,
                               post_scale=sc_)
                    p.wait("vector", act, act.n)
                    p.v("tensor_copy", rows16[0:1, 0:128], rr[0:1, 128:256],
                        ctr=dve)
                    p.wait("tensor", dve, dve.n)
                    p.mm(b3[:, 256:384], C["ones16"][0:1, 0:128],
                         rows16[0:1, 0:128], start=True, stop=True, ctr=pe)
                    p.wait("vector", pe, pe.n)
                    p.v("tensor_mul", dst, xsl, b3[:, 256:384], ctr=dve)
                p.wait("tensor", dve, dve.n)
                p.trans(tpB[:, 0:128], kn[:, hh * 128:(hh + 1) * 128], C["ident16"],
                        ctr=pe)
                p.wait("vector", pe, pe.n)
                p.v("tensor_copy", ktm[:, hh * 128:(hh + 1) * 128], tpB[:, 0:128],
                    ctr=dve)
            for h in range(8):
                vsl = qkvb[:, (8 + h) * 528 + 3 + tk: (8 + h) * 528 + 3 + tk + 128]
                p.wait("tensor", dve, dve.n)
                p.trans(tpB[:, 128:256], vsl, C["ident16"], ctr=pe)
                p.wait("vector", pe, pe.n)
                p.v("tensor_copy", vtm[:, h * 128:(h + 1) * 128], tpB[:, 128:256],
                    ctr=dve)
            prep_done = dve.n

            rbrow = rows16
            p.wait("tensor", dve, dve.n)
            p.wait("tensor", act, act.n)
            for h in range(8):
                p.mm(PS[:, 1024 + h * 128:1024 + (h + 1) * 128],
                     C["ones32"][0:1, 0:128], rows[0:1, h * 128:(h + 1) * 128],
                     start=True, stop=True, ctr=pe if h == 7 else None)
            p.wait("vector", pe, pe.n)
            for h in range(8):
                p.v("tensor_scalar", DDa[:, h * 128:(h + 1) * 128],
                    PS[:, 1024 + h * 128:1024 + (h + 1) * 128], bcol[:, h:h + 1],
                    None, ALU.subtract, ctr=dve if h == 7 else None)
            p.wait("vector", dve, dve.n)
            p.v("tensor_scalar", DDc, DDa, -1.0, 0.0, ALU.mult, ALU.min, ctr=dve)
            p.wait("scalar", dve, dve.n)
            p.act(DDc, DDc, AF.Exp, ctr=act)
            p.wait("vector", act, act.n)
            mlow_r = C["m_lows32"].rearrange("p (o f) -> p o f", o=1).to_broadcast([128, 8, 128])
            mupi_r = C["m_upi32"].rearrange("p (o f) -> p o f", o=1).to_broadcast([128, 8, 128])
            p.v("tensor_tensor", EMLa.rearrange("p (h f) -> p h f", f=128),
                DDc.rearrange("p (h f) -> p h f", f=128), mlow_r, ALU.mult,
                edge=True)
            p.v("tensor_scalar_min", DDc, DDa, 0.0, ctr=dve)
            p.wait("scalar", dve, dve.n)
            p.act(DDc, DDc, AF.Exp, ctr=act)
            p.wait("vector", act, act.n)
            p.v("tensor_tensor", EMUa.rearrange("p (h f) -> p h f", f=128),
                DDc.rearrange("p (h f) -> p h f", f=128), mupi_r, ALU.mult,
                ctr=dve)
            p.wait("tensor", dve, dve.n)
            for h in range(8):
                kf = kn[:, (h // 2) * 128:(h // 2 + 1) * 128]
                qf = qn[:, (h // 2) * 128:(h // 2 + 1) * 128]
                p.mm(PS[:, 1024 + h * 128:1024 + (h + 1) * 128], kf, kf,
                     start=True, stop=True)
                p.mm(PS[:, 2048 + h * 128:2048 + (h + 1) * 128], kf, qf,
                     start=True, stop=True, ctr=pe if h == 7 else None)
            p.wait("vector", pe, pe.n)
            p.v("tensor_copy", Aa, PS[:, 1024:2048], edge=True)
            p.v("tensor_tensor", AQTa, PS[:, 2048:3072], EMUa, ALU.mult, edge=True)
            for h in range(8):
                p.v("scalar_tensor_tensor", Va[:, h * 128:(h + 1) * 128],
                    Aa[:, h * 128:(h + 1) * 128], bwork[:, 16 + h:17 + h],
                    EMLa[:, h * 128:(h + 1) * 128], ALU.mult, ALU.mult,
                    ctr=dve if h == 7 else None)
            p.wait("tensor", dve, dve.n)
            for h in range(8):
                p.trans(tpA[:, h * 128:(h + 1) * 128], Va[:, h * 128:(h + 1) * 128],
                        C["ident16"], ctr=pe if h == 7 else None)
            p.wait("vector", pe, pe.n)
            p.v("tensor_copy", VTa, tpA[:, 0:1024], edge=True)
            for h in range(8):
                p.v("tensor_scalar_mul", XAPa[:, h * 256:h * 256 + 128],
                    vtm[:, h * 128:(h + 1) * 128], bwork[:, 8 + h:9 + h])
                p.v("tensor_scalar_mul", XAPa[:, h * 256 + 128:h * 256 + 256],
                    ktm[:, (h // 2) * 128:(h // 2 + 1) * 128],
                    bwork[:, 32 + h:33 + h], ctr=dve if h == 7 else None)
            p.wait("tensor", dve, dve.n)
            vc, vtc, v2c, v2tc = Va, VTa, V2a, V2Ta
            for itn in range(7):
                for h in range(8):
                    hb = PS[:, h * 256:(h + 1) * 256]
                    p.mm(hb, C["ident16"], XAPa[:, h * 256:(h + 1) * 256],
                         start=True, stop=False)
                    p.mm(hb, vtc[:, h * 128:(h + 1) * 128],
                         XAPa[:, h * 256:(h + 1) * 256],
                         start=False, stop=True,
                         ctr=pe if (h == 7 and itn == 6) else None)
                if itn < 6:
                    for h in range(8):
                        p.mm(PS[:, 2048 + h * 128:2048 + (h + 1) * 128],
                             vtc[:, h * 128:(h + 1) * 128],
                             vc[:, h * 128:(h + 1) * 128], start=True, stop=True,
                             ctr=pe if h == 7 else None)
                p.wait("vector", pe, pe.n)
                p.v("tensor_copy", XAPa, PS[:, 0:2048], edge=True)
                if itn < 6:
                    p.v("tensor_copy", v2c, PS[:, 2048:3072], ctr=dve)
                    p.wait("tensor", dve, dve.n)
                    for h in range(8):
                        p.mm(PS[:, 2048 + h * 128:2048 + (h + 1) * 128],
                             vc[:, h * 128:(h + 1) * 128],
                             vtc[:, h * 128:(h + 1) * 128], start=True, stop=True,
                             ctr=pe if h == 7 else None)
                    p.wait("vector", pe, pe.n)
                    p.v("tensor_copy", v2tc, PS[:, 2048:3072], ctr=dve)
                    vc, v2c = v2c, vc
                    vtc, v2tc = v2tc, vtc
                else:
                    p.v("engine_nop", ctr=dve)
                p.wait("tensor", dve, dve.n)
            for h in range(8):
                p.trans(tpB[:, h * 128:(h + 1) * 128],
                        XAPa[:, h * 256 + 128:h * 256 + 256], C["ident16"],
                        ctr=pe if h == 7 else None)
            p.wait("vector", pe, pe.n)
            p.v("tensor_copy", MKTa, tpB[:, 0:1024], ctr=dve)
            p.wait("tensor", dve, dve.n)
            for h in range(8):
                p.mm(PS[:, h * 128:(h + 1) * 128], MKTa[:, h * 128:(h + 1) * 128],
                     S0b[:, h * 128:(h + 1) * 128], start=True, stop=True,
                     ctr=pe if h == 7 else None)
            p.wait("vector", pe, pe.n)
            p.v("tensor_tensor", DLTa.rearrange("p (h f) -> p h f", f=128),
                PS[:, 0:1024].rearrange("p (h f) -> p h f", f=128),
                XAPa.rearrange("p (h f) -> p h f", f=256)[:, :, 0:128],
                ALU.add, ctr=dve)
            p.wait("tensor", dve, dve.n)
            for h in range(8):
                p.mm(PS[:, 1024 + h * 128:1024 + (h + 1) * 128],
                     C["ones16"][0:1, 0:128], erow16[0:1, h * 128:(h + 1) * 128],
                     start=True, stop=True, ctr=pe if h == 7 else None)
            p.wait("vector", pe, pe.n)
            p.v("tensor_tensor", QTa.rearrange("p (h r f) -> p h r f", r=2, f=128),
                qn.rearrange("p (h o f) -> p h o f", o=1, f=128)
                  .to_broadcast([128, 4, 2, 128]),
                PS[:, 1024:2048].rearrange("p (h r f) -> p h r f", r=2, f=128),
                ALU.mult, ctr=dve)
            p.wait("tensor", dve, dve.n)
            for h in range(8):
                ob = PS[:, 1024 + h * 128:1024 + (h + 1) * 128]
                p.mm(ob, S0b[:, h * 128:(h + 1) * 128],
                     QTa[:, h * 128:(h + 1) * 128], start=True, stop=False)
                p.mm(ob, DLTa[:, h * 128:(h + 1) * 128],
                     AQTa[:, h * 128:(h + 1) * 128], start=False, stop=True,
                     ctr=pe if h == 7 else None)
            p.wait("vector", pe, pe.n)
            p.v("tensor_copy", OSBa, PS[:, 1024:2048], edge=True)
            for h in range(8):
                p.v("tensor_scalar_mul", KPa[:, h * 128:(h + 1) * 128],
                    ktm[:, (h // 2) * 128:(h // 2 + 1) * 128],
                    bwork[:, 56 + h:57 + h], ctr=dve if h == 7 else None)
            p.wait("tensor", dve, dve.n)
            for h in range(8):
                p.mm(PS[:, 2048 + h * 128:2048 + (h + 1) * 128],
                     KPa[:, h * 128:(h + 1) * 128], DLTa[:, h * 128:(h + 1) * 128],
                     start=True, stop=True, ctr=pe if h == 7 else None)
            p.wait("vector", pe, pe.n)
            for h in range(8):
                p.v("scalar_tensor_tensor", S0[:, h * 128:(h + 1) * 128],
                    S0[:, h * 128:(h + 1) * 128], bwork[:, 64 + h:65 + h],
                    PS[:, 2048 + h * 128:2048 + (h + 1) * 128], ALU.mult, ALU.add,
                    ctr=dve if h == 7 else None)
            p.wait("vector", dve, dve.n)
            p.v("tensor_copy", S0b, S0, edge=True)
            p.v("tensor_mul", KPa, OSBa, OSBa, ctr=dve)
            p.wait("tensor", dve, dve.n)
            p.mm(PS[0:1, 0:512], C["ones16"][:, 0:1], KPa[:, 0:512],
                 start=True, stop=True)
            p.mm(PS[0:1, 512:1024], C["ones16"][:, 0:1], KPa[:, 512:1024],
                 start=True, stop=True, ctr=pe)
            p.wait("scalar", pe, pe.n)
            emit_rsqrt(p, rr[0:1, 0:1024], PS[0:1, 0:1024], pre_scale=1.0 / 128,
                       pre_bias=1e-6)
            p.wait("vector", act, act.n)
            p.v("tensor_copy", rbrow[0:1, 0:1024], rr[0:1, 0:1024], ctr=dve)
            p.wait("tensor", dve, dve.n)
            for h in range(8):
                p.mm(PS[:, 1024 + h * 128:1024 + (h + 1) * 128],
                     C["ones16"][0:1, 0:128], rbrow[0:1, h * 128:(h + 1) * 128],
                     start=True, stop=True, ctr=pe if h == 7 else None)
            p.wait("vector", pe, pe.n)
            p.v("scalar_tensor_tensor", OSBa, OSBa, gnw[:, 0:1], PS[:, 1024:2048],
                ALU.mult, ALU.mult, edge=True)
            p.v("tensor_tensor", core.rearrange("p (h f) -> p h f", f=128),
                OSBa.rearrange("p (h f) -> p h f", f=128),
                zsil.rearrange("p (h f) -> p h f", f=512)[:, :, tk:tk + 128],
                ALU.mult, ctr=dve)
            core_done = dve.n

            p.dma("sync", xres, xn1r[ti], qs)
            ldx = qs.n
            p.wait("tensor", dve, core_done)
            for half in range(2):
                bnk = b0 if half == 0 else b1
                for t in range(8):
                    p.mm(bnk, core[:, t * 128:(t + 1) * 128],
                         wout[:, t * D + half * 512: t * D + half * 512 + 512],
                         start=(t == 0), stop=(t == 7), ctr=pe if t == 7 else None)
                p.wait("vector", pe, pe.n)
                p.wait("vector", qs, ldx)
                p.v("tensor_add", x2t[:, half * 512:(half + 1) * 512],
                    xres[:, half * 512:(half + 1) * 512], bnk, ctr=dve)
            x2_done = dve.n
            p.wait("sync", dve, x2_done)
            p.dma("sync", x2r[ti], x2t, qs)
            p.wait("scalar", dve, x2_done)
            p.act(sq2, x2t, AF.Square, accum_out=mv2[:, 0:1], edge=True)
            emit_rsqrt(p, mv2[:, 1:2], mv2[:, 0:1], pre_scale=1.0 / D,
                       pre_bias=1e-6)
            p.wait("vector", act, act.n)
            p.v("scalar_tensor_tensor", xht, x2t, mv2[:, 1:2], ln2bc,
                ALU.mult, ALU.mult, ctr=dve)
            p.wait("sync", dve, dve.n)
            p.dma("sync", xhr[ti], xht, qs)
    a16.pop()


def prepass_slices(mk, p):
    d = mk.d
    qg = p.sems["qg"]
    jobs = []
    for srcT, dstT, R in [
        (d["L1_ffn_shared_w1"], d["sw1_16"], 4096),
        (d["L1_ffn_shared_w2"], d["sw2_16"], 1024),
        (d["L1_ffn_experts_w1"].rearrange("e f d -> (e f) d"), d["ew1_16"], 8 * 4096),
        (d["L1_ffn_experts_w2"].rearrange("e d f -> (e d) f"), d["ew2_16"], 8 * 1024),
        (d["L1_ffn_gate_w"], d["gw_16"], 8),
    ]:
        r = 0
        while r < R:
            rr_ = min(1024, R - r)
            jobs.append((srcT, dstT, r, rr_))
            r += rr_
    ns = (len(jobs) + 7) // 8
    out = []
    for s in range(8):
        part = jobs[s * ns:(s + 1) * ns]
        def emit(part=part):
            for srcT, dstT, r, rr_ in part:
                p.dma("gpsimd", dstT[r:r + rr_, :], srcT[r:r + rr_, :], qg)
        out.append(emit)
    return out


def phase_moe2(mk, p, a16, which):
    d, C = mk.d, mk.C
    qs, qg = p.sems["qs"], p.sems["qg"]
    pe, act, dve = p.sems["pe"], p.sems["act"], p.sems["dve"]
    PS = mk.PS
    C = mk.C
    a16.push()
    entry_guard(p)

    if which == "G":
        src16, acc_h = d["xh16"], d["moe1"]
        NFF, glu = 4, True
    else:
        src16, acc_h = d["xn1_16"], d["ffn1"]
        NFF, glu = 32, False

    c32 = a16.alloc32(NT * 8)
    xfm = [a16.alloc(4096) for _ in range(2)]
    a16.push()
    rw = a16.alloc(8 * 8)
    if which == "G":
        for t in range(8):
            p.dma("gpsimd", rw[:, t * 8:(t + 1) * 8],
                  d["L1_attn_W_router"][t * 128:(t + 1) * 128, :], qg)
    else:
        gwrow = a16.alloc(1024)
        p.dma("sync", gwrow[0:8, :], d["gw_16"][:, :], qs)
        p.wait("tensor", qs, qs.n)
        for t in range(8):
            p.trans(mk.PT[0:128, 0:8], gwrow[0:8, t * 128:(t + 1) * 128],
                    C["ident16"][0:8, 0:8], ctr=pe)
            p.wait("vector", pe, pe.n)
            p.v("tensor_copy", rw[:, t * 8:(t + 1) * 8], mk.PT[0:128, 0:8], ctr=dve)
            p.wait("tensor", dve, dve.n)
    rw_at = (qg.n, qs.n)
    m8 = a16.alloc32(8)
    lgt = a16.alloc32(8)
    wv = a16.alloc32(4)
    ctmp = a16.alloc32(8)
    bR = PS[:, 0:512]

    def load_xfm(c, buf):
        for t in range(8):
            p.dma("sync", buf[:, t * 512:(t + 1) * 512],
                  src16[c * 512:(c + 1) * 512, t * 128:(t + 1) * 128],
                  qs, transpose=True)
        return qs.n

    xfm_free = [0, 0]
    for c in range(NC5):
        buf = xfm[c % 2]
        p.wait("sync", pe, xfm_free[c % 2])
        ld = load_xfm(c, buf)
        p.wait("tensor", qs, ld)
        if c == 0:
            p.wait("tensor", qg, rw_at[0])
            p.wait("tensor", qs, rw_at[1])
        for ts in range(4):
            it = c * 4 + ts
            p.wait("tensor", dve, dve.n)
            for t in range(8):
                p.mm(bR[:, 0:8], buf[:, t * 512 + ts * 128: t * 512 + (ts + 1) * 128],
                     rw[:, t * 8:(t + 1) * 8], start=(t == 0), stop=(t == 7),
                     ctr=pe if t == 7 else None)
            p.wait("vector", pe, pe.n)
            p.v("tensor_copy", lgt, bR[:, 0:8], edge=True)
            p.v("max", m8, lgt, edge=True)
            p.v("tensor_sub", wv[:, 0:1], m8[:, 0:1], m8[:, 1:2], ctr=dve)
            p.wait("scalar", dve, dve.n)
            p.act(wv[:, 1:2], wv[:, 0:1], AF.Sigmoid, ctr=act)
            p.wait("vector", act, act.n)
            p.v("tensor_scalar", wv[:, 2:3], wv[:, 1:2], -1.0, 1.0, ALU.mult,
                ALU.add, edge=True)
            p.v("tensor_scalar", ctmp, lgt, m8[:, 0:1], wv[:, 1:2],
                ALU.is_equal, ALU.mult, edge=True)
            p.v("tensor_scalar", c32[:, it * 8:(it + 1) * 8], lgt, m8[:, 1:2],
                wv[:, 2:3], ALU.is_equal, ALU.mult, edge=True)
            p.v("tensor_add", c32[:, it * 8:(it + 1) * 8],
                c32[:, it * 8:(it + 1) * 8], ctmp, ctr=dve)
        xfm_free[c % 2] = pe.n
    routing_done = p.sems["dve"].n
    a16.pop()
    p.wait("gpsimd", pe, pe.n)
    p.wait("gpsimd", dve, dve.n)
    p.wait("gpsimd", act, act.n)
    p.wait("sync", dve, dve.n)

    if which == "G":
        w1e = a16.alloc(8 * 512)
        wue = a16.alloc(8 * 512)
        w2e = a16.alloc(4 * 1024)
    else:
        w1e = a16.alloc(8 * 4096)
        w2e = a16.alloc(32 * 1024)
    he = a16.alloc(NFF * 512)
    hetmp = a16.alloc(512)
    ysb = a16.alloc32(4 * 512)
    bank_he = [PS[:, 2048:2560], PS[:, 2560:3072]]
    bank_y = [PS[:, 0:512], PS[:, 512:1024], PS[:, 1024:1536], PS[:, 1536:2048]]

    w_free = 0
    w1_free = 0
    acc_order = {}
    ysb_free = {}
    xfm_ld = [0, 0]
    elist = list(range(8)) + (["sh"] if which == "H" else [])
    for e in elist:
        p.wait("gpsimd", pe, w1_free)
        p.wait("sync", pe, w1_free)
        if which == "G":
            for t in range(8):
                p.dma("gpsimd", w1e[:, t * 512:(t + 1) * 512],
                      d["L1_attn_W_gate_e"][e, t * 128:(t + 1) * 128, :], qg)
                p.dma("gpsimd", wue[:, t * 512:(t + 1) * 512],
                      d["L1_attn_W_up_e"][e, t * 128:(t + 1) * 128, :], qg)
            p.wait("gpsimd", pe, w_free)
            for t in range(4):
                p.dma("gpsimd", w2e[:, t * 1024:(t + 1) * 1024],
                      d["L1_attn_W_down_e"][e, t * 128:(t + 1) * 128, :], qg)
            w_at = (qg.n, qs.n)
        else:
            if e == "sh":
                w1src, w2src, ro1, ro2 = d["sw1_16"], d["sw2_16"], 0, 0
            else:
                w1src, w2src = d["ew1_16"], d["ew2_16"]
                ro1, ro2 = e * 4096, e * 1024
            for t in range(8):
                p.dma("sync", w1e[:, t * 4096:(t + 1) * 4096],
                      w1src[ro1:ro1 + 4096, t * 128:(t + 1) * 128], qs,
                      transpose=True)
            p.wait("sync", pe, w_free)
            for t in range(32):
                p.dma("sync", w2e[:, t * 1024:(t + 1) * 1024],
                      w2src[ro2:ro2 + 1024, t * 128:(t + 1) * 128], qs,
                      transpose=True)
            w_at = (qg.n, qs.n)

        for c in range(NC5):
            buf = xfm[c % 2]
            if c == 0:
                p.wait("sync", pe, xfm_free[0])
                xfm_ld[0] = load_xfm(0, buf)
            ld = xfm_ld[c % 2]
            if c + 1 < NC5:
                p.wait("sync", pe, xfm_free[(c + 1) % 2])
                xfm_ld[(c + 1) % 2] = load_xfm(c + 1, xfm[(c + 1) % 2])
            p.wait("tensor", qs, ld)
            if c == 0:
                p.wait("tensor", qg, w_at[0])
                p.wait("tensor", qs, w_at[1])
                if e == 0:
                    p.wait("scalar", dve, routing_done)
                    p.wait("vector", dve, routing_done)
            he_evac = {}
            for ff in range(NFF):
                bnk = bank_he[ff % 2] if not glu else bank_he[0]
                if glu:
                    bup = bank_he[1]
                    if ff >= 1:
                        p.wait("tensor", dve, he_evac[ff - 1])
                    for t in range(8):
                        p.mm(bnk, w1e[:, t * 512 + ff * 128: t * 512 + (ff + 1) * 128],
                             buf[:, t * 512:(t + 1) * 512], start=(t == 0),
                             stop=(t == 7))
                    for t in range(8):
                        p.mm(bup, wue[:, t * 512 + ff * 128: t * 512 + (ff + 1) * 128],
                             buf[:, t * 512:(t + 1) * 512], start=(t == 0),
                             stop=(t == 7), ctr=pe if t == 7 else None)
                    p.wait("scalar", pe, pe.n)
                    p.act(hetmp, bnk, AF.Silu, ctr=act)
                    p.wait("vector", act, act.n)
                    p.v("tensor_mul", he[:, ff * 512:(ff + 1) * 512], hetmp, bup,
                        ctr=dve)
                    he_evac[ff] = dve.n
                else:
                    if ff >= 2:
                        p.wait("tensor", act, he_evac[ff - 2])
                    for t in range(8):
                        p.mm(bnk, w1e[:, t * 4096 + ff * 128: t * 4096 + (ff + 1) * 128],
                             buf[:, t * 512:(t + 1) * 512], start=(t == 0),
                             stop=(t == 7), ctr=pe if t == 7 else None)
                    p.wait("scalar", pe, pe.n)
                    p.act(he[:, ff * 512:(ff + 1) * 512], bnk, AF.Silu, ctr=act)
                    he_evac[ff] = act.n
            he_all = he_evac[NFF - 1]
            if c == NC5 - 1:
                w1_free = pe.n
            for dh in range(2):
                for ts in range(4):
                    it = c * 4 + ts
                    bnk = bank_y[ts]
                    if ts in ysb_free:
                        p.wait("tensor", qg, ysb_free[ts])
                        p.wait("scalar", qg, ysb_free[ts])
                        p.wait("vector", qg, ysb_free[ts])
                    for ff in range(NFF):
                        if ff == 0:
                            if glu:
                                p.wait("tensor", dve, he_all)
                            else:
                                p.wait("tensor", act, he_all)
                        p.mm(bnk, he[:, ff * 512 + ts * 128: ff * 512 + (ts + 1) * 128],
                             w2e[:, ff * 1024 + dh * 512: ff * 1024 + dh * 512 + 512],
                             start=(ff == 0), stop=(ff == NFF - 1),
                             ctr=pe if ff == NFF - 1 else None)
                    mmv = pe.n
                    ysl = ysb[:, ts * 512:(ts + 1) * 512]
                    if e == "sh":
                        p.wait("scalar", pe, mmv)
                        p.act(ysl, bnk, AF.Copy, ctr=act)
                        ev = ("act", act.n)
                    else:
                        csc = c32[:, it * 8 + e: it * 8 + e + 1]
                        if dh == 0:
                            p.wait("scalar", pe, mmv)
                            p.act(ysl, bnk, AF.Copy, scale=csc, ctr=act)
                            ev = ("act", act.n)
                        else:
                            p.wait("vector", pe, mmv)
                            p.v("tensor_scalar_mul", ysl, bnk, csc, ctr=dve)
                            ev = ("dve", dve.n)
                    p.wait("gpsimd", p.sems[ev[0]], ev[1])
                    if (it, dh) in acc_order:
                        p.wait("gpsimd", qg, acc_order[(it, dh)])
                    dst = acc_h[it * 128:(it + 1) * 128, dh * 512:(dh + 1) * 512]
                    p.dma("gpsimd", dst, ysl, qg, accum=(e != 0))
                    acc_order[(it, dh)] = qg.n
                    ysb_free[ts] = qg.n
            xfm_free[c % 2] = pe.n
        w_free = pe.n
    a16.pop()


def phase_add(mk, p, a16, out_name, a_name, b_name):
    d = mk.d
    qs, dve = p.sems["qs"], p.sems["dve"]
    a16.push()
    entry_guard(p)
    a_r = d[a_name].rearrange("(n p) d -> n p d", p=128)
    b_r = d[b_name].rearrange("(n p) d -> n p d", p=128)
    o_r = d[out_name].rearrange("(n p) d -> n p d", p=128)
    ab = [a16.alloc32(D) for _ in range(2)]
    bb = [a16.alloc32(D) for _ in range(2)]
    free3 = {}
    for i in range(NT):
        bi = i % 2
        if i == 0:
            p.wait("tensor", dve, dve.n)
            p.wait("tensor", p.sems["act"], p.sems["act"].n)
        p.mm(mk.PS[0:64, 0:64], mk.C["ident16"][:, 0:64], mk.C["ident16"][:, 0:64],
             start=True, stop=True)
        if i >= 2:
            p.wait("sync", dve, free3[i - 2][0])
            p.wait("sync", qs, free3[i - 2][1])
        p.dma("sync", ab[bi], a_r[i], qs)
        p.dma("sync", bb[bi], b_r[i], qs)
        ld = qs.n
        p.wait("vector", qs, ld)
        p.v("tensor_add", ab[bi], ab[bi], bb[bi], ctr=dve)
        dn = dve.n
        p.wait("sync", dve, dn)
        p.dma("sync", o_r[i], ab[bi], qs)
        free3[i] = (dn, qs.n)
    a16.pop()


def phase_combine2(mk, p, a16, out_name, x_name, s_name, f_name):
    d = mk.d
    qs, dve, act = p.sems["qs"], p.sems["dve"], p.sems["act"]
    a16.push()
    entry_guard(p)
    x_r = d[x_name].rearrange("(n p) d -> n p d", p=128)
    s_r = d[s_name].rearrange("(n p) d -> n p d", p=128)
    f_r = d[f_name].rearrange("(n p) d -> n p d", p=128)
    o_r = d[out_name].rearrange("(n p) d -> n p d", p=128)
    xb = [a16.alloc32(D) for _ in range(2)]
    sb_ = [a16.alloc32(D) for _ in range(2)]
    fb = [a16.alloc32(D) for _ in range(2)]
    gb = [a16.alloc32(D) for _ in range(2)]
    free3 = {}
    for i in range(NT):
        bi = i % 2
        if i == 0:
            p.wait("tensor", dve, dve.n)
            p.wait("tensor", p.sems["act"], p.sems["act"].n)
        p.mm(mk.PS[0:64, 0:64], mk.C["ident16"][:, 0:64], mk.C["ident16"][:, 0:64],
             start=True, stop=True)
        if i >= 2:
            p.wait("sync", dve, free3[i - 2][0])
            p.wait("sync", qs, free3[i - 2][1])
        p.dma("sync", xb[bi], x_r[i], qs)
        p.dma("sync", sb_[bi], s_r[i], qs)
        p.dma("sync", fb[bi], f_r[i], qs)
        ld = qs.n
        p.wait("scalar", qs, ld)
        p.act(gb[bi], fb[bi], AF.Silu, ctr=act)
        p.wait("vector", act, act.n)
        p.v("tensor_mul", gb[bi], gb[bi], sb_[bi], edge=True)
        p.v("tensor_add", gb[bi], gb[bi], xb[bi], ctr=dve)
        dn = dve.n
        p.wait("sync", dve, dn)
        p.dma("sync", o_r[i], gb[bi], qs)
        free3[i] = (dn, qs.n)
    a16.pop()


def kernel(**inputs):
    nc = bass.Bass("TRN2", target_bir_lowering=False)
    m = MK(nc, phases=("A", "B", "C", "D", "E", "F", "G", "H"))
    nc2 = m.build()
    in_maps = [make_inputs_per_core(inputs, b) for b in range(B)]
    res = run_bass_kernel_spmd(nc2, in_maps, core_ids=list(range(B)))
    return np.stack([res.results[b]["out"] for b in range(B)], axis=0)
